# version 04
# baseline (speedup 1.0000x reference)
import os
import math
import contextlib
import numpy as np
import ml_dtypes
import concourse.bass as bass
import concourse.mybir as mybir
from concourse.bass_utils import run_bass_kernel_spmd

F32 = mybir.dt.float32
BF16 = mybir.dt.bfloat16
I32 = mybir.dt.int32
AF = mybir.ActivationFunctionType
ALU = mybir.AluOpType
AX = mybir.AxisListType
BFD = ml_dtypes.bfloat16

B, S, E, H, D = 2, 4096, 4096, 32, 128
NK = E // 128
NQ = NK + 2
TPC = 1024
NT = TPC // 128
HPC = H // 4
SCALING = float(D) ** -0.5
EPS_LN = 1e-5
RANKOF = [a if a < 4 else 7 - a for a in range(8)]
LOCOF = [0 if a < 4 else 1 for a in range(8)]
UNITS = [(a, kc, kc >= 4 * a) for a in range(8) for kc in range(4 * a + 4)]
U = len(UNITS)
UPFX = [0]
for a in range(8):
    UPFX.append(UPFX[-1] + 4 * a + 4)
NDIAG = np.cumsum([int(d) for (_, _, d) in UNITS]).tolist()


def _slopes():
    base = 2.0 ** (-(2.0 ** (-(math.log2(H) - 3))))
    return (base ** np.arange(1, H + 1, dtype=np.float64)).astype(np.float32)


class SemC:
    def __init__(self, handle):
        self.h = handle
        self.n = 0


class Sched:
    def __init__(self):
        self.ops = []

    def op(self, eng, emit, waits=(), inc=None, amt=1):
        incs = []
        if inc is not None and amt > 0:
            inc.n += amt
            incs = [(inc, amt)]
        self.ops.append((eng, [(s, v) for (s, v) in waits], emit, incs))

    def emit_into(self, nc):
        keys = sorted({o[0] for o in self.ops})
        with nc.Block() as blk:
            for key in keys:
                def body(engv, key=key):
                    for (ek, waits, emit, incs) in self.ops:
                        if ek != key:
                            continue
                        for (s, v) in waits:
                            engv.wait_ge(s.h, v)
                        inst = emit(engv)
                        for (s, amt) in incs:
                            inst.then_inc(s.h, amt)
                getattr(blk, {"t": "tensor", "v": "vector", "s": "scalar",
                              "g": "gpsimd", "y": "sync"}[key])(body)


def build_nc(dbg=False):
    import os as _os
    _PH = int(_os.environ.get("MKPHASES", "99"))
    nc = bass.Bass("TRN2", target_bir_lowering=False)
    ap = lambda t: t.ap()

    xs = ap(nc.dram_tensor("xs", [TPC, E], F32, kind="ExternalInput"))
    wq_t = ap(nc.dram_tensor("wq_t", [NQ, 128, NK * 128], F32, kind="ExternalInput"))
    wc_t = ap(nc.dram_tensor("wc_t", [NK, 128, NK * 128], F32, kind="ExternalInput"))
    wb_t = ap(nc.dram_tensor("wb_t", [NK, 128, NK * 128], F32, kind="ExternalInput"))
    acat_t = ap(nc.dram_tensor("acat_t", [128, NK * 128], F32, kind="ExternalInput"))
    bcat_t = ap(nc.dram_tensor("bcat_t", [128, NK * 128], F32, kind="ExternalInput"))
    rw_t = ap(nc.dram_tensor("rw_t", [128, NK * 128], F32, kind="ExternalInput"))
    bbT_i = ap(nc.dram_tensor("bbT_i", [128, 32], F32, kind="ExternalInput"))
    rb_row = ap(nc.dram_tensor("rb_row", [1, 16], F32, kind="ExternalInput"))
    prior_row = ap(nc.dram_tensor("prior_row", [1, 16], F32, kind="ExternalInput"))
    active_col = ap(nc.dram_tensor("active_col", [4, 1], I32, kind="ExternalInput"))
    biastab_i = ap(nc.dram_tensor("biastab_i", [128, HPC * 32], F32, kind="ExternalInput"))
    negslope_i = ap(nc.dram_tensor("negslope_i", [1, HPC], F32, kind="ExternalInput"))
    posrow_i = ap(nc.dram_tensor("posrow_i", [1, S], BF16, kind="ExternalInput"))
    maskadd_i = ap(nc.dram_tensor("maskadd_i", [128, 4 * 512], F32, kind="ExternalInput"))
    iota16_i = ap(nc.dram_tensor("iota16_i", [4, 16], F32, kind="ExternalInput"))
    e16sc_i = ap(nc.dram_tensor("e16sc_i", [16, 128], F32, kind="ExternalInput"))
    identf_i = ap(nc.dram_tensor("identf_i", [128, 128], F32, kind="ExternalInput"))
    coff_i = ap(nc.dram_tensor("coff_i", [1, 4], I32, kind="ExternalInput"))

    out_t = ap(nc.dram_tensor("out", [TPC, E], F32, kind="ExternalOutput"))

    x1_s = ap(nc.dram_tensor("x1_s", [TPC, E], BF16))
    qT_s = nc.dram_tensor("qT_s", [NK * 128, TPC], BF16)
    kv_s = nc.dram_tensor("kv_s", [256, TPC], BF16)
    gat1 = nc.dram_tensor("gat1", [4 * NK * 128, TPC], BF16)
    gkv = nc.dram_tensor("gkv", [4 * 256, TPC], BF16)
    attc_s = nc.dram_tensor("attc_s", [S, HPC * 128], BF16)
    gat2 = nc.dram_tensor("gat2", [4 * S, HPC * 128], BF16)
    qex = nc.dram_tensor("qex", [4096, TPC], BF16)
    attex = nc.dram_tensor("attex", [4096, TPC], BF16)
    proj_s = ap(nc.dram_tensor("proj_s", [E, TPC], BF16))
    xmid_s = ap(nc.dram_tensor("xmid_s", [TPC, E], F32))
    x2_s = ap(nc.dram_tensor("x2_s", [TPC, E], BF16))
    ffn_s = ap(nc.dram_tensor("ffn_s", [E, TPC], BF16))

    dbg_pairs = []
    if dbg:
        for nm, src in [("d_x1", x1_s), ("d_qkvT", qT_s.ap()),
                        ("d_attc", attc_s.ap()), ("d_proj", proj_s),
                        ("d_xmid", xmid_s), ("d_x2", x2_s), ("d_ffn", ffn_s)]:
            o = ap(nc.dram_tensor(nm, list(src.tensor.shape), src.tensor.dtype,
                                  kind="ExternalOutput"))
            dbg_pairs.append((o, src))

    st = contextlib.ExitStack()
    with st:
        e = st.enter_context

        featT = e(nc.sbuf_tensor("featT", [128, NK * 1024], BF16))
        acat_sb = e(nc.sbuf_tensor("acat_sb", [128, NK * 128], BF16))
        bcat_sb = e(nc.sbuf_tensor("bcat_sb", [128, NK * 128], BF16))
        rwsb = e(nc.sbuf_tensor("rwsb", [128, NK * 128], BF16))
        maskadd = e(nc.sbuf_tensor("maskadd", [128, 4 * 512], F32))
        biastab = e(nc.sbuf_tensor("biastab", [128, HPC * 32], F32))
        posrow = e(nc.sbuf_tensor("posrow", [1, S], BF16))
        negslope = e(nc.sbuf_tensor("negslope", [1, HPC], F32))
        bbT = e(nc.sbuf_tensor("bbT", [128, 32], F32))
        identf = e(nc.sbuf_tensor("identf", [128, 128], F32))
        e16sc = e(nc.sbuf_tensor("e16sc", [16, 128], F32))
        iota16 = e(nc.sbuf_tensor("iota16", [4, 16], F32))
        activef = e(nc.sbuf_tensor("activef", [4, 1], F32))
        activei = e(nc.sbuf_tensor("activei", [4, 1], I32))
        onehotT = e(nc.sbuf_tensor("onehotT", [4, 16], F32))
        rblnp = e(nc.sbuf_tensor("rblnp", [1, 16], F32))
        r16a = e(nc.sbuf_tensor("r16a", [1, 16], F32))
        r16b = e(nc.sbuf_tensor("r16b", [1, 16], F32))
        ones_c128b = e(nc.sbuf_tensor("ones_c128b", [128, 1], BF16))
        ones_r128b = e(nc.sbuf_tensor("ones_r128b", [1, 128], BF16))
        ones_r128f = e(nc.sbuf_tensor("ones_r128f", [1, 128], F32))
        ones_c4f = e(nc.sbuf_tensor("ones_c4f", [4, 1], F32))
        negb = [e(nc.sbuf_tensor(f"negb{i}", [1, S], BF16)) for i in range(2)]
        qnk = e(nc.sbuf_tensor("qnk", [1, S], F32))
        kneg2 = e(nc.sbuf_tensor("kneg2", [1, 1], F32))
        kk = e(nc.sbuf_tensor("kk", [1, 1], F32))
        ktmp = e(nc.sbuf_tensor("ktmp", [1, 16], F32))
        statc = e(nc.sbuf_tensor("statc", [128, 16], F32))
        epsln_c = e(nc.sbuf_tensor("epsln_c", [128, 1], F32))
        eps12 = e(nc.sbuf_tensor("eps12", [1, 1], F32))
        eps30 = e(nc.sbuf_tensor("eps30", [1, 1], F32))
        coffs = e(nc.sbuf_tensor("coffs", [1, 4], I32))

        _sems = {}
        def S_(name):
            if name not in _sems:
                _sems[name] = SemC(e(nc.semaphore(name)))
            return _sems[name]

        sch = Sched()
        c0 = S_("c0")
        for dst, src in [(acat_sb, acat_t), (bcat_sb, bcat_t), (rwsb, rw_t),
                         (maskadd, maskadd_i), (biastab, biastab_i),
                         (posrow, posrow_i), (negslope, negslope_i), (bbT, bbT_i),
                         (identf, identf_i), (e16sc, e16sc_i), (iota16, iota16_i),
                         (activei, active_col), (r16a, rb_row), (r16b, prior_row),
                         (coffs, coff_i)]:
            sch.op("g", (lambda d, s2: (lambda g: g.dma_start(d[:, :], s2)))(dst, src),
                   inc=c0, amt=16)
        for buf, val in [(ones_c128b, 1.0), (ones_r128b, 1.0),
                         (ones_r128f, 1.0), (ones_c4f, 1.0),
                         (epsln_c, EPS_LN), (eps12, 1e-12), (eps30, 1e-30)]:
            sch.op("g", (lambda b2, v2: (lambda g: g.memset(b2[:, :], v2)))(buf, val),
                   inc=c0, amt=0)
        sch.emit_into(nc)

        b0b_st = contextlib.ExitStack()
        ps_cnt = b0b_st.enter_context(nc.psum_tensor("ps_cnt", [1, 16], F32))
        sch = Sched()
        mi = S_("mi")
        sch.op("v", lambda v: v.tensor_copy(activef[:, :], activei[:, :]),
               waits=[(c0, c0.n)], inc=mi)
        sch.op("v", lambda v: v.tensor_scalar(onehotT[:, :], iota16[:, :],
                                              activef[:, :], None, ALU.is_equal),
               waits=[(mi, 1)], inc=mi)
        sch.op("t", lambda t: t.matmul(ps_cnt[:, :], ones_c4f[:, :], onehotT[:, :],
                                       start=True, stop=True),
               waits=[(mi, 2)], inc=mi)
        sch.op("s", lambda sc: sc.activation(r16b[:, :], r16b[:, :], AF.Ln, bias=eps12[0:1, 0:1]),
               waits=[(c0, c0.n), (mi, 3)], inc=mi)
        sch.op("s", lambda sc: sc.activation(rblnp[:, :], ps_cnt[:, :], AF.Ln, bias=eps30[0:1, 0:1]),
               waits=[(mi, 4)], inc=mi)
        sch.op("v", lambda v: v.tensor_add(rblnp[:, :], rblnp[:, :], r16a[:, :]),
               waits=[(mi, 5)], inc=mi)
        sch.op("v", lambda v: v.tensor_add(rblnp[:, :], rblnp[:, :], r16b[:, :]),
               waits=[(mi, 6)], inc=mi)
        sch.emit_into(nc)
        b0b_st.close()
        if _PH <= 1:
            nc.finalize(); return nc

        with contextlib.ExitStack() as r1:
            e1 = r1.enter_context
            xt = [e1(nc.sbuf_tensor(f"xt{i}", [128, E], F32)) for i in range(2)]
            x1o = [e1(nc.sbuf_tensor(f"x1o{i}", [128, E], BF16)) for i in range(2)]
            jnk = e1(nc.sbuf_tensor("jnk", [128, E], BF16))

            sch = Sched()
            XL = [S_("XL0"), S_("XL1")]
            CH = S_("CH")
            SQ = S_("SQ")
            XW = [S_("XW0"), S_("XW1")]
            for tI in range(NT):
                b = tI % 2
                w = [(CH, (tI - 2) * 6 + 6)] if tI >= 2 else []
                sch.op("g", (lambda tI=tI, b=b: (lambda g: g.dma_start(
                    xt[b][:, :], xs[tI * 128:(tI + 1) * 128, :])))(),
                    waits=w, inc=XL[b], amt=16)
            for tI in range(NT):
                b = tI % 2
                ldw = [(XL[b], (tI // 2 + 1) * 16)]
                sch.op("v", (lambda b=b: (lambda v: v.reduce_sum(
                    statc[:, 0 + b:1 + b], xt[b][:, :], axis=AX.X)))(),
                    waits=ldw + ([(CH, (tI - 2) * 6 + 6)] if tI >= 2 else []),
                    inc=CH)
                sqw = list(ldw)
                if tI >= 1:
                    sqw.append((SQ, 2 * (tI - 1) + 1))
                if tI >= 2:
                    sqw.append((CH, (tI - 2) * 6 + 6))
                sch.op("s", (lambda b=b: (lambda sc: sc.activation(
                    jnk[:, :], xt[b][:, :], AF.Square, accum_out=statc[:, 2 + b:3 + b])))(),
                    waits=sqw, inc=SQ)
                sch.op("v", (lambda b=b: (lambda v: v.tensor_scalar(
                    statc[:, 4 + b:5 + b], statc[:, 0 + b:1 + b], 1.0 / E, None, ALU.mult)))(),
                    waits=[(CH, tI * 6 + 1)], inc=CH)
                sch.op("v", (lambda b=b: (lambda v: v.tensor_mul(
                    statc[:, 6 + b:7 + b], statc[:, 4 + b:5 + b], statc[:, 4 + b:5 + b])))(),
                    waits=[(CH, tI * 6 + 2)], inc=CH)
                sch.op("v", (lambda b=b: (lambda v: v.scalar_tensor_tensor(
                    statc[:, 8 + b:9 + b], statc[:, 2 + b:3 + b], 1.0 / E,
                    statc[:, 6 + b:7 + b], ALU.mult, ALU.subtract)))(),
                    waits=[(CH, tI * 6 + 3), (SQ, 2 * tI + 1)], inc=CH)
                sqw2 = [(CH, tI * 6 + 4)]
                if tI >= 2:
                    sqw2.append((CH, (tI - 2) * 6 + 5))
                sch.op("s", (lambda b=b: (lambda sc: sc.activation(
                    statc[:, 10 + b:11 + b], statc[:, 8 + b:9 + b], AF.Sqrt, bias=epsln_c[:, :])))(),
                    waits=sqw2, inc=SQ)
                sch.op("v", (lambda b=b: (lambda v: v.reciprocal(
                    statc[:, 12 + b:13 + b], statc[:, 10 + b:11 + b])))(),
                    waits=[(SQ, 2 * tI + 2)], inc=CH)
                aw = [(CH, tI * 6 + 5)]
                if tI >= 2:
                    aw.append((XW[b], (tI // 2) * 16))
                sch.op("v", (lambda b=b: (lambda v: v.tensor_scalar(
                    x1o[b][:, :], xt[b][:, :], statc[:, 4 + b:5 + b],
                    statc[:, 12 + b:13 + b], ALU.subtract, ALU.mult)))(),
                    waits=aw, inc=CH)
                sch.op("y", (lambda tI=tI, b=b: (lambda y: y.dma_start(
                    x1_s[tI * 128:(tI + 1) * 128, :], x1o[b][:, :])))(),
                    waits=[(CH, tI * 6 + 6)], inc=XW[b], amt=16)
            sch.emit_into(nc)
        if _PH <= 2:
            nc.finalize(); return nc

        sch = Sched()
        FT = S_("FT")
        XW = [S_("XW0"), S_("XW1")]
        for k in range(NK):
            w = [(XW[0], XW[0].n), (XW[1], XW[1].n)] if k == 0 else []
            sch.op("y", (lambda k=k: (lambda y: y.dma_start(
                featT[:, k * 1024:(k + 1) * 1024], x1_s[:, k * 128:(k + 1) * 128],
                transpose=True)))(), waits=w, inc=FT, amt=16)
        sch.emit_into(nc)
        if _PH <= 3:
            nc.finalize(); return nc

        def mm_phase2(nm_id, w_tiles, n_m, dst, first_waits, scale_fn=None,
                      bias_col=None, extra_mm=None, extra_mm_waits=(),
                      psq=None, wslab=None, oev=None, load_first_waits=(),
                      m_order=None, dst_fn=None, post_m=None):
            sch = Sched()
            WL = [S_(f"{nm_id}WL0"), S_(f"{nm_id}WL1")]
            MM = S_(f"{nm_id}MM")
            EV = S_(f"{nm_id}EV")
            QW = [S_(f"{nm_id}QW0"), S_(f"{nm_id}QW1")]
            morder = list(m_order) if m_order is not None else list(range(n_m))
            nload = [0, 0]
            for mi_, mv_ in enumerate(morder):
                m, b = mi_, mi_ % 2
                w = [(MM, m - 1)] if m >= 2 else []
                if m == 0:
                    w += list(load_first_waits)
                nload[b] += 1
                sch.op("g", (lambda mv_=mv_, b=b: (lambda g: g.dma_start(
                    wslab[b][:, :], w_tiles[mv_, :, :])))(),
                    waits=w, inc=WL[b], amt=16)
                if post_m is not None:
                    post_m(sch, "load", mi_, QW)
            nl = [0, 0]
            for mi_, mv_ in enumerate(morder):
                m, b = mi_, mi_ % 2
                nl[b] += 1
                def emit_mm(t, m=m, b=b):
                    mm = None
                    for g2 in range(2):
                        p = psq[2 * b + g2]
                        for k in range(NK):
                            mm = t.matmul(p[:, :], wslab[b][:, k * 128:(k + 1) * 128],
                                          featT[:, k * 1024 + g2 * 512:
                                                k * 1024 + g2 * 512 + 512],
                                          start=(k == 0),
                                          stop=(k == NK - 1 and extra_mm is None))
                        if extra_mm is not None:
                            mm = extra_mm(t, p, m, g2)
                    return mm
                w = [(WL[b], nl[b] * 16)]
                if m == 0:
                    w += list(first_waits) + list(extra_mm_waits)
                if m >= 2:
                    w.append((EV, (m - 2) * 2 + 2))
                sch.op("t", emit_mm, waits=w, inc=MM)
            for mi_, mv_ in enumerate(morder):
                m, b = mi_, mi_ % 2
                if bias_col is None:
                    def emit_ev(sc, mv_=mv_, b=b, g2=0):
                        scl = scale_fn(mv_) if scale_fn else 1.0
                        return sc.activation(oev[b][:, g2 * 512:(g2 + 1) * 512],
                                             psq[2 * b + g2][:, :], AF.Copy,
                                             scale=scl)
                    eng_ev = "s"
                else:
                    def emit_ev(v, mv_=mv_, b=b, g2=0):
                        return v.tensor_scalar(oev[b][:, g2 * 512:(g2 + 1) * 512],
                                               psq[2 * b + g2][:, :], bias_col(mv_),
                                               None, ALU.add)
                    eng_ev = "v"
                w = [(MM, m + 1)]
                if m >= 2:
                    w.append((QW[b], (m - 2) // 2 * 16 + 16))
                import functools
                sch.op(eng_ev, functools.partial(emit_ev, g2=0), waits=w, inc=EV)
                sch.op(eng_ev, functools.partial(emit_ev, g2=1), waits=[], inc=EV)
                if dst_fn is None:
                    emit_st = (lambda mv_=mv_, b=b: (lambda y: y.dma_start(
                        dst[mv_ * 128:(mv_ + 1) * 128, :], oev[b][:, :])))()
                else:
                    emit_st = (lambda mv_=mv_, b=b: (lambda y: y.dma_start(
                        dst_fn(mv_), oev[b][:, :])))()
                sch.op("y", emit_st, waits=[(EV, (m + 1) * 2)], inc=QW[b], amt=16)
            return sch, QW

        with contextlib.ExitStack() as r2:
            e2 = r2.enter_context
            wslab = [e2(nc.sbuf_tensor(f"wslab{i}", [128, NK * 128], BF16)) for i in range(2)]
            oev = [e2(nc.sbuf_tensor(f"oev{i}", [128, TPC], BF16)) for i in range(2)]
            psq = [e2(nc.psum_tensor(f"psq{i}", [128, 512], F32)) for i in range(4)]
            XW = [S_("XW0"), S_("XW1")]
            AG1 = S_("AG1")
            AGK = S_("AGK")

            def qdst(mv):
                if mv < NK:
                    return qT_s.ap()[mv * 128:(mv + 1) * 128, :]
                return kv_s.ap()[(mv - NK) * 128:(mv - NK + 1) * 128, :]

            def post_m(sch_, kind, mi_, QWl):
                if kind == "load" and mi_ == 4:
                    sch_.op("g", lambda g: g.collective_compute(
                        "AllGather", ALU.bypass,
                        replica_groups=[[0, 1, 2, 3], [4, 5, 6, 7]],
                        ins=[kv_s.ap().opt()], outs=[gkv.ap().opt()]),
                        waits=[(QWl[0], 16), (QWl[1], 16)], inc=AGK)

            sch, QW = mm_phase2("P2", wq_t, NQ, None,
                                first_waits=[(FT, NK * 16)],
                                scale_fn=lambda m: SCALING if m < NK else 1.0,
                                psq=psq, wslab=wslab, oev=oev,
                                load_first_waits=[(XW[0], XW[0].n), (XW[1], XW[1].n)],
                                m_order=[NK, NK + 1] + list(range(NK)),
                                dst_fn=qdst, post_m=post_m)
            sch.op("g", lambda g: g.collective_compute(
                "AllGather", ALU.bypass,
                replica_groups=[[0, 1, 2, 3], [4, 5, 6, 7]],
                ins=[qT_s.ap().opt()], outs=[gat1.ap().opt()]),
                waits=[(QW[0], QW[0].n), (QW[1], QW[1].n)], inc=AG1)
            sch.emit_into(nc)
        if _PH <= 4:
            nc.finalize(); return nc

        with contextlib.ExitStack() as r3:
            e3 = r3.enter_context
            KT = e3(nc.sbuf_tensor("KT", [128, S], BF16))
            Vsb = e3(nc.sbuf_tensor("Vsb", [128, 32 * 160], BF16))
            sqb = e3(nc.sbuf_tensor("sqb", [128, E], BF16))
            qbh = [e3(nc.sbuf_tensor(f"qbh{i}", [128, S], BF16)) for i in range(2)]
            expT = [e3(nc.sbuf_tensor(f"expT{i}", [128, 512], BF16)) for i in range(2)]
            apc = [e3(nc.sbuf_tensor(f"apc{i}", [128, 512], BF16)) for i in range(2)]
            negbT = [e3(nc.sbuf_tensor(f"negbT{i}", [128, 512], BF16)) for i in range(2)]
            recipc = e3(nc.sbuf_tensor("recipc", [128, 8], F32))
            ps_o = [e3(nc.psum_tensor(f"ps_o{i}", [128, 129], F32)) for i in range(4)]
            ps_s = [e3(nc.psum_tensor(f"ps_s{i}", [128, 512], F32)) for i in range(2)]
            ps_q = [e3(nc.psum_tensor(f"ps_q{i}", [1, 512], F32)) for i in range(2)]
            gA = gat1.ap()
            gK = gkv.ap()

            sch = Sched()
            VM = S_("VM")
            KTs = S_("KTs")
            VV = S_("VV")
            KX = S_("KX")
            sch.op("g", lambda g: g.memset(Vsb[:, :], 1.0), waits=[(AGK, 1)], inc=VM)
            for kc in range(32):
                a = kc // 4
                roff = RANKOF[a] * 256
                coff = LOCOF[a] * 512 + (kc % 4) * 128
                sch.op("g", (lambda kc=kc, roff=roff, coff=coff: (lambda g: g.dma_start(
                    KT[:, kc * 128:(kc + 1) * 128],
                    gK[roff:roff + 128, coff:coff + 128])))(),
                    waits=[(AGK, 1)] if kc == 0 else [], inc=KTs, amt=16)
            for kc in range(32):
                a = kc // 4
                roff = RANKOF[a] * 256 + 128
                coff = LOCOF[a] * 512 + (kc % 4) * 128
                sch.op("y", (lambda kc=kc, roff=roff, coff=coff: (lambda y: y.dma_start(
                    Vsb[:, kc * 160:kc * 160 + 128],
                    gK[roff:roff + 128, coff:coff + 128], transpose=True)))(),
                    waits=[(AGK, 1), (VM, 1)] if kc == 0 else [], inc=VV, amt=16)
            sch.op("s", lambda sc: sc.activation(sqb[:, :], KT[:, :], AF.Square),
                   waits=[(KTs, 32 * 16)], inc=KX)
            KQ = S_("KQ")
            for j in range(8):
                sch.op("t", (lambda j=j: (lambda t: t.matmul(
                    ps_q[j % 2][:, :], ones_c128b[:, :], sqb[:, j * 512:(j + 1) * 512],
                    start=True, stop=True)))(),
                    waits=[(KX, 1)] + ([(KQ, j - 1)] if j >= 2 else []), inc=KX)
                sch.op("v", (lambda j=j: (lambda v: v.reduce_max(
                    ktmp[:, j:j + 1], ps_q[j % 2][:, :], axis=AX.X)))(),
                    waits=[(KX, j + 2)], inc=KQ)
            sch.op("v", lambda v: v.reduce_max(kk[:, :], ktmp[:, 0:8], axis=AX.X),
                   waits=[(KQ, 8)], inc=KQ)
            sch.op("v", lambda v: v.tensor_scalar(kneg2[:, :], kk[:, :], 1.024144,
                                                  None, ALU.mult),
                   waits=[(KQ, 9)], inc=KQ)
            sch.emit_into(nc)
            if _PH <= 5:
                nc.finalize(); return nc

            sch = Sched()
            QB = [S_("QB0"), S_("QB1")]
            SQA = S_("SQA")
            SSQ = S_("SSQ")
            QE = S_("QE")
            NB = S_("NB")
            SC = S_("SC")
            NBI = S_("NBI")
            NBT = S_("NBT")
            VA = S_("VA")
            EX = S_("EX")
            MK = S_("MK")
            PV = S_("PV")
            RC = S_("RC")
            FIN = S_("FIN")
            ACW = [S_("ACW0"), S_("ACW1")]
            AG2 = S_("AG2")

            QX = S_("QX")
            def emit_qx(y):
                with y.register("qxa") as r1, y.register("qxb") as r2, \
                        y.register("qxc") as r0:
                    y.reg_load(r0, coffs[0:1, 0:1])
                    y.reg_mul(r1, r0, TPC)
                    for r in range(4):
                        y.reg_add(r2, r1, r * (NK * 128) * TPC)
                        y.dma_start(qex.ap()[r * 1024:(r + 1) * 1024, :],
                                    bass.AP(gat1, r2, [[TPC, 1024], [1, TPC]])) \
                            .then_inc(QX.h, 16)
                return None
            sch.ops.append(("y", [(AG1, AG1.n)], emit_qx, []))
            QX.n += 4 * 16
            qexa = qex.ap()

            def emit_qload_f(hh):
                b = hh % 2
                def emit_qload(y, hh=hh, b=b):
                    for a in range(8):
                        roff = RANKOF[a] * 1024 + hh * 128
                        coff = LOCOF[a] * 512
                        y.dma_start(qbh[b][:, a * 512:(a + 1) * 512],
                                    qexa[roff:roff + 128, coff:coff + 512]) \
                            .then_inc(QB[b].h, 16)
                    return None
                w = [(QX, 4 * 16)] if hh == 0 else []
                if hh >= 2:
                    w.append((PV, (hh - 1) * U))
                sch.ops.append(("y", list(w), emit_qload, []))
                QB[b].n += 8 * 16

            _NHH = int(_os.environ.get("MKNHH", str(HPC)))
            emit_qload_f(0)
            if _NHH > 1:
                emit_qload_f(1)
            for hh in range(_NHH):
                b = hh % 2
                w = [(QB[b], (hh // 2 + 1) * 8 * 16)]
                if hh >= 1:
                    w.append((SSQ, 8 * hh))
                else:
                    w.append((KQ, 10))
                sch.op("s", (lambda b=b: (lambda sc: sc.activation(
                    sqb[:, :], qbh[b][:, :], AF.Square)))(), waits=w, inc=SQA)
                for a in range(8):
                    w = [(SQA, hh + 1)]
                    if hh * 8 + a >= 2:
                        w.append((QE, hh * 8 + a - 1))
                    if hh == 0 and a < 2:
                        w.append((KQ, 10))
                    sch.op("t", (lambda a=a: (lambda t: t.matmul(
                        ps_q[a % 2][:, :], ones_c128b[:, :],
                        sqb[:, a * 512:(a + 1) * 512], start=True, stop=True)))(),
                        waits=w, inc=SSQ)
                    w2 = [(SSQ, hh * 8 + a + 1), (KQ, 10)]
                    if hh >= 1 and a == 0:
                        w2.append((NB, hh))
                    sch.op("s", (lambda a=a: (lambda sc: sc.activation(
                        qnk[0:1, a * 512:(a + 1) * 512], ps_q[a % 2][:, :],
                        AF.Sqrt, scale=kneg2[0:1, 0:1])))(),
                        waits=w2, inc=QE)
                w = [(QE, (hh + 1) * 8)]
                if hh >= 2:
                    w.append((NBI, (hh - 1) * 8))
                sch.op("v", (lambda hh=hh, b=b: (lambda v: v.scalar_tensor_tensor(
                    negb[b][:, :], posrow[:, :], negslope[0:1, hh:hh + 1], qnk[:, :],
                    ALU.mult, ALU.subtract)))(), waits=w, inc=NB)

                def emit_pv(a, kc, gu, pi):
                    w = [(EX, gu + 1)]
                    if kc == 0 and pi >= 1:
                        w.append((FIN, pi * 4))
                    sch.op("t", (lambda a=a, kc=kc, bu=gu % 2: (lambda t: [
                        t.matmul(ps_o[qt][:, :], expT[bu][:, qt * 128:(qt + 1) * 128],
                                 Vsb[:, kc * 160:kc * 160 + 129],
                                 start=(kc == 0), stop=(kc == 4 * a + 3))
                        for qt in range(4)][-1]))(),
                        waits=w, inc=PV)

                pending = None
                for a in range(8):
                    pi = hh * 8 + a
                    bq = pi % 2
                    gu0 = hh * U + UPFX[a]
                    w = [(NB, hh + 1)]
                    if gu0 >= 2:
                        w.append((EX, gu0 - 1))
                    if pi >= 2:
                        w.append((VA, hh * U + UPFX[a - 1] if a >= 2 else
                                  (hh - 1) * U + UPFX[a + 7]))
                    if gu0 == 0:
                        w += [(KTs, 32 * 16), (VV, 32 * 16)]
                    sch.op("t", (lambda a=a, b=b, bu=gu0 % 2: (lambda t: t.matmul(
                        ps_s[bu][:, :], ones_r128b[:, :],
                        negb[b][0:1, a * 512:(a + 1) * 512],
                        start=True, stop=True)))(), waits=w, inc=NBI)
                    sch.op("s", (lambda pi=pi, bu=gu0 % 2: (lambda sc: sc.activation(
                        negbT[pi % 2][:, :], ps_s[bu][:, :], AF.Copy)))(),
                        waits=[(NBI, pi + 1)], inc=NBT)
                    for kc in range(4 * a + 4):
                        u = UPFX[a] + kc
                        gu = hh * U + u
                        diag = kc >= 4 * a
                        w = []
                        if gu >= 2:
                            w.append((EX, gu - 1))
                        if kc == 0:
                            w.append((NBT, pi + 1))
                        sch.op("t", (lambda a=a, kc=kc, b=b, bu=gu % 2: (lambda t: t.matmul(
                            ps_s[bu][:, :], KT[:, kc * 128:(kc + 1) * 128],
                            qbh[b][:, a * 512:(a + 1) * 512],
                            start=True, stop=True)))(),
                            waits=w, inc=SC)
                        if pending is not None:
                            emit_pv(*pending)
                        pending = (a, kc, gu, pi)
                        sch.op("v", (lambda pi=pi, bu=gu % 2: (lambda v: v.tensor_add(
                            ps_s[bu][:, :], ps_s[bu][:, :], negbT[pi % 2][:, :])))(),
                            waits=[(SC, gu + 1), (NBT, pi + 1)], inc=VA)
                        if diag:
                            j = kc - 4 * a
                            sch.op("v", (lambda j=j, bu=gu % 2: (lambda v: v.tensor_add(
                                ps_s[bu][:, :], ps_s[bu][:, :],
                                maskadd[:, j * 512:(j + 1) * 512])))(),
                                waits=[(VA, gu + 1)], inc=MK)
                        w = [(VA, gu + 1)]
                        if diag:
                            w.append((MK, MK.n))
                        if gu >= 2:
                            w.append((PV, gu - 1))
                        sch.op("s", (lambda hh=hh, kc=kc, bu=gu % 2: (lambda sc: sc.activation(
                            expT[bu][:, :], ps_s[bu][:, :], AF.Exp,
                            bias=biastab[:, hh * 32 + kc:hh * 32 + kc + 1])))(),
                            waits=w, inc=EX)
                        if kc == 4 * a + 3:
                            emit_pv(*pending)
                            pending = None
                    for qt in range(4):
                        w = [(PV, hh * U + UPFX[a + 1])]
                        if pi >= 2:
                            w.append((FIN, (pi - 1) * 4))
                        sch.op("v", (lambda qt=qt, bq=bq: (lambda v: v.reciprocal(
                            recipc[:, bq * 4 + qt:bq * 4 + qt + 1],
                            ps_o[qt][:, 128:129])))(), waits=w, inc=RC)
                    for qt in range(4):
                        w = [(RC, pi * 4 + qt + 1)]
                        if pi >= 2:
                            w.append((ACW[bq], (pi // 2) * 16))
                        sch.op("s", (lambda qt=qt, bq=bq: (lambda sc: sc.activation(
                            apc[bq][:, qt * 128:(qt + 1) * 128], ps_o[qt][:, 0:128],
                            AF.Copy, scale=recipc[:, bq * 4 + qt:bq * 4 + qt + 1])))(),
                            waits=w, inc=FIN)
                    sch.op("y", (lambda hh=hh, a=a, bq=bq: (lambda y: y.dma_start(
                        attc_s.ap().rearrange("(q p) c -> p q c", p=128)
                        [:, 4 * a:4 * a + 4, hh * 128:(hh + 1) * 128],
                        apc[bq][:, :].rearrange("p (q c) -> p q c", q=4))))(),
                        waits=[(FIN, (pi + 1) * 4)], inc=ACW[bq], amt=16)
                if hh + 2 < _NHH:
                    emit_qload_f(hh + 2)
            sch.op("g", lambda g: g.collective_compute(
                "AllGather", ALU.bypass,
                replica_groups=[[0, 1, 2, 3], [4, 5, 6, 7]],
                ins=[attc_s.ap().opt()], outs=[gat2.ap().opt()]),
                waits=[(ACW[0], ACW[0].n), (ACW[1], ACW[1].n)], inc=AG2)
            sch.emit_into(nc)
            if _PH <= 6:
                nc.finalize(); return nc

        with contextlib.ExitStack() as r4:
            e4 = r4.enter_context
            wslab = [e4(nc.sbuf_tensor(f"w2slab{i}", [128, NK * 128], BF16)) for i in range(2)]
            oev = [e4(nc.sbuf_tensor(f"o2ev{i}", [128, TPC], BF16)) for i in range(2)]
            psq = [e4(nc.psum_tensor(f"ps2q{i}", [128, 512], F32)) for i in range(4)]
            gB = gat2.ap()
            AG2 = S_("AG2")
            FT2 = S_("FT2")

            sch = Sched()
            AX2 = S_("AX2")
            def emit_atx(g):
                with g.register("axa") as r1, g.register("axb") as r2, \
                        g.register("axc") as r0:
                    for half, cidx in ((0, 1), (1, 2)):
                        g.reg_load(r0, coffs[0:1, cidx:cidx + 1])
                        g.reg_mul(r1, r0, HPC * 128)
                        for J in range(4):
                            g.reg_add(r2, r1, J * S * (HPC * 128))
                            g.dma_start(
                                attex.ap()[J * 1024 + half * 512:
                                           J * 1024 + half * 512 + 512, :],
                                bass.AP(gat2, r2, [[HPC * 128, 512], [1, HPC * 128]])) \
                                .then_inc(AX2.h, 16)
                return None
            sch.ops.append(("g", [(AG2, 1)], emit_atx, []))
            AX2.n += 8 * 16
            atxa = attex.ap()
            def emit_atl(y):
                ii = None
                for k in range(NK):
                    J, HHl = k // HPC, k % HPC
                    ii = y.dma_start(
                        featT[:, k * 1024:(k + 1) * 1024],
                        atxa[J * 1024:(J + 1) * 1024, HHl * 128:(HHl + 1) * 128],
                        transpose=True)
                    ii.then_inc(FT2.h, 16)
                return None
            sch.ops.append(("y", [(AX2, 8 * 16)], emit_atl, []))
            FT2.n += NK * 16
            ACW = [S_("ACW0"), S_("ACW1")]
            sch2, QW2 = mm_phase2("P4", wc_t, NK, proj_s,
                                  first_waits=[(FT2, FT2.n)],
                                  psq=psq, wslab=wslab, oev=oev,
                                  load_first_waits=[(ACW[0], ACW[0].n),
                                                    (ACW[1], ACW[1].n)])
            sch.ops.extend(sch2.ops)
            sch.emit_into(nc)
            PJ = QW2
        if _PH <= 7:
            nc.finalize(); return nc

        with contextlib.ExitStack() as r5:
            e5 = r5.enter_context
            xt1 = e5(nc.sbuf_tensor("xt1b7", [128, E], F32))
            ptok = [e5(nc.sbuf_tensor(f"ptok{i}", [128, E], BF16)) for i in range(2)]
            xmid_sb = e5(nc.sbuf_tensor("xmid_sb", [128, E], F32))
            x2t = e5(nc.sbuf_tensor("x2t", [128, E], BF16))

            sch = Sched()
            XL7 = S_("XL7")
            PT7 = [S_("PT70"), S_("PT71")]
            C7 = S_("C7")
            SQ7 = S_("SQ7")
            XMW = S_("XMW")
            X2W = S_("X2W")
            NS = 8

            def b7_load(tI):
                b = tI % 2
                w = ([(C7, (tI - 1) * NS + 1)] if tI >= 1 else
                     [(PJ[0], PJ[0].n), (PJ[1], PJ[1].n)])
                sch.op("g", (lambda tI=tI: (lambda g: g.dma_start(
                    xt1[:, :], xs[tI * 128:(tI + 1) * 128, :])))(),
                    waits=w, inc=XL7, amt=16)
                w = [(PJ[0], PJ[0].n), (PJ[1], PJ[1].n)] if tI == 0 else []
                if tI >= 2:
                    w.append((C7, (tI - 1) * NS + 1))
                sch.op("y", (lambda tI=tI, b=b: (lambda y: y.dma_start(
                    ptok[b][:, :], proj_s[:, tI * 128:(tI + 1) * 128],
                    transpose=True)))(), waits=w, inc=PT7[b], amt=16)

            b7_load(0)
            b7_load(1)
            for tI in range(NT):
                b = tI % 2
                nb7 = (tI // 2 + 1) * 16
                w = [(XL7, (tI + 1) * 16), (PT7[b], nb7)]
                if tI >= 1:
                    w += [(XMW, tI * 16), (C7, tI * NS)]
                sch.op("v", (lambda b=b: (lambda v: v.tensor_add(
                    xmid_sb[:, :], xt1[:, :], ptok[b][:, :])))(), waits=w, inc=C7)
                sch.op("v", lambda v: v.reduce_sum(statc[:, 0:1], xmid_sb[:, :],
                                                   axis=AX.X),
                       waits=[(C7, tI * NS + 1)], inc=C7)
                sqw7 = [(C7, tI * NS + 1)]
                if tI >= 1:
                    sqw7.append((X2W, tI * 16))
                sch.op("s", lambda sc: sc.activation(x2t[:, :], xmid_sb[:, :],
                                                     AF.Square, accum_out=statc[:, 2:3]),
                       waits=sqw7, inc=SQ7)
                sch.op("v", lambda v: v.tensor_scalar(statc[:, 4:5], statc[:, 0:1],
                                                      1.0 / E, None, ALU.mult),
                       waits=[(C7, tI * NS + 2)], inc=C7)
                sch.op("v", lambda v: v.tensor_mul(statc[:, 6:7], statc[:, 4:5],
                                                   statc[:, 4:5]),
                       waits=[(C7, tI * NS + 3)], inc=C7)
                sch.op("v", lambda v: v.scalar_tensor_tensor(
                    statc[:, 8:9], statc[:, 2:3], 1.0 / E, statc[:, 6:7],
                    ALU.mult, ALU.subtract),
                    waits=[(C7, tI * NS + 4), (SQ7, 2 * tI + 1)], inc=C7)
                sch.op("s", lambda sc: sc.activation(statc[:, 10:11], statc[:, 8:9],
                                                     AF.Sqrt, bias=epsln_c[:, :]),
                       waits=[(C7, tI * NS + 5)], inc=SQ7)
                sch.op("v", lambda v: v.reciprocal(statc[:, 12:13], statc[:, 10:11]),
                       waits=[(SQ7, 2 * tI + 2)], inc=C7)
                w = [(C7, tI * NS + 6)]
                if tI >= 1:
                    w.append((X2W, tI * 16))
                sch.op("v", lambda v: v.tensor_scalar(
                    x2t[:, :], xmid_sb[:, :], statc[:, 4:5], statc[:, 12:13],
                    ALU.subtract, ALU.mult), waits=w, inc=C7)
                sch.op("g", (lambda tI=tI: (lambda g: g.dma_start(
                    xmid_s[tI * 128:(tI + 1) * 128, :], xmid_sb[:, :])))(),
                    waits=[(C7, tI * NS + 7)], inc=XMW, amt=16)
                sch.op("y", (lambda tI=tI: (lambda y: y.dma_start(
                    x2_s[tI * 128:(tI + 1) * 128, :], x2t[:, :])))(),
                    waits=[(C7, tI * NS + 7)], inc=X2W, amt=16)
                sch.op("v", lambda v: v.tensor_copy(statc[0:1, 14:15], statc[0:1, 0:1]),
                       waits=[(C7, tI * NS + 7)], inc=C7)
                if tI + 2 < NT:
                    b7_load(tI + 2)
            sch.emit_into(nc)
        if _PH <= 8:
            nc.finalize(); return nc

        with contextlib.ExitStack() as r6:
            e6 = r6.enter_context
            wslab = [e6(nc.sbuf_tensor(f"w3slab{i}", [128, NK * 128], BF16)) for i in range(2)]
            oev = [e6(nc.sbuf_tensor(f"o3ev{i}", [128, TPC], BF16)) for i in range(2)]
            ztil = [e6(nc.sbuf_tensor(f"ztil{i}", [128, 512], BF16)) for i in range(2)]
            zf32 = e6(nc.sbuf_tensor("zf32", [128, 512], F32))
            alpha = e6(nc.sbuf_tensor("alpha", [128, 32], F32))
            alphaT = e6(nc.sbuf_tensor("alphaT", [16, 1024], F32))
            nmaxc = e6(nc.sbuf_tensor("nmaxc", [128, 2], F32))
            sumc = e6(nc.sbuf_tensor("sumc", [128, 2], F32))
            rsumc = e6(nc.sbuf_tensor("rsumc", [128, 2], F32))
            psq = [e6(nc.psum_tensor(f"ps3q{i}", [128, 512], F32)) for i in range(4)]
            ps_r = [e6(nc.psum_tensor(f"ps_r{i}", [128, 16], F32)) for i in range(2)]
            ps_aT = e6(nc.psum_tensor("ps_aT", [16, 128], F32))
            ps_zw = e6(nc.psum_tensor("ps_zw", [128, 512], F32))

            sch = Sched()
            FT3 = S_("FT3")
            XMW = S_("XMW")
            X2W = S_("X2W")
            for k in range(NK):
                w = [(X2W, X2W.n)] if k == 0 else []
                sch.op("y", (lambda k=k: (lambda y: y.dma_start(
                    featT[:, k * 1024:(k + 1) * 1024], x2_s[:, k * 128:(k + 1) * 128],
                    transpose=True)))(), waits=w, inc=FT3, amt=16)

            RMM = S_("RMM"); RMX = S_("RMX"); AEX = S_("AEX"); ARC = S_("ARC")
            ANM = S_("ANM"); ATR = S_("ATR"); AEV = S_("AEV"); WMM = S_("WMM")
            ZMM = S_("ZMM"); ZEV = S_("ZEV"); ZTL = S_("ZTL")
            mi = S_("mi")
            for g2 in range(2):
                for ts in range(4):
                    i = g2 * 4 + ts
                    bts = i % 2
                    w = [(FT3, NK * 16), (mi, 7)] if i == 0 else []
                    if i >= 2:
                        w.append((AEX, i - 1))
                    def emit_r(t, g2=g2, ts=ts, bts=bts):
                        pr = ps_r[bts]
                        t.matmul(pr[:, :], ones_r128f[:, :], rblnp[:, :],
                                 start=True, stop=False)
                        mm = None
                        for k in range(NK):
                            mm = t.matmul(pr[:, :],
                                          featT[:, k * 1024 + g2 * 512 + ts * 128:
                                                k * 1024 + g2 * 512 + ts * 128 + 128],
                                          rwsb[:, k * 128:k * 128 + 16],
                                          start=False, stop=(k == NK - 1))
                        return mm
                    sch.op("t", emit_r, waits=w, inc=RMM)
                    w = [(RMM, i + 1)]
                    if i >= 2:
                        w.append((AEX, i - 1))
                    sch.op("v", (lambda bts=bts: (lambda v: v.reduce_max(
                        nmaxc[:, bts:bts + 1], ps_r[bts][:, :], axis=AX.X,
                        negate=True)))(), waits=w, inc=RMX)
                    w = [(RMX, i + 1)]
                    if i >= 2:
                        w.append((ATR, i - 1))
                    sch.op("s", (lambda bts=bts: (lambda sc: sc.activation(
                        alpha[:, bts * 16:bts * 16 + 16], ps_r[bts][:, :], AF.Exp,
                        bias=nmaxc[:, bts:bts + 1],
                        accum_out=sumc[:, bts:bts + 1])))(), waits=w, inc=AEX)
                    sch.op("v", (lambda bts=bts: (lambda v: v.reciprocal(
                        rsumc[:, bts:bts + 1], sumc[:, bts:bts + 1])))(),
                        waits=[(AEX, i + 1)], inc=ARC)
                    sch.op("v", (lambda bts=bts: (lambda v: v.tensor_scalar(
                        alpha[:, bts * 16:bts * 16 + 16], alpha[:, bts * 16:bts * 16 + 16],
                        rsumc[:, bts:bts + 1], None, ALU.mult)))(),
                        waits=[(ARC, i + 1)], inc=ANM)
                    w = [(ANM, i + 1)]
                    if i >= 1:
                        w.append((AEV, i))
                    sch.op("t", (lambda bts=bts: (lambda t: t.transpose(
                        ps_aT[:, :], alpha[:, bts * 16:bts * 16 + 16], identf[:, :])))(),
                        waits=w, inc=ATR)
                    sch.op("s", (lambda g2=g2, ts=ts: (lambda sc: sc.activation(
                        alphaT[:, g2 * 512 + ts * 128:g2 * 512 + (ts + 1) * 128],
                        ps_aT[:, :], AF.Copy)))(), waits=[(ATR, i + 1)], inc=AEV)
            for g2 in range(2):
                w = [(FT3, NK * 16)]
                if g2 == 1:
                    w.append((ZTL, 1))
                def emit_z(t, g2=g2):
                    mm = None
                    for k in range(NK):
                        mm = t.matmul(ps_zw[:, :], acat_sb[:, k * 128:(k + 1) * 128],
                                      featT[:, k * 1024 + g2 * 512:
                                            k * 1024 + g2 * 512 + 512],
                                      start=(k == 0), stop=(k == NK - 1))
                    return mm
                sch.op("t", emit_z, waits=w, inc=ZMM)
                zevw = [(ZMM, g2 + 1)]
                if g2 == 1:
                    zevw.append((ZTL, 1))
                sch.op("v", (lambda: (lambda v: v.tensor_copy(zf32[:, :],
                                                              ps_zw[:, :])))(),
                       waits=zevw, inc=ZEV)
                sch.op("t", (lambda g2=g2: (lambda t: t.matmul(
                    ps_zw[:, :], e16sc[:, :], alphaT[:, g2 * 512:(g2 + 1) * 512],
                    start=True, stop=True)))(),
                    waits=[(AEV, g2 * 4 + 4), (ZEV, g2 + 1)], inc=WMM)
                sch.op("v", (lambda g2=g2: (lambda v: v.tensor_tensor(
                    ztil[g2][:, :], zf32[:, :], ps_zw[:, :], ALU.mult)))(),
                    waits=[(ZEV, g2 + 1), (WMM, g2 + 1)], inc=ZTL)

            def ffn_extra(t, p, m, g2):
                return t.matmul(p[:, :], bcat_sb[:, m * 128:(m + 1) * 128],
                                ztil[g2][:, :], start=False, stop=True)
            sch2, QW3 = mm_phase2("P5", wb_t, NK, ffn_s,
                                  first_waits=[(FT3, NK * 16)],
                                  bias_col=lambda m: bbT[:, m:m + 1],
                                  extra_mm=ffn_extra, extra_mm_waits=[(ZTL, 2)],
                                  psq=psq, wslab=wslab, oev=oev,
                                  load_first_waits=[(XMW, XMW.n), (X2W, X2W.n)])
            sch.ops.extend(sch2.ops)
            sch.emit_into(nc)
            FW = QW3
        if _PH <= 9:
            nc.finalize(); return nc

        with contextlib.ExitStack() as r7:
            e7 = r7.enter_context
            xm2 = [e7(nc.sbuf_tensor(f"xm2_{i}", [128, E], F32)) for i in range(2)]
            ftok = [e7(nc.sbuf_tensor(f"ftok{i}", [128, E], BF16)) for i in range(2)]

            sch = Sched()
            XL9 = [S_("XL90"), S_("XL91")]
            FT9 = [S_("FT90"), S_("FT91")]
            A9 = S_("A9")
            OW = [S_("OW0"), S_("OW1")]
            def b9_load(tI):
                b = tI % 2
                w = ([(XMW, XMW.n), (FW[0], FW[0].n), (FW[1], FW[1].n)]
                     if tI == 0 else [])
                if tI >= 2:
                    w.append((OW[b], (tI // 2) * 16))
                sch.op("g", (lambda tI=tI, b=b: (lambda g: g.dma_start(
                    xm2[b][:, :], xmid_s[tI * 128:(tI + 1) * 128, :])))(),
                    waits=w, inc=XL9[b], amt=16)
                w = ([(FW[0], FW[0].n), (FW[1], FW[1].n), (XMW, XMW.n)]
                     if tI == 0 else [])
                if tI >= 2:
                    w.append((A9, tI - 1))
                sch.op("y", (lambda tI=tI, b=b: (lambda y: y.dma_start(
                    ftok[b][:, :], ffn_s[:, tI * 128:(tI + 1) * 128],
                    transpose=True)))(), waits=w, inc=FT9[b], amt=16)

            b9_load(0)
            b9_load(1)
            for tI in range(NT):
                b = tI % 2
                w = [(XL9[b], (tI // 2 + 1) * 16), (FT9[b], (tI // 2 + 1) * 16)]
                sch.op("v", (lambda b=b: (lambda v: v.tensor_add(
                    xm2[b][:, :], xm2[b][:, :], ftok[b][:, :])))(), waits=w, inc=A9)
                sch.op("g", (lambda tI=tI, b=b: (lambda g: g.dma_start(
                    out_t[tI * 128:(tI + 1) * 128, :], xm2[b][:, :])))(),
                    waits=[(A9, tI + 1)], inc=OW[b], amt=16)
                if tI + 2 < NT:
                    b9_load(tI + 2)
            DBG = S_("DBG")
            for (o, src) in dbg_pairs:
                sch.op("g", (lambda o=o, src=src: (lambda g: g.dma_start(o, src)))(),
                       waits=[(A9, NT)], inc=DBG, amt=16)
            def fin_wait(g):
                g.wait_ge(OW[0].h, OW[0].n)
                g.wait_ge(OW[1].h, OW[1].n)
                if dbg_pairs:
                    g.wait_ge(DBG.h, DBG.n)
                return None
            sch.ops.append(("g", [], fin_wait, []))
            sch.emit_into(nc)

    nc.finalize()
    return nc


def _prep_shared(w_qkv, w_c, W_base, b_base, lora_A, lora_B, router_W,
                 router_b, domain_prior, active):
    def tile4(w, n_m):
        return np.ascontiguousarray(
            w.reshape(NK, 128, n_m, 128).transpose(2, 1, 0, 3).reshape(n_m, 128, NK * 128))
    sh = {}
    sh["wq_t"] = tile4(np.asarray(w_qkv, np.float32), NQ)
    sh["wc_t"] = tile4(np.asarray(w_c, np.float32), NK)
    sh["wb_t"] = tile4(np.ascontiguousarray(np.asarray(W_base, np.float32).T), NK)
    A = np.asarray(lora_A, np.float32).reshape(128, E)
    sh["acat_t"] = np.ascontiguousarray(
        A.T.reshape(NK, 128, 128).transpose(1, 0, 2).reshape(128, NK * 128))
    sh["bcat_t"] = np.ascontiguousarray(
        np.asarray(lora_B, np.float32).transpose(0, 2, 1).reshape(128, E))
    rwT = np.zeros((E, 128), np.float32)
    rwT[:, :16] = np.asarray(router_W, np.float32).T
    sh["rw_t"] = np.ascontiguousarray(
        rwT.reshape(NK, 128, 128).transpose(1, 0, 2).reshape(128, NK * 128))
    sh["bbT_i"] = np.ascontiguousarray(np.asarray(b_base, np.float32).reshape(32, 128).T)
    sh["rb_row"] = np.asarray(router_b, np.float32).reshape(1, 16)
    sh["prior_row"] = np.asarray(domain_prior, np.float32).reshape(1, 16)
    sh["active_col"] = np.asarray(active, np.int32).reshape(4, 1)
    sh["posrow_i"] = (np.arange(S, dtype=np.float64) + 8.0).astype(BFD).reshape(1, S)
    mk = np.zeros((128, 4 * 512), np.float32)
    for j in range(4):
        p = np.arange(128)[:, None]
        f = np.arange(512)[None, :]
        mk[:, j * 512:(j + 1) * 512] = np.where(f >= p + 128 * j, 0.0, -1e30)
    sh["maskadd_i"] = mk
    sh["iota16_i"] = np.broadcast_to(np.arange(16, dtype=np.float32)[None, :], (4, 16)).copy()
    e16 = np.zeros((16, 128), np.float32)
    for j in range(16):
        e16[j, j * 8:(j + 1) * 8] = 2.0
    sh["e16sc_i"] = e16
    sh["identf_i"] = np.eye(128, dtype=np.float32)
    return sh


def _prep_core(core):
    sl = _slopes()
    cc = core % 4
    heads = np.arange(8 * cc, 8 * cc + 8)
    bt = np.zeros((128, HPC * 32), np.float32)
    for hh in range(HPC):
        s = sl[heads[hh]]
        for kc in range(32):
            bt[:, hh * 32 + kc] = s * (128.0 * kc + np.arange(128))
    d = {"biastab_i": bt,
         "negslope_i": (-sl[heads]).reshape(1, HPC).astype(np.float32),
         "coff_i": np.asarray([[cc * 1024, cc * 512, (7 - cc) * 512, 0]], np.int32)}
    return d


def kernel(L0_attn_hidden_states, L0_attn_w_qkv, L0_attn_w_c, L0_ffn_W_base,
           L0_ffn_b_base, L0_ffn_lora_A, L0_ffn_lora_B, L0_ffn_router_W,
           L0_ffn_router_b, L0_ffn_domain_prior, L0_ffn_active):
    x = np.asarray(L0_attn_hidden_states, np.float32)
    sh = _prep_shared(L0_attn_w_qkv, L0_attn_w_c, L0_ffn_W_base, L0_ffn_b_base,
                      L0_ffn_lora_A, L0_ffn_lora_B, L0_ffn_router_W,
                      L0_ffn_router_b, L0_ffn_domain_prior, L0_ffn_active)
    dbg = bool(int(os.environ.get("MKDBG", "0")))
    nc = build_nc(dbg=dbg)
    in_maps = []
    for core in range(8):
        b, cc = core // 4, core % 4
        xs = np.concatenate([x[b, 512 * cc:512 * cc + 512],
                             x[b, 512 * (7 - cc):512 * (7 - cc) + 512]], axis=0)
        m = dict(sh)
        m["xs"] = np.ascontiguousarray(xs)
        m.update(_prep_core(core))
        in_maps.append(m)
    res = run_bass_kernel_spmd(nc, in_maps, core_ids=list(range(8)))
    out = np.empty((B, S, E), np.float32)
    for core in range(8):
        b, cc = core // 4, core % 4
        o = np.asarray(res.results[core]["out"])
        out[b, 512 * cc:512 * cc + 512] = o[0:512]
        out[b, 512 * (7 - cc):512 * (7 - cc) + 512] = o[512:1024]
    if dbg:
        kernel._dbg = res.results
    return out


# version 07
# speedup vs baseline: 1.0169x; 1.0169x over previous
import os
import math
import contextlib
import numpy as np
import ml_dtypes
import concourse.bass as bass
import concourse.mybir as mybir
from concourse.bass_utils import run_bass_kernel_spmd

F32 = mybir.dt.float32
BF16 = mybir.dt.bfloat16
I32 = mybir.dt.int32
AF = mybir.ActivationFunctionType
ALU = mybir.AluOpType
AX = mybir.AxisListType
BFD = ml_dtypes.bfloat16

B, S, E, H, D = 2, 4096, 4096, 32, 128
NK = E // 128
NQ = NK + 2
TPC = 1024
NT = TPC // 128
HPC = H // 4
SCALING = float(D) ** -0.5
EPS_LN = 1e-5
RANKOF = [a if a < 4 else 7 - a for a in range(8)]
LOCOF = [0 if a < 4 else 1 for a in range(8)]
UNITS = [(a, kc, kc >= 4 * a) for a in range(8) for kc in range(4 * a + 4)]
U = len(UNITS)
UPFX = [0]
for a in range(8):
    UPFX.append(UPFX[-1] + 4 * a + 4)
NDIAG = np.cumsum([int(d) for (_, _, d) in UNITS]).tolist()


def _slopes():
    base = 2.0 ** (-(2.0 ** (-(math.log2(H) - 3))))
    return (base ** np.arange(1, H + 1, dtype=np.float64)).astype(np.float32)


class SemC:
    def __init__(self, handle):
        self.h = handle
        self.n = 0


class Sched:
    def __init__(self):
        self.ops = []

    def op(self, eng, emit, waits=(), inc=None, amt=1):
        incs = []
        if inc is not None and amt > 0:
            inc.n += amt
            incs = [(inc, amt)]
        self.ops.append((eng, [(s, v) for (s, v) in waits], emit, incs))

    def emit_into(self, nc):
        keys = sorted({o[0] for o in self.ops})
        with nc.Block() as blk:
            for key in keys:
                def body(engv, key=key):
                    for (ek, waits, emit, incs) in self.ops:
                        if ek != key:
                            continue
                        for (s, v) in waits:
                            engv.wait_ge(s.h, v)
                        inst = emit(engv)
                        for (s, amt) in incs:
                            inst.then_inc(s.h, amt)
                getattr(blk, {"t": "tensor", "v": "vector", "s": "scalar",
                              "g": "gpsimd", "y": "sync"}[key])(body)


def build_nc(dbg=False):
    import os as _os
    _PH = int(_os.environ.get("MKPHASES", "99"))
    nc = bass.Bass("TRN2", target_bir_lowering=False)
    ap = lambda t: t.ap()

    xs = ap(nc.dram_tensor("xs", [TPC, E], F32, kind="ExternalInput"))
    wq_t = ap(nc.dram_tensor("wq_t", [NQ, 128, NK * 128], F32, kind="ExternalInput"))
    wc_t = ap(nc.dram_tensor("wc_t", [NK, 128, NK * 128], F32, kind="ExternalInput"))
    wb_t = ap(nc.dram_tensor("wb_t", [NK, 128, NK * 128], F32, kind="ExternalInput"))
    acat_t = ap(nc.dram_tensor("acat_t", [128, NK * 128], F32, kind="ExternalInput"))
    bcat_t = ap(nc.dram_tensor("bcat_t", [128, NK * 128], F32, kind="ExternalInput"))
    rw_t = ap(nc.dram_tensor("rw_t", [128, NK * 128], F32, kind="ExternalInput"))
    bbT_i = ap(nc.dram_tensor("bbT_i", [128, 32], F32, kind="ExternalInput"))
    rb_row = ap(nc.dram_tensor("rb_row", [1, 16], F32, kind="ExternalInput"))
    prior_row = ap(nc.dram_tensor("prior_row", [1, 16], F32, kind="ExternalInput"))
    active_col = ap(nc.dram_tensor("active_col", [4, 1], I32, kind="ExternalInput"))
    biastab_i = ap(nc.dram_tensor("biastab_i", [128, HPC * 32], F32, kind="ExternalInput"))
    negslope_i = ap(nc.dram_tensor("negslope_i", [1, HPC], F32, kind="ExternalInput"))
    posrow_i = ap(nc.dram_tensor("posrow_i", [1, S], BF16, kind="ExternalInput"))
    maskadd_i = ap(nc.dram_tensor("maskadd_i", [128, 4 * 512], F32, kind="ExternalInput"))
    iota16_i = ap(nc.dram_tensor("iota16_i", [4, 16], F32, kind="ExternalInput"))
    e16sc_i = ap(nc.dram_tensor("e16sc_i", [16, 128], F32, kind="ExternalInput"))
    identf_i = ap(nc.dram_tensor("identf_i", [128, 128], F32, kind="ExternalInput"))
    coff_i = ap(nc.dram_tensor("coff_i", [1, 4], I32, kind="ExternalInput"))

    out_t = ap(nc.dram_tensor("out", [TPC, E], F32, kind="ExternalOutput"))

    x1_s = ap(nc.dram_tensor("x1_s", [TPC, E], BF16))
    qT_s = nc.dram_tensor("qT_s", [NK * 128, TPC], BF16)
    kv_s = nc.dram_tensor("kv_s", [256, TPC], BF16)
    gat1 = nc.dram_tensor("gat1", [4 * NK * 128, TPC], BF16)
    gkv = nc.dram_tensor("gkv", [4 * 256, TPC], BF16)
    attc_s = nc.dram_tensor("attc_s", [S, HPC * 128], BF16)
    gat2 = nc.dram_tensor("gat2", [4 * S, HPC * 128], BF16)
    qex = nc.dram_tensor("qex", [4096, TPC], BF16)
    attex = nc.dram_tensor("attex", [4096, TPC], BF16)
    proj_s = ap(nc.dram_tensor("proj_s", [E, TPC], BF16))
    xmid_s = ap(nc.dram_tensor("xmid_s", [TPC, E], F32))
    x2_s = ap(nc.dram_tensor("x2_s", [TPC, E], BF16))
    ffn_s = ap(nc.dram_tensor("ffn_s", [E, TPC], BF16))

    dbg_pairs = []
    if dbg:
        for nm, src in [("d_x1", x1_s), ("d_qkvT", qT_s.ap()),
                        ("d_attc", attc_s.ap()), ("d_proj", proj_s),
                        ("d_xmid", xmid_s), ("d_x2", x2_s), ("d_ffn", ffn_s)]:
            o = ap(nc.dram_tensor(nm, list(src.tensor.shape), src.tensor.dtype,
                                  kind="ExternalOutput"))
            dbg_pairs.append((o, src))

    st = contextlib.ExitStack()
    with st:
        e = st.enter_context

        featT = e(nc.sbuf_tensor("featT", [128, NK * 1024], BF16))
        acat_sb = e(nc.sbuf_tensor("acat_sb", [128, NK * 128], BF16))
        bcat_sb = e(nc.sbuf_tensor("bcat_sb", [128, NK * 128], BF16))
        rwsb = e(nc.sbuf_tensor("rwsb", [128, NK * 128], BF16))
        maskadd = e(nc.sbuf_tensor("maskadd", [128, 4 * 512], F32))
        biastab = e(nc.sbuf_tensor("biastab", [128, HPC * 32], F32))
        posrow = e(nc.sbuf_tensor("posrow", [1, S], BF16))
        negslope = e(nc.sbuf_tensor("negslope", [1, HPC], F32))
        bbT = e(nc.sbuf_tensor("bbT", [128, 32], F32))
        identf = e(nc.sbuf_tensor("identf", [128, 128], F32))
        e16sc = e(nc.sbuf_tensor("e16sc", [16, 128], F32))
        iota16 = e(nc.sbuf_tensor("iota16", [4, 16], F32))
        activef = e(nc.sbuf_tensor("activef", [4, 1], F32))
        activei = e(nc.sbuf_tensor("activei", [4, 1], I32))
        onehotT = e(nc.sbuf_tensor("onehotT", [4, 16], F32))
        rblnp = e(nc.sbuf_tensor("rblnp", [1, 16], F32))
        r16a = e(nc.sbuf_tensor("r16a", [1, 16], F32))
        r16b = e(nc.sbuf_tensor("r16b", [1, 16], F32))
        ones_c128b = e(nc.sbuf_tensor("ones_c128b", [128, 1], BF16))
        ones_r128b = e(nc.sbuf_tensor("ones_r128b", [1, 128], BF16))
        ones_r128f = e(nc.sbuf_tensor("ones_r128f", [1, 128], F32))
        ones_c4f = e(nc.sbuf_tensor("ones_c4f", [4, 1], F32))
        negb = [e(nc.sbuf_tensor(f"negb{i}", [1, S], BF16)) for i in range(2)]
        qnk = e(nc.sbuf_tensor("qnk", [1, S], F32))
        kneg2 = e(nc.sbuf_tensor("kneg2", [1, 1], F32))
        kk = e(nc.sbuf_tensor("kk", [1, 1], F32))
        ktmp = e(nc.sbuf_tensor("ktmp", [1, 16], F32))
        statc = e(nc.sbuf_tensor("statc", [128, 16], F32))
        epsln_c = e(nc.sbuf_tensor("epsln_c", [128, 1], F32))
        eps12 = e(nc.sbuf_tensor("eps12", [1, 1], F32))
        eps30 = e(nc.sbuf_tensor("eps30", [1, 1], F32))
        coffs = e(nc.sbuf_tensor("coffs", [1, 4], I32))

        _sems = {}
        def S_(name):
            if name not in _sems:
                _sems[name] = SemC(e(nc.semaphore(name)))
            return _sems[name]

        sch = Sched()
        c0 = S_("c0")
        for dst, src in [(acat_sb, acat_t), (bcat_sb, bcat_t), (rwsb, rw_t),
                         (maskadd, maskadd_i), (biastab, biastab_i),
                         (posrow, posrow_i), (negslope, negslope_i), (bbT, bbT_i),
                         (identf, identf_i), (e16sc, e16sc_i), (iota16, iota16_i),
                         (activei, active_col), (r16a, rb_row), (r16b, prior_row),
                         (coffs, coff_i)]:
            sch.op("g", (lambda d, s2: (lambda g: g.dma_start(d[:, :], s2)))(dst, src),
                   inc=c0, amt=16)
        for buf, val in [(ones_c128b, 1.0), (ones_r128b, 1.0),
                         (ones_r128f, 1.0), (ones_c4f, 1.0),
                         (epsln_c, EPS_LN), (eps12, 1e-12), (eps30, 1e-30)]:
            sch.op("g", (lambda b2, v2: (lambda g: g.memset(b2[:, :], v2)))(buf, val),
                   inc=c0, amt=0)
        sch.emit_into(nc)

        b0b_st = contextlib.ExitStack()
        ps_cnt = b0b_st.enter_context(nc.psum_tensor("ps_cnt", [1, 16], F32))
        sch = Sched()
        mi = S_("mi")
        sch.op("v", lambda v: v.tensor_copy(activef[:, :], activei[:, :]),
               waits=[(c0, c0.n)], inc=mi)
        sch.op("v", lambda v: v.tensor_scalar(onehotT[:, :], iota16[:, :],
                                              activef[:, :], None, ALU.is_equal),
               waits=[(mi, 1)], inc=mi)
        sch.op("t", lambda t: t.matmul(ps_cnt[:, :], ones_c4f[:, :], onehotT[:, :],
                                       start=True, stop=True),
               waits=[(mi, 2)], inc=mi)
        sch.op("s", lambda sc: sc.activation(r16b[:, :], r16b[:, :], AF.Ln, bias=eps12[0:1, 0:1]),
               waits=[(c0, c0.n), (mi, 3)], inc=mi)
        sch.op("s", lambda sc: sc.activation(rblnp[:, :], ps_cnt[:, :], AF.Ln, bias=eps30[0:1, 0:1]),
               waits=[(mi, 4)], inc=mi)
        sch.op("v", lambda v: v.tensor_add(rblnp[:, :], rblnp[:, :], r16a[:, :]),
               waits=[(mi, 5)], inc=mi)
        sch.op("v", lambda v: v.tensor_add(rblnp[:, :], rblnp[:, :], r16b[:, :]),
               waits=[(mi, 6)], inc=mi)
        sch.emit_into(nc)
        b0b_st.close()
        if _PH <= 1:
            nc.finalize(); return nc

        with contextlib.ExitStack() as r1:
            e1 = r1.enter_context
            xt = [e1(nc.sbuf_tensor(f"xt{i}", [128, E], F32)) for i in range(2)]
            x1o = [e1(nc.sbuf_tensor(f"x1o{i}", [128, E], BF16)) for i in range(2)]
            jnk = e1(nc.sbuf_tensor("jnk", [128, E], BF16))

            sch = Sched()
            XL = [S_("XL0"), S_("XL1")]
            CH = S_("CH")
            SQ = S_("SQ")
            XW = [S_("XW0"), S_("XW1")]
            for tI in range(NT):
                b = tI % 2
                w = [(CH, (tI - 2) * 6 + 6)] if tI >= 2 else []
                sch.op("g", (lambda tI=tI, b=b: (lambda g: g.dma_start(
                    xt[b][:, :], xs[tI * 128:(tI + 1) * 128, :])))(),
                    waits=w, inc=XL[b], amt=16)
            for tI in range(NT):
                b = tI % 2
                ldw = [(XL[b], (tI // 2 + 1) * 16)]
                sch.op("v", (lambda b=b: (lambda v: v.reduce_sum(
                    statc[:, 0 + b:1 + b], xt[b][:, :], axis=AX.X)))(),
                    waits=ldw + ([(CH, (tI - 2) * 6 + 6)] if tI >= 2 else []),
                    inc=CH)
                sqw = list(ldw)
                if tI >= 1:
                    sqw.append((SQ, 2 * (tI - 1) + 1))
                if tI >= 2:
                    sqw.append((CH, (tI - 2) * 6 + 6))
                sch.op("s", (lambda b=b: (lambda sc: sc.activation(
                    jnk[:, :], xt[b][:, :], AF.Square, accum_out=statc[:, 2 + b:3 + b])))(),
                    waits=sqw, inc=SQ)
                sch.op("v", (lambda b=b: (lambda v: v.tensor_scalar(
                    statc[:, 4 + b:5 + b], statc[:, 0 + b:1 + b], 1.0 / E, None, ALU.mult)))(),
                    waits=[(CH, tI * 6 + 1)], inc=CH)
                sch.op("v", (lambda b=b: (lambda v: v.tensor_mul(
                    statc[:, 6 + b:7 + b], statc[:, 4 + b:5 + b], statc[:, 4 + b:5 + b])))(),
                    waits=[(CH, tI * 6 + 2)], inc=CH)
                sch.op("v", (lambda b=b: (lambda v: v.scalar_tensor_tensor(
                    statc[:, 8 + b:9 + b], statc[:, 2 + b:3 + b], 1.0 / E,
                    statc[:, 6 + b:7 + b], ALU.mult, ALU.subtract)))(),
                    waits=[(CH, tI * 6 + 3), (SQ, 2 * tI + 1)], inc=CH)
                sqw2 = [(CH, tI * 6 + 4)]
                if tI >= 2:
                    sqw2.append((CH, (tI - 2) * 6 + 5))
                sch.op("s", (lambda b=b: (lambda sc: sc.activation(
                    statc[:, 10 + b:11 + b], statc[:, 8 + b:9 + b], AF.Sqrt, bias=epsln_c[:, :])))(),
                    waits=sqw2, inc=SQ)
                sch.op("v", (lambda b=b: (lambda v: v.reciprocal(
                    statc[:, 12 + b:13 + b], statc[:, 10 + b:11 + b])))(),
                    waits=[(SQ, 2 * tI + 2)], inc=CH)
                aw = [(CH, tI * 6 + 5)]
                if tI >= 2:
                    aw.append((XW[b], (tI // 2) * 16))
                sch.op("v", (lambda b=b: (lambda v: v.tensor_scalar(
                    x1o[b][:, :], xt[b][:, :], statc[:, 4 + b:5 + b],
                    statc[:, 12 + b:13 + b], ALU.subtract, ALU.mult)))(),
                    waits=aw, inc=CH)
                sch.op("y", (lambda tI=tI, b=b: (lambda y: y.dma_start(
                    x1_s[tI * 128:(tI + 1) * 128, :], x1o[b][:, :])))(),
                    waits=[(CH, tI * 6 + 6)], inc=XW[b], amt=16)
            sch.emit_into(nc)
        if _PH <= 2:
            nc.finalize(); return nc

        sch = Sched()
        FT = S_("FT")
        XW = [S_("XW0"), S_("XW1")]
        for k in range(NK):
            w = [(XW[0], XW[0].n), (XW[1], XW[1].n)] if k == 0 else []
            sch.op("y", (lambda k=k: (lambda y: y.dma_start(
                featT[:, k * 1024:(k + 1) * 1024], x1_s[:, k * 128:(k + 1) * 128],
                transpose=True)))(), waits=w, inc=FT, amt=16)
        sch.emit_into(nc)
        if _PH <= 3:
            nc.finalize(); return nc

        def mm_phase2(nm_id, w_tiles, n_m, dst, first_waits, scale_fn=None,
                      bias_col=None, extra_mm=None, extra_mm_waits=(),
                      psq=None, wslab=None, oev=None, load_first_waits=(),
                      m_order=None, dst_fn=None, post_m=None):
            sch = Sched()
            WL = [S_(f"{nm_id}WL0"), S_(f"{nm_id}WL1")]
            MM = S_(f"{nm_id}MM")
            EV = S_(f"{nm_id}EV")
            QW = [S_(f"{nm_id}QW0"), S_(f"{nm_id}QW1")]
            morder = list(m_order) if m_order is not None else list(range(n_m))
            nload = [0, 0]
            for mi_, mv_ in enumerate(morder):
                m, b = mi_, mi_ % 2
                w = [(MM, m - 1)] if m >= 2 else []
                if m == 0:
                    w += list(load_first_waits)
                nload[b] += 1
                sch.op("g", (lambda mv_=mv_, b=b: (lambda g: g.dma_start(
                    wslab[b][:, :], w_tiles[mv_, :, :])))(),
                    waits=w, inc=WL[b], amt=16)
                if post_m is not None:
                    post_m(sch, "load", mi_, QW)
            nl = [0, 0]
            for mi_, mv_ in enumerate(morder):
                m, b = mi_, mi_ % 2
                nl[b] += 1
                def emit_mm(t, m=m, b=b):
                    mm = None
                    for g2 in range(2):
                        p = psq[2 * b + g2]
                        for k in range(NK):
                            mm = t.matmul(p[:, :], wslab[b][:, k * 128:(k + 1) * 128],
                                          featT[:, k * 1024 + g2 * 512:
                                                k * 1024 + g2 * 512 + 512],
                                          start=(k == 0),
                                          stop=(k == NK - 1 and extra_mm is None))
                        if extra_mm is not None:
                            mm = extra_mm(t, p, m, g2)
                    return mm
                w = [(WL[b], nl[b] * 16)]
                if m == 0:
                    w += list(first_waits) + list(extra_mm_waits)
                if m >= 2:
                    w.append((EV, (m - 2) * 2 + 2))
                sch.op("t", emit_mm, waits=w, inc=MM)
            for mi_, mv_ in enumerate(morder):
                m, b = mi_, mi_ % 2
                if bias_col is None:
                    def emit_ev(sc, mv_=mv_, b=b, g2=0):
                        scl = scale_fn(mv_) if scale_fn else 1.0
                        return sc.activation(oev[b][:, g2 * 512:(g2 + 1) * 512],
                                             psq[2 * b + g2][:, :], AF.Copy,
                                             scale=scl)
                    eng_ev = "s"
                else:
                    def emit_ev(v, mv_=mv_, b=b, g2=0):
                        return v.tensor_scalar(oev[b][:, g2 * 512:(g2 + 1) * 512],
                                               psq[2 * b + g2][:, :], bias_col(mv_),
                                               None, ALU.add)
                    eng_ev = "v"
                w = [(MM, m + 1)]
                if m >= 2:
                    w.append((QW[b], (m - 2) // 2 * 16 + 16))
                import functools
                sch.op(eng_ev, functools.partial(emit_ev, g2=0), waits=w, inc=EV)
                sch.op(eng_ev, functools.partial(emit_ev, g2=1), waits=[], inc=EV)
                if dst_fn is None:
                    emit_st = (lambda mv_=mv_, b=b: (lambda y: y.dma_start(
                        dst[mv_ * 128:(mv_ + 1) * 128, :], oev[b][:, :])))()
                else:
                    emit_st = (lambda mv_=mv_, b=b: (lambda y: y.dma_start(
                        dst_fn(mv_), oev[b][:, :])))()
                sch.op("y", emit_st, waits=[(EV, (m + 1) * 2)], inc=QW[b], amt=16)
            return sch, QW

        with contextlib.ExitStack() as r2:
            e2 = r2.enter_context
            wslab = [e2(nc.sbuf_tensor(f"wslab{i}", [128, NK * 128], BF16)) for i in range(2)]
            oev = [e2(nc.sbuf_tensor(f"oev{i}", [128, TPC], BF16)) for i in range(2)]
            psq = [e2(nc.psum_tensor(f"psq{i}", [128, 512], F32)) for i in range(4)]
            XW = [S_("XW0"), S_("XW1")]
            AG1 = S_("AG1")
            AGK = S_("AGK")

            def qdst(mv):
                if mv < NK:
                    return qT_s.ap()[mv * 128:(mv + 1) * 128, :]
                return kv_s.ap()[(mv - NK) * 128:(mv - NK + 1) * 128, :]

            def post_m(sch_, kind, mi_, QWl):
                if kind == "load" and mi_ == 4:
                    sch_.op("g", lambda g: g.collective_compute(
                        "AllGather", ALU.bypass,
                        replica_groups=[[0, 1, 2, 3], [4, 5, 6, 7]],
                        ins=[kv_s.ap().opt()], outs=[gkv.ap().opt()]),
                        waits=[(QWl[0], 16), (QWl[1], 16)], inc=AGK)

            sch, QW = mm_phase2("P2", wq_t, NQ, None,
                                first_waits=[(FT, NK * 16)],
                                scale_fn=lambda m: SCALING if m < NK else 1.0,
                                psq=psq, wslab=wslab, oev=oev,
                                load_first_waits=[(XW[0], XW[0].n), (XW[1], XW[1].n)],
                                m_order=[NK, NK + 1] + list(range(NK)),
                                dst_fn=qdst, post_m=post_m)
            sch.op("g", lambda g: g.collective_compute(
                "AllGather", ALU.bypass,
                replica_groups=[[0, 1, 2, 3], [4, 5, 6, 7]],
                ins=[qT_s.ap().opt()], outs=[gat1.ap().opt()]),
                waits=[(QW[0], QW[0].n), (QW[1], QW[1].n)], inc=AG1)
            sch.emit_into(nc)
        if _PH <= 4:
            nc.finalize(); return nc

        with contextlib.ExitStack() as r3:
            e3 = r3.enter_context
            KT = e3(nc.sbuf_tensor("KT", [128, S], BF16))
            Vsb = e3(nc.sbuf_tensor("Vsb", [128, 32 * 160], BF16))
            sqb = e3(nc.sbuf_tensor("sqb", [128, E], BF16))
            qbh = [e3(nc.sbuf_tensor(f"qbh{i}", [128, S], BF16)) for i in range(2)]
            expT = [e3(nc.sbuf_tensor(f"expT{i}", [128, 512], BF16)) for i in range(3)]
            apc = [e3(nc.sbuf_tensor(f"apc{i}", [128, 512], BF16)) for i in range(2)]
            negbT = [e3(nc.sbuf_tensor(f"negbT{i}", [128, 512], BF16)) for i in range(2)]
            recipc = e3(nc.sbuf_tensor("recipc", [128, 8], F32))
            ps_o = [e3(nc.psum_tensor(f"ps_o{i}", [128, 129], F32)) for i in range(4)]
            ps_s = [e3(nc.psum_tensor(f"ps_s{i}", [128, 512], F32)) for i in range(3)]
            ps_q = [e3(nc.psum_tensor("ps_q0", [1, 512], F32))] * 2
            gA = gat1.ap()
            gK = gkv.ap()

            sch = Sched()
            VM = S_("VM")
            KTs = S_("KTs")
            VV = S_("VV")
            KX = S_("KX")
            sch.op("g", lambda g: g.memset(Vsb[:, :], 1.0), waits=[(AGK, 1)], inc=VM)
            for kc in range(32):
                a = kc // 4
                roff = RANKOF[a] * 256
                coff = LOCOF[a] * 512 + (kc % 4) * 128
                sch.op("g", (lambda kc=kc, roff=roff, coff=coff: (lambda g: g.dma_start(
                    KT[:, kc * 128:(kc + 1) * 128],
                    gK[roff:roff + 128, coff:coff + 128])))(),
                    waits=[(AGK, 1)] if kc == 0 else [], inc=KTs, amt=16)
            for kc in range(32):
                a = kc // 4
                roff = RANKOF[a] * 256 + 128
                coff = LOCOF[a] * 512 + (kc % 4) * 128
                sch.op("y", (lambda kc=kc, roff=roff, coff=coff: (lambda y: y.dma_start(
                    Vsb[:, kc * 160:kc * 160 + 128],
                    gK[roff:roff + 128, coff:coff + 128], transpose=True)))(),
                    waits=[(AGK, 1), (VM, 1)] if kc == 0 else [], inc=VV, amt=16)
            sch.op("s", lambda sc: sc.activation(sqb[:, :], KT[:, :], AF.Square),
                   waits=[(KTs, 32 * 16)], inc=KX)
            KQ = S_("KQ")
            for j in range(8):
                sch.op("t", (lambda j=j: (lambda t: t.matmul(
                    ps_q[j % 2][:, :], ones_c128b[:, :], sqb[:, j * 512:(j + 1) * 512],
                    start=True, stop=True)))(),
                    waits=[(KX, 1)] + ([(KQ, j)] if j >= 1 else []), inc=KX)
                sch.op("v", (lambda j=j: (lambda v: v.reduce_max(
                    ktmp[:, j:j + 1], ps_q[j % 2][:, :], axis=AX.X)))(),
                    waits=[(KX, j + 2)], inc=KQ)
            sch.op("v", lambda v: v.reduce_max(kk[:, :], ktmp[:, 0:8], axis=AX.X),
                   waits=[(KQ, 8)], inc=KQ)
            sch.op("v", lambda v: v.tensor_scalar(kneg2[:, :], kk[:, :], 1.024144,
                                                  None, ALU.mult),
                   waits=[(KQ, 9)], inc=KQ)
            sch.emit_into(nc)
            if _PH <= 5:
                nc.finalize(); return nc

            sch = Sched()
            QB = [S_("QB0"), S_("QB1")]
            SQA = S_("SQA")
            SSQ = S_("SSQ")
            QE = S_("QE")
            NB = S_("NB")
            SC = S_("SC")
            NBI = S_("NBI")
            NBT = S_("NBT")
            VA = S_("VA")
            EX = S_("EX")
            MK = S_("MK")
            PV = S_("PV")
            RC = S_("RC")
            FIN = S_("FIN")
            ACW = [S_("ACW0"), S_("ACW1")]
            AG2 = S_("AG2")

            QX = S_("QX")
            def emit_qx(y):
                with y.register("qxa") as r1, y.register("qxb") as r2, \
                        y.register("qxc") as r0:
                    y.reg_load(r0, coffs[0:1, 0:1])
                    y.reg_mul(r1, r0, TPC)
                    for r in range(4):
                        y.reg_add(r2, r1, r * (NK * 128) * TPC)
                        y.dma_start(qex.ap()[r * 1024:(r + 1) * 1024, :],
                                    bass.AP(gat1, r2, [[TPC, 1024], [1, TPC]])) \
                            .then_inc(QX.h, 16)
                return None
            sch.ops.append(("y", [(AG1, AG1.n)], emit_qx, []))
            QX.n += 4 * 16
            qexa = qex.ap()

            def emit_qload_f(hh):
                b = hh % 2
                def emit_qload(y, hh=hh, b=b):
                    for a in range(8):
                        roff = RANKOF[a] * 1024 + hh * 128
                        coff = LOCOF[a] * 512
                        y.dma_start(qbh[b][:, a * 512:(a + 1) * 512],
                                    qexa[roff:roff + 128, coff:coff + 512]) \
                            .then_inc(QB[b].h, 16)
                    return None
                w = [(QX, 4 * 16)] if hh == 0 else []
                if hh >= 2:
                    w.append((PV, (hh - 1) * U))
                sch.ops.append(("y", list(w), emit_qload, []))
                QB[b].n += 8 * 16

            _NHH = int(_os.environ.get("MKNHH", str(HPC)))
            emit_qload_f(0)
            if _NHH > 1:
                emit_qload_f(1)
            for hh in range(_NHH):
                b = hh % 2
                w = [(QB[b], (hh // 2 + 1) * 8 * 16)]
                if hh >= 1:
                    w.append((SSQ, 8 * hh))
                else:
                    w.append((KQ, 10))
                sch.op("s", (lambda b=b: (lambda sc: sc.activation(
                    sqb[:, :], qbh[b][:, :], AF.Square)))(), waits=w, inc=SQA)
                for a in range(8):
                    w = [(SQA, hh + 1)]
                    if hh * 8 + a >= 1:
                        w.append((QE, hh * 8 + a))
                    if hh == 0 and a < 1:
                        w.append((KQ, 10))
                    sch.op("t", (lambda a=a: (lambda t: t.matmul(
                        ps_q[a % 2][:, :], ones_c128b[:, :],
                        sqb[:, a * 512:(a + 1) * 512], start=True, stop=True)))(),
                        waits=w, inc=SSQ)
                    w2 = [(SSQ, hh * 8 + a + 1), (KQ, 10)]
                    if hh >= 1 and a == 0:
                        w2.append((NB, hh))
                    sch.op("s", (lambda a=a: (lambda sc: sc.activation(
                        qnk[0:1, a * 512:(a + 1) * 512], ps_q[a % 2][:, :],
                        AF.Sqrt, scale=kneg2[0:1, 0:1])))(),
                        waits=w2, inc=QE)
                w = [(QE, (hh + 1) * 8)]
                if hh >= 2:
                    w.append((NBI, (hh - 1) * 8))
                sch.op("v", (lambda hh=hh, b=b: (lambda v: v.scalar_tensor_tensor(
                    negb[b][:, :], posrow[:, :], negslope[0:1, hh:hh + 1], qnk[:, :],
                    ALU.mult, ALU.subtract)))(), waits=w, inc=NB)

                def emit_pv(a, kc, gu, pi):
                    w = [(EX, gu + 1)]
                    if kc == 0 and pi >= 1:
                        w.append((FIN, pi * 4))
                    sch.op("t", (lambda a=a, kc=kc, bu=gu % 3: (lambda t: [
                        t.matmul(ps_o[qt][:, :], expT[bu][:, qt * 128:(qt + 1) * 128],
                                 Vsb[:, kc * 160:kc * 160 + 129],
                                 start=(kc == 0), stop=(kc == 4 * a + 3))
                        for qt in range(4)][-1]))(),
                        waits=w, inc=PV)

                pending = None
                for a in range(8):
                    pi = hh * 8 + a
                    bq = pi % 2
                    gu0 = hh * U + UPFX[a]
                    w = [(NB, hh + 1)]
                    if gu0 >= 3:
                        w.append((EX, gu0 - 2))
                    if pi >= 2:
                        w.append((VA, hh * U + UPFX[a - 1] if a >= 2 else
                                  (hh - 1) * U + UPFX[a + 7]))
                    if gu0 == 0:
                        w += [(KTs, 32 * 16), (VV, 32 * 16)]
                    sch.op("t", (lambda a=a, b=b, bu=gu0 % 3: (lambda t: t.matmul(
                        ps_s[bu][:, :], ones_r128b[:, :],
                        negb[b][0:1, a * 512:(a + 1) * 512],
                        start=True, stop=True)))(), waits=w, inc=NBI)
                    sch.op("s", (lambda pi=pi, bu=gu0 % 3: (lambda sc: sc.activation(
                        negbT[pi % 2][:, :], ps_s[bu][:, :], AF.Copy)))(),
                        waits=[(NBI, pi + 1)], inc=NBT)
                    for kc in range(4 * a + 4):
                        u = UPFX[a] + kc
                        gu = hh * U + u
                        diag = kc >= 4 * a
                        w = []
                        if gu >= 3:
                            w.append((EX, gu - 2))
                        if kc == 0:
                            w.append((NBT, pi + 1))
                        sch.op("t", (lambda a=a, kc=kc, b=b, bu=gu % 3: (lambda t: t.matmul(
                            ps_s[bu][:, :], KT[:, kc * 128:(kc + 1) * 128],
                            qbh[b][:, a * 512:(a + 1) * 512],
                            start=True, stop=True)))(),
                            waits=w, inc=SC)
                        if pending is not None:
                            emit_pv(*pending)
                        pending = (a, kc, gu, pi)
                        sch.op("v", (lambda pi=pi, bu=gu % 3: (lambda v: v.tensor_add(
                            ps_s[bu][:, :], ps_s[bu][:, :], negbT[pi % 2][:, :])))(),
                            waits=[(SC, gu + 1), (NBT, pi + 1)], inc=VA)
                        if diag:
                            j = kc - 4 * a
                            sch.op("v", (lambda j=j, bu=gu % 3: (lambda v: v.tensor_add(
                                ps_s[bu][:, :], ps_s[bu][:, :],
                                maskadd[:, j * 512:(j + 1) * 512])))(),
                                waits=[(VA, gu + 1)], inc=MK)
                        w = [(VA, gu + 1)]
                        if diag:
                            w.append((MK, MK.n))
                        if gu >= 3:
                            w.append((PV, gu - 2))
                        sch.op("s", (lambda hh=hh, kc=kc, bu=gu % 3: (lambda sc: sc.activation(
                            expT[bu][:, :], ps_s[bu][:, :], AF.Exp,
                            bias=biastab[:, hh * 32 + kc:hh * 32 + kc + 1])))(),
                            waits=w, inc=EX)
                        if kc == 4 * a + 3:
                            emit_pv(*pending)
                            pending = None
                    for qt in range(4):
                        w = [(PV, hh * U + UPFX[a + 1])]
                        if pi >= 2:
                            w.append((FIN, (pi - 1) * 4))
                        sch.op("v", (lambda qt=qt, bq=bq: (lambda v: v.reciprocal(
                            recipc[:, bq * 4 + qt:bq * 4 + qt + 1],
                            ps_o[qt][:, 128:129])))(), waits=w, inc=RC)
                    for qt in range(4):
                        w = [(RC, pi * 4 + qt + 1)]
                        if pi >= 2:
                            w.append((ACW[bq], (pi // 2) * 16))
                        sch.op("v", (lambda qt=qt, bq=bq: (lambda v: v.tensor_scalar(
                            apc[bq][:, qt * 128:(qt + 1) * 128],
                            ps_o[qt][:, 0:128],
                            recipc[:, bq * 4 + qt:bq * 4 + qt + 1], None, ALU.mult)))(),
                            waits=w, inc=FIN)
                    sch.op("y", (lambda hh=hh, a=a, bq=bq: (lambda y: y.dma_start(
                        attc_s.ap().rearrange("(q p) c -> p q c", p=128)
                        [:, 4 * a:4 * a + 4, hh * 128:(hh + 1) * 128],
                        apc[bq][:, :].rearrange("p (q c) -> p q c", q=4))))(),
                        waits=[(FIN, (pi + 1) * 4)], inc=ACW[bq], amt=16)
                if hh + 2 < _NHH:
                    emit_qload_f(hh + 2)
            sch.op("g", lambda g: g.collective_compute(
                "AllGather", ALU.bypass,
                replica_groups=[[0, 1, 2, 3], [4, 5, 6, 7]],
                ins=[attc_s.ap().opt()], outs=[gat2.ap().opt()]),
                waits=[(ACW[0], ACW[0].n), (ACW[1], ACW[1].n)], inc=AG2)
            sch.emit_into(nc)
            if _PH <= 6:
                nc.finalize(); return nc

        with contextlib.ExitStack() as r4:
            e4 = r4.enter_context
            wslab = [e4(nc.sbuf_tensor(f"w2slab{i}", [128, NK * 128], BF16)) for i in range(2)]
            oev = [e4(nc.sbuf_tensor(f"o2ev{i}", [128, TPC], BF16)) for i in range(2)]
            psq = [e4(nc.psum_tensor(f"ps2q{i}", [128, 512], F32)) for i in range(4)]
            gB = gat2.ap()
            AG2 = S_("AG2")
            FT2 = S_("FT2")

            sch = Sched()
            AX2 = S_("AX2")
            def emit_atx(g):
                with g.register("axa") as r1, g.register("axb") as r2, \
                        g.register("axc") as r0:
                    for half, cidx in ((0, 1), (1, 2)):
                        g.reg_load(r0, coffs[0:1, cidx:cidx + 1])
                        g.reg_mul(r1, r0, HPC * 128)
                        for J in range(4):
                            g.reg_add(r2, r1, J * S * (HPC * 128))
                            g.dma_start(
                                attex.ap()[J * 1024 + half * 512:
                                           J * 1024 + half * 512 + 512, :],
                                bass.AP(gat2, r2, [[HPC * 128, 512], [1, HPC * 128]])) \
                                .then_inc(AX2.h, 16)
                return None
            sch.ops.append(("g", [(AG2, 1)], emit_atx, []))
            AX2.n += 8 * 16
            atxa = attex.ap()
            def emit_atl(y):
                ii = None
                for k in range(NK):
                    J, HHl = k // HPC, k % HPC
                    ii = y.dma_start(
                        featT[:, k * 1024:(k + 1) * 1024],
                        atxa[J * 1024:(J + 1) * 1024, HHl * 128:(HHl + 1) * 128],
                        transpose=True)
                    ii.then_inc(FT2.h, 16)
                return None
            sch.ops.append(("y", [(AX2, 8 * 16)], emit_atl, []))
            FT2.n += NK * 16
            ACW = [S_("ACW0"), S_("ACW1")]
            sch2, QW2 = mm_phase2("P4", wc_t, NK, proj_s,
                                  first_waits=[(FT2, FT2.n)],
                                  psq=psq, wslab=wslab, oev=oev,
                                  load_first_waits=[(ACW[0], ACW[0].n),
                                                    (ACW[1], ACW[1].n)])
            sch.ops.extend(sch2.ops)
            sch.emit_into(nc)
            PJ = QW2
        if _PH <= 7:
            nc.finalize(); return nc

        with contextlib.ExitStack() as r5:
            e5 = r5.enter_context
            xt1 = e5(nc.sbuf_tensor("xt1b7", [128, E], F32))
            ptok = [e5(nc.sbuf_tensor(f"ptok{i}", [128, E], BF16)) for i in range(2)]
            xmid_sb = e5(nc.sbuf_tensor("xmid_sb", [128, E], F32))
            x2t = e5(nc.sbuf_tensor("x2t", [128, E], BF16))

            sch = Sched()
            XL7 = S_("XL7")
            PT7 = [S_("PT70"), S_("PT71")]
            C7 = S_("C7")
            SQ7 = S_("SQ7")
            XMW = S_("XMW")
            X2W = S_("X2W")
            NS = 8

            def b7_load(tI):
                b = tI % 2
                w = ([(C7, (tI - 1) * NS + 1)] if tI >= 1 else
                     [(PJ[0], PJ[0].n), (PJ[1], PJ[1].n)])
                sch.op("g", (lambda tI=tI: (lambda g: g.dma_start(
                    xt1[:, :], xs[tI * 128:(tI + 1) * 128, :])))(),
                    waits=w, inc=XL7, amt=16)
                w = [(PJ[0], PJ[0].n), (PJ[1], PJ[1].n)] if tI == 0 else []
                if tI >= 2:
                    w.append((C7, (tI - 1) * NS + 1))
                sch.op("y", (lambda tI=tI, b=b: (lambda y: y.dma_start(
                    ptok[b][:, :], proj_s[:, tI * 128:(tI + 1) * 128],
                    transpose=True)))(), waits=w, inc=PT7[b], amt=16)

            b7_load(0)
            b7_load(1)
            for tI in range(NT):
                b = tI % 2
                nb7 = (tI // 2 + 1) * 16
                w = [(XL7, (tI + 1) * 16), (PT7[b], nb7)]
                if tI >= 1:
                    w += [(XMW, tI * 16), (C7, tI * NS)]
                sch.op("v", (lambda b=b: (lambda v: v.tensor_add(
                    xmid_sb[:, :], xt1[:, :], ptok[b][:, :])))(), waits=w, inc=C7)
                sch.op("v", lambda v: v.reduce_sum(statc[:, 0:1], xmid_sb[:, :],
                                                   axis=AX.X),
                       waits=[(C7, tI * NS + 1)], inc=C7)
                sqw7 = [(C7, tI * NS + 1)]
                if tI >= 1:
                    sqw7.append((X2W, tI * 16))
                sch.op("s", lambda sc: sc.activation(x2t[:, :], xmid_sb[:, :],
                                                     AF.Square, accum_out=statc[:, 2:3]),
                       waits=sqw7, inc=SQ7)
                sch.op("v", lambda v: v.tensor_scalar(statc[:, 4:5], statc[:, 0:1],
                                                      1.0 / E, None, ALU.mult),
                       waits=[(C7, tI * NS + 2)], inc=C7)
                sch.op("v", lambda v: v.tensor_mul(statc[:, 6:7], statc[:, 4:5],
                                                   statc[:, 4:5]),
                       waits=[(C7, tI * NS + 3)], inc=C7)
                sch.op("v", lambda v: v.scalar_tensor_tensor(
                    statc[:, 8:9], statc[:, 2:3], 1.0 / E, statc[:, 6:7],
                    ALU.mult, ALU.subtract),
                    waits=[(C7, tI * NS + 4), (SQ7, 2 * tI + 1)], inc=C7)
                sch.op("s", lambda sc: sc.activation(statc[:, 10:11], statc[:, 8:9],
                                                     AF.Sqrt, bias=epsln_c[:, :]),
                       waits=[(C7, tI * NS + 5)], inc=SQ7)
                sch.op("v", lambda v: v.reciprocal(statc[:, 12:13], statc[:, 10:11]),
                       waits=[(SQ7, 2 * tI + 2)], inc=C7)
                w = [(C7, tI * NS + 6)]
                if tI >= 1:
                    w.append((X2W, tI * 16))
                sch.op("v", lambda v: v.tensor_scalar(
                    x2t[:, :], xmid_sb[:, :], statc[:, 4:5], statc[:, 12:13],
                    ALU.subtract, ALU.mult), waits=w, inc=C7)
                sch.op("g", (lambda tI=tI: (lambda g: g.dma_start(
                    xmid_s[tI * 128:(tI + 1) * 128, :], xmid_sb[:, :])))(),
                    waits=[(C7, tI * NS + 7)], inc=XMW, amt=16)
                sch.op("y", (lambda tI=tI: (lambda y: y.dma_start(
                    x2_s[tI * 128:(tI + 1) * 128, :], x2t[:, :])))(),
                    waits=[(C7, tI * NS + 7)], inc=X2W, amt=16)
                sch.op("v", lambda v: v.tensor_copy(statc[0:1, 14:15], statc[0:1, 0:1]),
                       waits=[(C7, tI * NS + 7)], inc=C7)
                if tI + 2 < NT:
                    b7_load(tI + 2)
            sch.emit_into(nc)
        if _PH <= 8:
            nc.finalize(); return nc

        with contextlib.ExitStack() as r6:
            e6 = r6.enter_context
            wslab = [e6(nc.sbuf_tensor(f"w3slab{i}", [128, NK * 128], BF16)) for i in range(2)]
            oev = [e6(nc.sbuf_tensor(f"o3ev{i}", [128, TPC], BF16)) for i in range(2)]
            ztil = [e6(nc.sbuf_tensor(f"ztil{i}", [128, 512], BF16)) for i in range(2)]
            zf32 = e6(nc.sbuf_tensor("zf32", [128, 512], F32))
            alpha = e6(nc.sbuf_tensor("alpha", [128, 32], F32))
            alphaT = e6(nc.sbuf_tensor("alphaT", [16, 1024], F32))
            nmaxc = e6(nc.sbuf_tensor("nmaxc", [128, 2], F32))
            sumc = e6(nc.sbuf_tensor("sumc", [128, 2], F32))
            rsumc = e6(nc.sbuf_tensor("rsumc", [128, 2], F32))
            psq = [e6(nc.psum_tensor(f"ps3q{i}", [128, 512], F32)) for i in range(4)]
            ps_r = [e6(nc.psum_tensor(f"ps_r{i}", [128, 16], F32)) for i in range(2)]
            ps_aT = e6(nc.psum_tensor("ps_aT", [16, 128], F32))
            ps_zw = e6(nc.psum_tensor("ps_zw", [128, 512], F32))

            sch = Sched()
            FT3 = S_("FT3")
            XMW = S_("XMW")
            X2W = S_("X2W")
            for k in range(NK):
                w = [(X2W, X2W.n)] if k == 0 else []
                sch.op("y", (lambda k=k: (lambda y: y.dma_start(
                    featT[:, k * 1024:(k + 1) * 1024], x2_s[:, k * 128:(k + 1) * 128],
                    transpose=True)))(), waits=w, inc=FT3, amt=16)

            RMM = S_("RMM"); RMX = S_("RMX"); AEX = S_("AEX"); ARC = S_("ARC")
            ANM = S_("ANM"); ATR = S_("ATR"); AEV = S_("AEV"); WMM = S_("WMM")
            ZMM = S_("ZMM"); ZEV = S_("ZEV"); ZTL = S_("ZTL")
            mi = S_("mi")
            for g2 in range(2):
                for ts in range(4):
                    i = g2 * 4 + ts
                    bts = i % 2
                    w = [(FT3, NK * 16), (mi, 7)] if i == 0 else []
                    if i >= 2:
                        w.append((AEX, i - 1))
                    def emit_r(t, g2=g2, ts=ts, bts=bts):
                        pr = ps_r[bts]
                        t.matmul(pr[:, :], ones_r128f[:, :], rblnp[:, :],
                                 start=True, stop=False)
                        mm = None
                        for k in range(NK):
                            mm = t.matmul(pr[:, :],
                                          featT[:, k * 1024 + g2 * 512 + ts * 128:
                                                k * 1024 + g2 * 512 + ts * 128 + 128],
                                          rwsb[:, k * 128:k * 128 + 16],
                                          start=False, stop=(k == NK - 1))
                        return mm
                    sch.op("t", emit_r, waits=w, inc=RMM)
                    w = [(RMM, i + 1)]
                    if i >= 2:
                        w.append((AEX, i - 1))
                    sch.op("v", (lambda bts=bts: (lambda v: v.reduce_max(
                        nmaxc[:, bts:bts + 1], ps_r[bts][:, :], axis=AX.X,
                        negate=True)))(), waits=w, inc=RMX)
                    w = [(RMX, i + 1)]
                    if i >= 2:
                        w.append((ATR, i - 1))
                    sch.op("s", (lambda bts=bts: (lambda sc: sc.activation(
                        alpha[:, bts * 16:bts * 16 + 16], ps_r[bts][:, :], AF.Exp,
                        bias=nmaxc[:, bts:bts + 1],
                        accum_out=sumc[:, bts:bts + 1])))(), waits=w, inc=AEX)
                    sch.op("v", (lambda bts=bts: (lambda v: v.reciprocal(
                        rsumc[:, bts:bts + 1], sumc[:, bts:bts + 1])))(),
                        waits=[(AEX, i + 1)], inc=ARC)
                    sch.op("v", (lambda bts=bts: (lambda v: v.tensor_scalar(
                        alpha[:, bts * 16:bts * 16 + 16], alpha[:, bts * 16:bts * 16 + 16],
                        rsumc[:, bts:bts + 1], None, ALU.mult)))(),
                        waits=[(ARC, i + 1)], inc=ANM)
                    w = [(ANM, i + 1)]
                    if i >= 1:
                        w.append((AEV, i))
                    sch.op("t", (lambda bts=bts: (lambda t: t.transpose(
                        ps_aT[:, :], alpha[:, bts * 16:bts * 16 + 16], identf[:, :])))(),
                        waits=w, inc=ATR)
                    sch.op("s", (lambda g2=g2, ts=ts: (lambda sc: sc.activation(
                        alphaT[:, g2 * 512 + ts * 128:g2 * 512 + (ts + 1) * 128],
                        ps_aT[:, :], AF.Copy)))(), waits=[(ATR, i + 1)], inc=AEV)
            for g2 in range(2):
                w = [(FT3, NK * 16)]
                if g2 == 1:
                    w.append((ZTL, 1))
                def emit_z(t, g2=g2):
                    mm = None
                    for k in range(NK):
                        mm = t.matmul(ps_zw[:, :], acat_sb[:, k * 128:(k + 1) * 128],
                                      featT[:, k * 1024 + g2 * 512:
                                            k * 1024 + g2 * 512 + 512],
                                      start=(k == 0), stop=(k == NK - 1))
                    return mm
                sch.op("t", emit_z, waits=w, inc=ZMM)
                zevw = [(ZMM, g2 + 1)]
                if g2 == 1:
                    zevw.append((ZTL, 1))
                sch.op("v", (lambda: (lambda v: v.tensor_copy(zf32[:, :],
                                                              ps_zw[:, :])))(),
                       waits=zevw, inc=ZEV)
                sch.op("t", (lambda g2=g2: (lambda t: t.matmul(
                    ps_zw[:, :], e16sc[:, :], alphaT[:, g2 * 512:(g2 + 1) * 512],
                    start=True, stop=True)))(),
                    waits=[(AEV, g2 * 4 + 4), (ZEV, g2 + 1)], inc=WMM)
                sch.op("v", (lambda g2=g2: (lambda v: v.tensor_tensor(
                    ztil[g2][:, :], zf32[:, :], ps_zw[:, :], ALU.mult)))(),
                    waits=[(ZEV, g2 + 1), (WMM, g2 + 1)], inc=ZTL)

            def ffn_extra(t, p, m, g2):
                return t.matmul(p[:, :], bcat_sb[:, m * 128:(m + 1) * 128],
                                ztil[g2][:, :], start=False, stop=True)
            sch2, QW3 = mm_phase2("P5", wb_t, NK, ffn_s,
                                  first_waits=[(FT3, NK * 16)],
                                  bias_col=lambda m: bbT[:, m:m + 1],
                                  extra_mm=ffn_extra, extra_mm_waits=[(ZTL, 2)],
                                  psq=psq, wslab=wslab, oev=oev,
                                  load_first_waits=[(XMW, XMW.n), (X2W, X2W.n)])
            sch.ops.extend(sch2.ops)
            sch.emit_into(nc)
            FW = QW3
        if _PH <= 9:
            nc.finalize(); return nc

        with contextlib.ExitStack() as r7:
            e7 = r7.enter_context
            xm2 = [e7(nc.sbuf_tensor(f"xm2_{i}", [128, E], F32)) for i in range(2)]
            ftok = [e7(nc.sbuf_tensor(f"ftok{i}", [128, E], BF16)) for i in range(2)]

            sch = Sched()
            XL9 = [S_("XL90"), S_("XL91")]
            FT9 = [S_("FT90"), S_("FT91")]
            A9 = S_("A9")
            OW = [S_("OW0"), S_("OW1")]
            def b9_load(tI):
                b = tI % 2
                w = ([(XMW, XMW.n), (FW[0], FW[0].n), (FW[1], FW[1].n)]
                     if tI == 0 else [])
                if tI >= 2:
                    w.append((OW[b], (tI // 2) * 16))
                sch.op("g", (lambda tI=tI, b=b: (lambda g: g.dma_start(
                    xm2[b][:, :], xmid_s[tI * 128:(tI + 1) * 128, :])))(),
                    waits=w, inc=XL9[b], amt=16)
                w = ([(FW[0], FW[0].n), (FW[1], FW[1].n), (XMW, XMW.n)]
                     if tI == 0 else [])
                if tI >= 2:
                    w.append((A9, tI - 1))
                sch.op("y", (lambda tI=tI, b=b: (lambda y: y.dma_start(
                    ftok[b][:, :], ffn_s[:, tI * 128:(tI + 1) * 128],
                    transpose=True)))(), waits=w, inc=FT9[b], amt=16)

            b9_load(0)
            b9_load(1)
            for tI in range(NT):
                b = tI % 2
                w = [(XL9[b], (tI // 2 + 1) * 16), (FT9[b], (tI // 2 + 1) * 16)]
                sch.op("v", (lambda b=b: (lambda v: v.tensor_add(
                    xm2[b][:, :], xm2[b][:, :], ftok[b][:, :])))(), waits=w, inc=A9)
                sch.op("g", (lambda tI=tI, b=b: (lambda g: g.dma_start(
                    out_t[tI * 128:(tI + 1) * 128, :], xm2[b][:, :])))(),
                    waits=[(A9, tI + 1)], inc=OW[b], amt=16)
                if tI + 2 < NT:
                    b9_load(tI + 2)
            DBG = S_("DBG")
            for (o, src) in dbg_pairs:
                sch.op("g", (lambda o=o, src=src: (lambda g: g.dma_start(o, src)))(),
                       waits=[(A9, NT)], inc=DBG, amt=16)
            def fin_wait(g):
                g.wait_ge(OW[0].h, OW[0].n)
                g.wait_ge(OW[1].h, OW[1].n)
                if dbg_pairs:
                    g.wait_ge(DBG.h, DBG.n)
                return None
            sch.ops.append(("g", [], fin_wait, []))
            sch.emit_into(nc)

    nc.finalize()
    return nc


def _prep_shared(w_qkv, w_c, W_base, b_base, lora_A, lora_B, router_W,
                 router_b, domain_prior, active):
    def tile4(w, n_m):
        return np.ascontiguousarray(
            w.reshape(NK, 128, n_m, 128).transpose(2, 1, 0, 3).reshape(n_m, 128, NK * 128))
    sh = {}
    sh["wq_t"] = tile4(np.asarray(w_qkv, np.float32), NQ)
    sh["wc_t"] = tile4(np.asarray(w_c, np.float32), NK)
    sh["wb_t"] = tile4(np.ascontiguousarray(np.asarray(W_base, np.float32).T), NK)
    A = np.asarray(lora_A, np.float32).reshape(128, E)
    sh["acat_t"] = np.ascontiguousarray(
        A.T.reshape(NK, 128, 128).transpose(1, 0, 2).reshape(128, NK * 128))
    sh["bcat_t"] = np.ascontiguousarray(
        np.asarray(lora_B, np.float32).transpose(0, 2, 1).reshape(128, E))
    rwT = np.zeros((E, 128), np.float32)
    rwT[:, :16] = np.asarray(router_W, np.float32).T
    sh["rw_t"] = np.ascontiguousarray(
        rwT.reshape(NK, 128, 128).transpose(1, 0, 2).reshape(128, NK * 128))
    sh["bbT_i"] = np.ascontiguousarray(np.asarray(b_base, np.float32).reshape(32, 128).T)
    sh["rb_row"] = np.asarray(router_b, np.float32).reshape(1, 16)
    sh["prior_row"] = np.asarray(domain_prior, np.float32).reshape(1, 16)
    sh["active_col"] = np.asarray(active, np.int32).reshape(4, 1)
    sh["posrow_i"] = (np.arange(S, dtype=np.float64) + 8.0).astype(BFD).reshape(1, S)
    mk = np.zeros((128, 4 * 512), np.float32)
    for j in range(4):
        p = np.arange(128)[:, None]
        f = np.arange(512)[None, :]
        mk[:, j * 512:(j + 1) * 512] = np.where(f >= p + 128 * j, 0.0, -1e30)
    sh["maskadd_i"] = mk
    sh["iota16_i"] = np.broadcast_to(np.arange(16, dtype=np.float32)[None, :], (4, 16)).copy()
    e16 = np.zeros((16, 128), np.float32)
    for j in range(16):
        e16[j, j * 8:(j + 1) * 8] = 2.0
    sh["e16sc_i"] = e16
    sh["identf_i"] = np.eye(128, dtype=np.float32)
    return sh


def _prep_core(core):
    sl = _slopes()
    cc = core % 4
    heads = np.arange(8 * cc, 8 * cc + 8)
    bt = np.zeros((128, HPC * 32), np.float32)
    for hh in range(HPC):
        s = sl[heads[hh]]
        for kc in range(32):
            bt[:, hh * 32 + kc] = s * (128.0 * kc + np.arange(128))
    d = {"biastab_i": bt,
         "negslope_i": (-sl[heads]).reshape(1, HPC).astype(np.float32),
         "coff_i": np.asarray([[cc * 1024, cc * 512, (7 - cc) * 512, 0]], np.int32)}
    return d


def kernel(L0_attn_hidden_states, L0_attn_w_qkv, L0_attn_w_c, L0_ffn_W_base,
           L0_ffn_b_base, L0_ffn_lora_A, L0_ffn_lora_B, L0_ffn_router_W,
           L0_ffn_router_b, L0_ffn_domain_prior, L0_ffn_active):
    x = np.asarray(L0_attn_hidden_states, np.float32)
    sh = _prep_shared(L0_attn_w_qkv, L0_attn_w_c, L0_ffn_W_base, L0_ffn_b_base,
                      L0_ffn_lora_A, L0_ffn_lora_B, L0_ffn_router_W,
                      L0_ffn_router_b, L0_ffn_domain_prior, L0_ffn_active)
    dbg = bool(int(os.environ.get("MKDBG", "0")))
    nc = build_nc(dbg=dbg)
    in_maps = []
    for core in range(8):
        b, cc = core // 4, core % 4
        xs = np.concatenate([x[b, 512 * cc:512 * cc + 512],
                             x[b, 512 * (7 - cc):512 * (7 - cc) + 512]], axis=0)
        m = dict(sh)
        m["xs"] = np.ascontiguousarray(xs)
        m.update(_prep_core(core))
        in_maps.append(m)
    res = run_bass_kernel_spmd(nc, in_maps, core_ids=list(range(8)))
    out = np.empty((B, S, E), np.float32)
    for core in range(8):
        b, cc = core // 4, core % 4
        o = np.asarray(res.results[core]["out"])
        out[b, 512 * cc:512 * cc + 512] = o[0:512]
        out[b, 512 * (7 - cc):512 * (7 - cc) + 512] = o[512:1024]
    if dbg:
        kernel._dbg = res.results
    return out


# version 09
# speedup vs baseline: 1.0520x; 1.0345x over previous
import os
import math
import contextlib
import numpy as np
import ml_dtypes
import concourse.bass as bass
import concourse.mybir as mybir
from concourse.bass_utils import run_bass_kernel_spmd

F32 = mybir.dt.float32
BF16 = mybir.dt.bfloat16
I32 = mybir.dt.int32
AF = mybir.ActivationFunctionType
ALU = mybir.AluOpType
AX = mybir.AxisListType
BFD = ml_dtypes.bfloat16

B, S, E, H, D = 2, 4096, 4096, 32, 128
NK = E // 128
NQ = NK + 2
TPC = 1024
NT = TPC // 128
HPC = H // 4
SCALING = float(D) ** -0.5
EPS_LN = 1e-5
RANKOF = [a if a < 4 else 7 - a for a in range(8)]
LOCOF = [0 if a < 4 else 1 for a in range(8)]
UNITS = [(a, kc, kc >= 4 * a) for a in range(8) for kc in range(4 * a + 4)]
U = len(UNITS)
UPFX = [0]
for a in range(8):
    UPFX.append(UPFX[-1] + 4 * a + 4)
NDIAG = np.cumsum([int(d) for (_, _, d) in UNITS]).tolist()


def _slopes():
    base = 2.0 ** (-(2.0 ** (-(math.log2(H) - 3))))
    return (base ** np.arange(1, H + 1, dtype=np.float64)).astype(np.float32)


class SemC:
    def __init__(self, handle):
        self.h = handle
        self.n = 0


class Sched:
    def __init__(self):
        self.ops = []

    def op(self, eng, emit, waits=(), inc=None, amt=1):
        incs = []
        if inc is not None and amt > 0:
            inc.n += amt
            incs = [(inc, amt)]
        self.ops.append((eng, [(s, v) for (s, v) in waits], emit, incs))

    def emit_into(self, nc):
        keys = sorted({o[0] for o in self.ops})
        with nc.Block() as blk:
            for key in keys:
                def body(engv, key=key):
                    for (ek, waits, emit, incs) in self.ops:
                        if ek != key:
                            continue
                        for (s, v) in waits:
                            engv.wait_ge(s.h, v)
                        inst = emit(engv)
                        for (s, amt) in incs:
                            inst.then_inc(s.h, amt)
                getattr(blk, {"t": "tensor", "v": "vector", "s": "scalar",
                              "g": "gpsimd", "y": "sync"}[key])(body)


def build_nc(dbg=False):
    import os as _os
    _PH = int(_os.environ.get("MKPHASES", "99"))
    nc = bass.Bass("TRN2", target_bir_lowering=False)
    ap = lambda t: t.ap()

    xs = ap(nc.dram_tensor("xs", [TPC, E], F32, kind="ExternalInput"))
    wq_t = ap(nc.dram_tensor("wq_t", [NQ, 128, NK * 128], F32, kind="ExternalInput"))
    wc_t = ap(nc.dram_tensor("wc_t", [NK, 128, NK * 128], F32, kind="ExternalInput"))
    wb_t = ap(nc.dram_tensor("wb_t", [NK, 128, NK * 128], F32, kind="ExternalInput"))
    acat_t = ap(nc.dram_tensor("acat_t", [128, NK * 128], F32, kind="ExternalInput"))
    bcat_t = ap(nc.dram_tensor("bcat_t", [128, NK * 128], F32, kind="ExternalInput"))
    rw_t = ap(nc.dram_tensor("rw_t", [128, NK * 128], F32, kind="ExternalInput"))
    bbT_i = ap(nc.dram_tensor("bbT_i", [128, 32], F32, kind="ExternalInput"))
    rb_row = ap(nc.dram_tensor("rb_row", [1, 16], F32, kind="ExternalInput"))
    prior_row = ap(nc.dram_tensor("prior_row", [1, 16], F32, kind="ExternalInput"))
    active_col = ap(nc.dram_tensor("active_col", [4, 1], I32, kind="ExternalInput"))
    biastab_i = ap(nc.dram_tensor("biastab_i", [128, HPC * 32], F32, kind="ExternalInput"))
    negslope_i = ap(nc.dram_tensor("negslope_i", [1, HPC], F32, kind="ExternalInput"))
    posrow_i = ap(nc.dram_tensor("posrow_i", [1, S], BF16, kind="ExternalInput"))
    maskadd_i = ap(nc.dram_tensor("maskadd_i", [128, 4 * 512], F32, kind="ExternalInput"))
    iota16_i = ap(nc.dram_tensor("iota16_i", [4, 16], F32, kind="ExternalInput"))
    e16sc_i = ap(nc.dram_tensor("e16sc_i", [16, 128], F32, kind="ExternalInput"))
    identf_i = ap(nc.dram_tensor("identf_i", [128, 128], F32, kind="ExternalInput"))
    coff_i = ap(nc.dram_tensor("coff_i", [1, 4], I32, kind="ExternalInput"))

    out_t = ap(nc.dram_tensor("out", [TPC, E], F32, kind="ExternalOutput"))

    x1_s = ap(nc.dram_tensor("x1_s", [TPC, E], BF16))
    qT_s = nc.dram_tensor("qT_s", [NK * 128, TPC], BF16)
    kv_s = nc.dram_tensor("kv_s", [256, TPC], BF16)
    gat1 = nc.dram_tensor("gat1", [4 * NK * 128, TPC], BF16)
    gkv = nc.dram_tensor("gkv", [4 * 256, TPC], BF16)
    attc_s = nc.dram_tensor("attc_s", [S, HPC * 128], BF16)
    gat2 = nc.dram_tensor("gat2", [4 * S, HPC * 128], BF16)
    qex = nc.dram_tensor("qex", [4096, TPC], BF16)
    attex = nc.dram_tensor("attex", [4096, TPC], BF16)
    proj_s = ap(nc.dram_tensor("proj_s", [E, TPC], BF16))
    xmid_s = ap(nc.dram_tensor("xmid_s", [TPC, E], F32))
    x2_s = ap(nc.dram_tensor("x2_s", [TPC, E], BF16))
    ffn_s = ap(nc.dram_tensor("ffn_s", [E, TPC], BF16))

    dbg_pairs = []
    if dbg:
        for nm, src in [("d_x1", x1_s), ("d_qkvT", qT_s.ap()),
                        ("d_attc", attc_s.ap()), ("d_proj", proj_s),
                        ("d_xmid", xmid_s), ("d_x2", x2_s), ("d_ffn", ffn_s)]:
            o = ap(nc.dram_tensor(nm, list(src.tensor.shape), src.tensor.dtype,
                                  kind="ExternalOutput"))
            dbg_pairs.append((o, src))

    st = contextlib.ExitStack()
    with st:
        e = st.enter_context

        featT = e(nc.sbuf_tensor("featT", [128, NK * 1024], BF16))
        acat_sb = e(nc.sbuf_tensor("acat_sb", [128, NK * 128], BF16))
        bcat_sb = e(nc.sbuf_tensor("bcat_sb", [128, NK * 128], BF16))
        rwsb = e(nc.sbuf_tensor("rwsb", [128, NK * 128], BF16))
        maskadd = e(nc.sbuf_tensor("maskadd", [128, 4 * 512], F32))
        biastab = e(nc.sbuf_tensor("biastab", [128, HPC * 32], F32))
        posrow = e(nc.sbuf_tensor("posrow", [1, S], BF16))
        negslope = e(nc.sbuf_tensor("negslope", [1, HPC], F32))
        bbT = e(nc.sbuf_tensor("bbT", [128, 32], F32))
        identf = e(nc.sbuf_tensor("identf", [128, 128], F32))
        e16sc = e(nc.sbuf_tensor("e16sc", [16, 128], F32))
        iota16 = e(nc.sbuf_tensor("iota16", [4, 16], F32))
        activef = e(nc.sbuf_tensor("activef", [4, 1], F32))
        activei = e(nc.sbuf_tensor("activei", [4, 1], I32))
        onehotT = e(nc.sbuf_tensor("onehotT", [4, 16], F32))
        rblnp = e(nc.sbuf_tensor("rblnp", [1, 16], F32))
        r16a = e(nc.sbuf_tensor("r16a", [1, 16], F32))
        r16b = e(nc.sbuf_tensor("r16b", [1, 16], F32))
        ones_c128b = e(nc.sbuf_tensor("ones_c128b", [128, 1], BF16))
        ones_r128b = e(nc.sbuf_tensor("ones_r128b", [1, 128], BF16))
        ones_r128f = e(nc.sbuf_tensor("ones_r128f", [1, 128], F32))
        ones_c4f = e(nc.sbuf_tensor("ones_c4f", [4, 1], F32))
        negb = [e(nc.sbuf_tensor(f"negb{i}", [1, S], BF16)) for i in range(2)]
        qnk = e(nc.sbuf_tensor("qnk", [1, S], F32))
        kneg2 = e(nc.sbuf_tensor("kneg2", [1, 1], F32))
        kk = e(nc.sbuf_tensor("kk", [1, 1], F32))
        ktmp = e(nc.sbuf_tensor("ktmp", [1, 16], F32))
        statc = e(nc.sbuf_tensor("statc", [128, 16], F32))
        epsln_c = e(nc.sbuf_tensor("epsln_c", [128, 1], F32))
        eps12 = e(nc.sbuf_tensor("eps12", [1, 1], F32))
        eps30 = e(nc.sbuf_tensor("eps30", [1, 1], F32))
        coffs = e(nc.sbuf_tensor("coffs", [1, 4], I32))

        _sems = {}
        def S_(name):
            if name not in _sems:
                _sems[name] = SemC(e(nc.semaphore(name)))
            return _sems[name]

        sch = Sched()
        c0 = S_("c0")
        for dst, src in [(acat_sb, acat_t), (bcat_sb, bcat_t), (rwsb, rw_t),
                         (maskadd, maskadd_i), (biastab, biastab_i),
                         (posrow, posrow_i), (negslope, negslope_i), (bbT, bbT_i),
                         (identf, identf_i), (e16sc, e16sc_i), (iota16, iota16_i),
                         (activei, active_col), (r16a, rb_row), (r16b, prior_row),
                         (coffs, coff_i)]:
            sch.op("g", (lambda d, s2: (lambda g: g.dma_start(d[:, :], s2)))(dst, src),
                   inc=c0, amt=16)
        for buf, val in [(ones_c128b, 1.0), (ones_r128b, 1.0),
                         (ones_r128f, 1.0), (ones_c4f, 1.0),
                         (epsln_c, EPS_LN), (eps12, 1e-12), (eps30, 1e-30)]:
            sch.op("g", (lambda b2, v2: (lambda g: g.memset(b2[:, :], v2)))(buf, val),
                   inc=c0, amt=0)
        sch.emit_into(nc)

        b0b_st = contextlib.ExitStack()
        ps_cnt = b0b_st.enter_context(nc.psum_tensor("ps_cnt", [1, 16], F32))
        sch = Sched()
        mi = S_("mi")
        sch.op("v", lambda v: v.tensor_copy(activef[:, :], activei[:, :]),
               waits=[(c0, c0.n)], inc=mi)
        sch.op("v", lambda v: v.tensor_scalar(onehotT[:, :], iota16[:, :],
                                              activef[:, :], None, ALU.is_equal),
               waits=[(mi, 1)], inc=mi)
        sch.op("t", lambda t: t.matmul(ps_cnt[:, :], ones_c4f[:, :], onehotT[:, :],
                                       start=True, stop=True),
               waits=[(mi, 2)], inc=mi)
        sch.op("s", lambda sc: sc.activation(r16b[:, :], r16b[:, :], AF.Ln, bias=eps12[0:1, 0:1]),
               waits=[(c0, c0.n), (mi, 3)], inc=mi)
        sch.op("s", lambda sc: sc.activation(rblnp[:, :], ps_cnt[:, :], AF.Ln, bias=eps30[0:1, 0:1]),
               waits=[(mi, 4)], inc=mi)
        sch.op("v", lambda v: v.tensor_add(rblnp[:, :], rblnp[:, :], r16a[:, :]),
               waits=[(mi, 5)], inc=mi)
        sch.op("v", lambda v: v.tensor_add(rblnp[:, :], rblnp[:, :], r16b[:, :]),
               waits=[(mi, 6)], inc=mi)
        sch.emit_into(nc)
        b0b_st.close()
        if _PH <= 1:
            nc.finalize(); return nc

        with contextlib.ExitStack() as r1:
            e1 = r1.enter_context
            xt = [e1(nc.sbuf_tensor(f"xt{i}", [128, E], F32)) for i in range(2)]
            x1o = [e1(nc.sbuf_tensor(f"x1o{i}", [128, E], BF16)) for i in range(2)]
            jnk = e1(nc.sbuf_tensor("jnk", [128, E], BF16))

            sch = Sched()
            XL = [S_("XL0"), S_("XL1")]
            CH = S_("CH")
            SQ = S_("SQ")
            XW = [S_("XW0"), S_("XW1")]
            for tI in range(NT):
                b = tI % 2
                w = [(CH, (tI - 2) * 6 + 6)] if tI >= 2 else []
                sch.op("g", (lambda tI=tI, b=b: (lambda g: g.dma_start(
                    xt[b][:, :], xs[tI * 128:(tI + 1) * 128, :])))(),
                    waits=w, inc=XL[b], amt=16)
            for tI in range(NT):
                b = tI % 2
                ldw = [(XL[b], (tI // 2 + 1) * 16)]
                sch.op("v", (lambda b=b: (lambda v: v.reduce_sum(
                    statc[:, 0 + b:1 + b], xt[b][:, :], axis=AX.X)))(),
                    waits=ldw + ([(CH, (tI - 2) * 6 + 6)] if tI >= 2 else []),
                    inc=CH)
                sqw = list(ldw)
                if tI >= 1:
                    sqw.append((SQ, 2 * (tI - 1) + 1))
                if tI >= 2:
                    sqw.append((CH, (tI - 2) * 6 + 6))
                sch.op("s", (lambda b=b: (lambda sc: sc.activation(
                    jnk[:, :], xt[b][:, :], AF.Square, accum_out=statc[:, 2 + b:3 + b])))(),
                    waits=sqw, inc=SQ)
                sch.op("v", (lambda b=b: (lambda v: v.tensor_scalar(
                    statc[:, 4 + b:5 + b], statc[:, 0 + b:1 + b], 1.0 / E, None, ALU.mult)))(),
                    waits=[(CH, tI * 6 + 1)], inc=CH)
                sch.op("v", (lambda b=b: (lambda v: v.tensor_mul(
                    statc[:, 6 + b:7 + b], statc[:, 4 + b:5 + b], statc[:, 4 + b:5 + b])))(),
                    waits=[(CH, tI * 6 + 2)], inc=CH)
                sch.op("v", (lambda b=b: (lambda v: v.scalar_tensor_tensor(
                    statc[:, 8 + b:9 + b], statc[:, 2 + b:3 + b], 1.0 / E,
                    statc[:, 6 + b:7 + b], ALU.mult, ALU.subtract)))(),
                    waits=[(CH, tI * 6 + 3), (SQ, 2 * tI + 1)], inc=CH)
                sqw2 = [(CH, tI * 6 + 4)]
                if tI >= 2:
                    sqw2.append((CH, (tI - 2) * 6 + 5))
                sch.op("s", (lambda b=b: (lambda sc: sc.activation(
                    statc[:, 10 + b:11 + b], statc[:, 8 + b:9 + b], AF.Sqrt, bias=epsln_c[:, :])))(),
                    waits=sqw2, inc=SQ)
                sch.op("v", (lambda b=b: (lambda v: v.reciprocal(
                    statc[:, 12 + b:13 + b], statc[:, 10 + b:11 + b])))(),
                    waits=[(SQ, 2 * tI + 2)], inc=CH)
                aw = [(CH, tI * 6 + 5)]
                if tI >= 2:
                    aw.append((XW[b], (tI // 2) * 16))
                sch.op("v", (lambda b=b: (lambda v: v.tensor_scalar(
                    x1o[b][:, :], xt[b][:, :], statc[:, 4 + b:5 + b],
                    statc[:, 12 + b:13 + b], ALU.subtract, ALU.mult)))(),
                    waits=aw, inc=CH)
                sch.op("y", (lambda tI=tI, b=b: (lambda y: y.dma_start(
                    x1_s[tI * 128:(tI + 1) * 128, :], x1o[b][:, :])))(),
                    waits=[(CH, tI * 6 + 6)], inc=XW[b], amt=16)
            sch.emit_into(nc)
        if _PH <= 2:
            nc.finalize(); return nc

        sch = Sched()
        FT = S_("FT")
        XW = [S_("XW0"), S_("XW1")]
        for k in range(NK):
            w = [(XW[0], XW[0].n), (XW[1], XW[1].n)] if k == 0 else []
            sch.op("y", (lambda k=k: (lambda y: y.dma_start(
                featT[:, k * 1024:(k + 1) * 1024], x1_s[:, k * 128:(k + 1) * 128],
                transpose=True)))(), waits=w, inc=FT, amt=16)
        sch.emit_into(nc)
        if _PH <= 3:
            nc.finalize(); return nc

        def mm_phase2(nm_id, w_tiles, n_m, dst, first_waits, scale_fn=None,
                      bias_col=None, extra_mm=None, extra_mm_waits=(),
                      psq=None, wslab=None, oev=None, load_first_waits=(),
                      m_order=None, dst_fn=None, post_m=None):
            sch = Sched()
            WL = [S_(f"{nm_id}WL0"), S_(f"{nm_id}WL1")]
            MM = S_(f"{nm_id}MM")
            EV = S_(f"{nm_id}EV")
            QW = [S_(f"{nm_id}QW0"), S_(f"{nm_id}QW1")]
            morder = list(m_order) if m_order is not None else list(range(n_m))
            nload = [0, 0]
            for mi_, mv_ in enumerate(morder):
                m, b = mi_, mi_ % 2
                w = [(MM, m - 1)] if m >= 2 else []
                if m == 0:
                    w += list(load_first_waits)
                nload[b] += 1
                sch.op("g", (lambda mv_=mv_, b=b: (lambda g: g.dma_start(
                    wslab[b][:, :], w_tiles[mv_, :, :])))(),
                    waits=w, inc=WL[b], amt=16)
                if post_m is not None:
                    post_m(sch, "load", mi_, QW)
            nl = [0, 0]
            for mi_, mv_ in enumerate(morder):
                m, b = mi_, mi_ % 2
                nl[b] += 1
                def emit_mm(t, m=m, b=b):
                    mm = None
                    for g2 in range(2):
                        p = psq[2 * b + g2]
                        for k in range(NK):
                            mm = t.matmul(p[:, :], wslab[b][:, k * 128:(k + 1) * 128],
                                          featT[:, k * 1024 + g2 * 512:
                                                k * 1024 + g2 * 512 + 512],
                                          start=(k == 0),
                                          stop=(k == NK - 1 and extra_mm is None))
                        if extra_mm is not None:
                            mm = extra_mm(t, p, m, g2)
                    return mm
                w = [(WL[b], nl[b] * 16)]
                if m == 0:
                    w += list(first_waits) + list(extra_mm_waits)
                if m >= 2:
                    w.append((EV, (m - 2) * 2 + 2))
                sch.op("t", emit_mm, waits=w, inc=MM)
            for mi_, mv_ in enumerate(morder):
                m, b = mi_, mi_ % 2
                if bias_col is None:
                    def emit_ev(sc, mv_=mv_, b=b, g2=0):
                        scl = scale_fn(mv_) if scale_fn else 1.0
                        return sc.activation(oev[b][:, g2 * 512:(g2 + 1) * 512],
                                             psq[2 * b + g2][:, :], AF.Copy,
                                             scale=scl)
                    eng_ev = "s"
                else:
                    def emit_ev(v, mv_=mv_, b=b, g2=0):
                        return v.tensor_scalar(oev[b][:, g2 * 512:(g2 + 1) * 512],
                                               psq[2 * b + g2][:, :], bias_col(mv_),
                                               None, ALU.add)
                    eng_ev = "v"
                w = [(MM, m + 1)]
                if m >= 2:
                    w.append((QW[b], (m - 2) // 2 * 16 + 16))
                import functools
                sch.op(eng_ev, functools.partial(emit_ev, g2=0), waits=w, inc=EV)
                sch.op(eng_ev, functools.partial(emit_ev, g2=1), waits=[], inc=EV)
                if dst_fn is None:
                    emit_st = (lambda mv_=mv_, b=b: (lambda y: y.dma_start(
                        dst[mv_ * 128:(mv_ + 1) * 128, :], oev[b][:, :])))()
                else:
                    emit_st = (lambda mv_=mv_, b=b: (lambda y: y.dma_start(
                        dst_fn(mv_), oev[b][:, :])))()
                sch.op("y", emit_st, waits=[(EV, (m + 1) * 2)], inc=QW[b], amt=16)
            return sch, QW

        with contextlib.ExitStack() as r2:
            e2 = r2.enter_context
            wslab = [e2(nc.sbuf_tensor(f"wslab{i}", [128, NK * 128], BF16)) for i in range(2)]
            oev = [e2(nc.sbuf_tensor(f"oev{i}", [128, TPC], BF16)) for i in range(2)]
            psq = [e2(nc.psum_tensor(f"psq{i}", [128, 512], F32)) for i in range(4)]
            XW = [S_("XW0"), S_("XW1")]
            AG1 = S_("AG1")
            AGK = S_("AGK")

            def qdst(mv):
                if mv < NK:
                    return qT_s.ap()[mv * 128:(mv + 1) * 128, :]
                return kv_s.ap()[(mv - NK) * 128:(mv - NK + 1) * 128, :]

            def post_m(sch_, kind, mi_, QWl):
                if kind == "load" and mi_ == 4:
                    sch_.op("g", lambda g: g.collective_compute(
                        "AllGather", ALU.bypass,
                        replica_groups=[[0, 1, 2, 3], [4, 5, 6, 7]],
                        ins=[kv_s.ap().opt()], outs=[gkv.ap().opt()]),
                        waits=[(QWl[0], 16), (QWl[1], 16)], inc=AGK)

            sch, QW = mm_phase2("P2", wq_t, NQ, None,
                                first_waits=[(FT, NK * 16)],
                                scale_fn=lambda m: SCALING if m < NK else 1.0,
                                psq=psq, wslab=wslab, oev=oev,
                                load_first_waits=[(XW[0], XW[0].n), (XW[1], XW[1].n)],
                                m_order=[NK, NK + 1] + list(range(NK)),
                                dst_fn=qdst, post_m=post_m)
            sch.op("g", lambda g: g.collective_compute(
                "AllGather", ALU.bypass,
                replica_groups=[[0, 1, 2, 3], [4, 5, 6, 7]],
                ins=[qT_s.ap().opt()], outs=[gat1.ap().opt()]),
                waits=[(QW[0], QW[0].n), (QW[1], QW[1].n)], inc=AG1)
            sch.emit_into(nc)
        if _PH <= 4:
            nc.finalize(); return nc

        with contextlib.ExitStack() as r3:
            e3 = r3.enter_context
            KT = e3(nc.sbuf_tensor("KT", [128, S], BF16))
            Vsb = e3(nc.sbuf_tensor("Vsb", [128, 32 * 160], BF16))
            sqb = e3(nc.sbuf_tensor("sqb", [128, E], BF16))
            qbh = [e3(nc.sbuf_tensor(f"qbh{i}", [128, S], BF16)) for i in range(2)]
            expT = [e3(nc.sbuf_tensor(f"expT{i}", [128, 512], BF16)) for i in range(3)]
            apc = [e3(nc.sbuf_tensor(f"apc{i}", [128, 512], BF16)) for i in range(2)]
            negbT = [e3(nc.sbuf_tensor(f"negbT{i}", [128, 512], BF16)) for i in range(2)]
            recipc = e3(nc.sbuf_tensor("recipc", [128, 8], F32))
            ps_o = [e3(nc.psum_tensor(f"ps_o{i}", [128, 129], F32)) for i in range(4)]
            ps_s = [e3(nc.psum_tensor(f"ps_s{i}", [128, 512], F32)) for i in range(3)]
            ps_q = [e3(nc.psum_tensor("ps_q0", [1, 512], F32))] * 2
            gA = gat1.ap()
            gK = gkv.ap()

            sch = Sched()
            VM = S_("VM")
            KTs = S_("KTs")
            VV = S_("VV")
            KX = S_("KX")
            sch.op("g", lambda g: g.memset(Vsb[:, :], 1.0), waits=[(AGK, 1)], inc=VM)
            for kc in range(32):
                a = kc // 4
                roff = RANKOF[a] * 256
                coff = LOCOF[a] * 512 + (kc % 4) * 128
                sch.op("g", (lambda kc=kc, roff=roff, coff=coff: (lambda g: g.dma_start(
                    KT[:, kc * 128:(kc + 1) * 128],
                    gK[roff:roff + 128, coff:coff + 128])))(),
                    waits=[(AGK, 1)] if kc == 0 else [], inc=KTs, amt=16)
            for kc in range(32):
                a = kc // 4
                roff = RANKOF[a] * 256 + 128
                coff = LOCOF[a] * 512 + (kc % 4) * 128
                sch.op("y", (lambda kc=kc, roff=roff, coff=coff: (lambda y: y.dma_start(
                    Vsb[:, kc * 160:kc * 160 + 128],
                    gK[roff:roff + 128, coff:coff + 128], transpose=True)))(),
                    waits=[(AGK, 1), (VM, 1)] if kc == 0 else [], inc=VV, amt=16)
            sch.op("s", lambda sc: sc.activation(sqb[:, :], KT[:, :], AF.Square),
                   waits=[(KTs, 32 * 16)], inc=KX)
            KQ = S_("KQ")
            for j in range(8):
                sch.op("t", (lambda j=j: (lambda t: t.matmul(
                    ps_q[j % 2][:, :], ones_c128b[:, :], sqb[:, j * 512:(j + 1) * 512],
                    start=True, stop=True)))(),
                    waits=[(KX, 1)] + ([(KQ, j)] if j >= 1 else []), inc=KX)
                sch.op("v", (lambda j=j: (lambda v: v.reduce_max(
                    ktmp[:, j:j + 1], ps_q[j % 2][:, :], axis=AX.X)))(),
                    waits=[(KX, j + 2)], inc=KQ)
            sch.op("v", lambda v: v.reduce_max(kk[:, :], ktmp[:, 0:8], axis=AX.X),
                   waits=[(KQ, 8)], inc=KQ)
            sch.op("v", lambda v: v.tensor_scalar(kneg2[:, :], kk[:, :], 1.024144,
                                                  None, ALU.mult),
                   waits=[(KQ, 9)], inc=KQ)
            sch.emit_into(nc)
            if _PH <= 5:
                nc.finalize(); return nc

            sch = Sched()
            QB = [S_("QB0"), S_("QB1")]
            SQA = S_("SQA")
            SSQ = S_("SSQ")
            QE = S_("QE")
            NB = S_("NB")
            SC = S_("SC")
            NBI = S_("NBI")
            NBT = S_("NBT")
            VA = S_("VA")
            EX = S_("EX")
            MK = S_("MK")
            PV = S_("PV")
            RC = S_("RC")
            FIN = S_("FIN")
            ACW = [S_("ACW0"), S_("ACW1")]
            AG2 = S_("AG2")

            QX = S_("QX")
            def emit_qx(y):
                with y.register("qxa") as r1, y.register("qxb") as r2, \
                        y.register("qxc") as r0:
                    y.reg_load(r0, coffs[0:1, 0:1])
                    y.reg_mul(r1, r0, TPC)
                    for r in range(4):
                        y.reg_add(r2, r1, r * (NK * 128) * TPC)
                        y.dma_start(qex.ap()[r * 1024:(r + 1) * 1024, :],
                                    bass.AP(gat1, r2, [[TPC, 1024], [1, TPC]])) \
                            .then_inc(QX.h, 16)
                return None
            sch.ops.append(("y", [(AG1, AG1.n)], emit_qx, []))
            QX.n += 4 * 16
            qexa = qex.ap()

            def emit_qload_f(hh):
                b = hh % 2
                def emit_qload(y, hh=hh, b=b):
                    for a in range(8):
                        roff = RANKOF[a] * 1024 + hh * 128
                        coff = LOCOF[a] * 512
                        y.dma_start(qbh[b][:, a * 512:(a + 1) * 512],
                                    qexa[roff:roff + 128, coff:coff + 512]) \
                            .then_inc(QB[b].h, 16)
                    return None
                w = [(QX, 4 * 16)] if hh == 0 else []
                if hh >= 2:
                    w.append((PV, (hh - 1) * U))
                sch.ops.append(("y", list(w), emit_qload, []))
                QB[b].n += 8 * 16

            _NHH = int(_os.environ.get("MKNHH", str(HPC)))
            emit_qload_f(0)
            if _NHH > 1:
                emit_qload_f(1)
            for hh in range(_NHH):
                b = hh % 2
                w = [(QB[b], (hh // 2 + 1) * 8 * 16)]
                if hh >= 1:
                    w.append((SSQ, 8 * hh))
                else:
                    w.append((KQ, 10))
                sch.op("s", (lambda b=b: (lambda sc: sc.activation(
                    sqb[:, :], qbh[b][:, :], AF.Square)))(), waits=w, inc=SQA)
                for a in range(8):
                    w = [(SQA, hh + 1)]
                    if hh * 8 + a >= 1:
                        w.append((QE, hh * 8 + a))
                    if hh == 0 and a < 1:
                        w.append((KQ, 10))
                    sch.op("t", (lambda a=a: (lambda t: t.matmul(
                        ps_q[a % 2][:, :], ones_c128b[:, :],
                        sqb[:, a * 512:(a + 1) * 512], start=True, stop=True)))(),
                        waits=w, inc=SSQ)
                    w2 = [(SSQ, hh * 8 + a + 1), (KQ, 10)]
                    if hh >= 1 and a == 0:
                        w2.append((NB, hh))
                    sch.op("s", (lambda a=a: (lambda sc: sc.activation(
                        qnk[0:1, a * 512:(a + 1) * 512], ps_q[a % 2][:, :],
                        AF.Sqrt, scale=kneg2[0:1, 0:1])))(),
                        waits=w2, inc=QE)
                w = [(QE, (hh + 1) * 8)]
                if hh >= 2:
                    w.append((SC, (hh - 1) * U))
                sch.op("v", (lambda hh=hh, b=b: (lambda v: v.scalar_tensor_tensor(
                    negb[b][:, :], posrow[:, :], negslope[0:1, hh:hh + 1], qnk[:, :],
                    ALU.mult, ALU.subtract)))(), waits=w, inc=NB)

                def emit_pv(a, kc, gu, pi):
                    w = [(EX, gu + 1)]
                    if kc == 0 and pi >= 1:
                        w.append((FIN, pi * 4))
                    sch.op("t", (lambda a=a, kc=kc, bu=gu % 3: (lambda t: [
                        t.matmul(ps_o[qt][:, :], expT[bu][:, qt * 128:(qt + 1) * 128],
                                 Vsb[:, kc * 160:kc * 160 + 129],
                                 start=(kc == 0), stop=(kc == 4 * a + 3))
                        for qt in range(4)][-1]))(),
                        waits=w, inc=PV)

                pending = []
                for a in range(8):
                    pi = hh * 8 + a
                    bq = pi % 2
                    for kc in range(4 * a + 4):
                        u = UPFX[a] + kc
                        gu = hh * U + u
                        diag = kc >= 4 * a
                        w = [(NB, hh + 1)] if u == 0 else []
                        if gu >= 3:
                            w.append((EX, gu - 2))
                        if gu == 0:
                            w += [(KTs, 32 * 16), (VV, 32 * 16)]
                        sch.op("t", (lambda a=a, kc=kc, b=b, bu=gu % 3: (lambda t: (
                            t.matmul(ps_s[bu][:, :], ones_r128b[:, :],
                                     negb[b][0:1, a * 512:(a + 1) * 512],
                                     start=True, stop=False),
                            t.matmul(ps_s[bu][:, :], KT[:, kc * 128:(kc + 1) * 128],
                                     qbh[b][:, a * 512:(a + 1) * 512],
                                     start=False, stop=True))[-1]))(),
                            waits=w, inc=SC)
                        pending.append((a, kc, gu, pi))
                        if len(pending) > 2:
                            emit_pv(*pending.pop(0))
                        if diag:
                            j = kc - 4 * a
                            sch.op("v", (lambda j=j, bu=gu % 3: (lambda v: v.tensor_add(
                                ps_s[bu][:, :], ps_s[bu][:, :],
                                maskadd[:, j * 512:(j + 1) * 512])))(),
                                waits=[(SC, gu + 1)], inc=MK)
                        w = [(SC, gu + 1)]
                        if diag:
                            w.append((MK, MK.n))
                        if gu >= 3:
                            w.append((PV, gu - 2))
                        sch.op("s", (lambda hh=hh, kc=kc, bu=gu % 3: (lambda sc: sc.activation(
                            expT[bu][:, :], ps_s[bu][:, :], AF.Exp,
                            bias=biastab[:, hh * 32 + kc:hh * 32 + kc + 1])))(),
                            waits=w, inc=EX)
                        if kc == 4 * a + 3:
                            while pending:
                                emit_pv(*pending.pop(0))
                    for qt in range(4):
                        w = [(PV, hh * U + UPFX[a + 1])]
                        if pi >= 2:
                            w.append((FIN, (pi - 1) * 4))
                        sch.op("v", (lambda qt=qt, bq=bq: (lambda v: v.reciprocal(
                            recipc[:, bq * 4 + qt:bq * 4 + qt + 1],
                            ps_o[qt][:, 128:129])))(), waits=w, inc=RC)
                    for qt in range(4):
                        w = [(RC, pi * 4 + qt + 1)]
                        if pi >= 2:
                            w.append((ACW[bq], (pi // 2) * 16))
                        sch.op("v", (lambda qt=qt, bq=bq: (lambda v: v.tensor_scalar(
                            apc[bq][:, qt * 128:(qt + 1) * 128],
                            ps_o[qt][:, 0:128],
                            recipc[:, bq * 4 + qt:bq * 4 + qt + 1], None, ALU.mult)))(),
                            waits=w, inc=FIN)
                    sch.op("y", (lambda hh=hh, a=a, bq=bq: (lambda y: y.dma_start(
                        attc_s.ap().rearrange("(q p) c -> p q c", p=128)
                        [:, 4 * a:4 * a + 4, hh * 128:(hh + 1) * 128],
                        apc[bq][:, :].rearrange("p (q c) -> p q c", q=4))))(),
                        waits=[(FIN, (pi + 1) * 4)], inc=ACW[bq], amt=16)
                if hh + 2 < _NHH:
                    emit_qload_f(hh + 2)
            sch.op("g", lambda g: g.collective_compute(
                "AllGather", ALU.bypass,
                replica_groups=[[0, 1, 2, 3], [4, 5, 6, 7]],
                ins=[attc_s.ap().opt()], outs=[gat2.ap().opt()]),
                waits=[(ACW[0], ACW[0].n), (ACW[1], ACW[1].n)], inc=AG2)
            sch.emit_into(nc)
            if _PH <= 6:
                nc.finalize(); return nc

        with contextlib.ExitStack() as r4:
            e4 = r4.enter_context
            wslab = [e4(nc.sbuf_tensor(f"w2slab{i}", [128, NK * 128], BF16)) for i in range(2)]
            oev = [e4(nc.sbuf_tensor(f"o2ev{i}", [128, TPC], BF16)) for i in range(2)]
            psq = [e4(nc.psum_tensor(f"ps2q{i}", [128, 512], F32)) for i in range(4)]
            gB = gat2.ap()
            AG2 = S_("AG2")
            FT2 = S_("FT2")

            sch = Sched()
            AX2 = S_("AX2")
            def emit_atx(g):
                with g.register("axa") as r1, g.register("axb") as r2, \
                        g.register("axc") as r0:
                    for half, cidx in ((0, 1), (1, 2)):
                        g.reg_load(r0, coffs[0:1, cidx:cidx + 1])
                        g.reg_mul(r1, r0, HPC * 128)
                        for J in range(4):
                            g.reg_add(r2, r1, J * S * (HPC * 128))
                            g.dma_start(
                                attex.ap()[J * 1024 + half * 512:
                                           J * 1024 + half * 512 + 512, :],
                                bass.AP(gat2, r2, [[HPC * 128, 512], [1, HPC * 128]])) \
                                .then_inc(AX2.h, 16)
                return None
            sch.ops.append(("g", [(AG2, 1)], emit_atx, []))
            AX2.n += 8 * 16
            atxa = attex.ap()
            def emit_atl(y):
                ii = None
                for k in range(NK):
                    J, HHl = k // HPC, k % HPC
                    ii = y.dma_start(
                        featT[:, k * 1024:(k + 1) * 1024],
                        atxa[J * 1024:(J + 1) * 1024, HHl * 128:(HHl + 1) * 128],
                        transpose=True)
                    ii.then_inc(FT2.h, 16)
                return None
            sch.ops.append(("y", [(AX2, 8 * 16)], emit_atl, []))
            FT2.n += NK * 16
            ACW = [S_("ACW0"), S_("ACW1")]
            sch2, QW2 = mm_phase2("P4", wc_t, NK, proj_s,
                                  first_waits=[(FT2, FT2.n)],
                                  psq=psq, wslab=wslab, oev=oev,
                                  load_first_waits=[(ACW[0], ACW[0].n),
                                                    (ACW[1], ACW[1].n)])
            sch.ops.extend(sch2.ops)
            sch.emit_into(nc)
            PJ = QW2
        if _PH <= 7:
            nc.finalize(); return nc

        with contextlib.ExitStack() as r5:
            e5 = r5.enter_context
            xt1 = e5(nc.sbuf_tensor("xt1b7", [128, E], F32))
            ptok = [e5(nc.sbuf_tensor(f"ptok{i}", [128, E], BF16)) for i in range(2)]
            xmid_sb = e5(nc.sbuf_tensor("xmid_sb", [128, E], F32))
            x2t = e5(nc.sbuf_tensor("x2t", [128, E], BF16))

            sch = Sched()
            XL7 = S_("XL7")
            PT7 = [S_("PT70"), S_("PT71")]
            C7 = S_("C7")
            SQ7 = S_("SQ7")
            XMW = S_("XMW")
            X2W = S_("X2W")
            NS = 8

            def b7_load(tI):
                b = tI % 2
                w = ([(C7, (tI - 1) * NS + 1)] if tI >= 1 else
                     [(PJ[0], PJ[0].n), (PJ[1], PJ[1].n)])
                sch.op("g", (lambda tI=tI: (lambda g: g.dma_start(
                    xt1[:, :], xs[tI * 128:(tI + 1) * 128, :])))(),
                    waits=w, inc=XL7, amt=16)
                w = [(PJ[0], PJ[0].n), (PJ[1], PJ[1].n)] if tI == 0 else []
                if tI >= 2:
                    w.append((C7, (tI - 1) * NS + 1))
                sch.op("y", (lambda tI=tI, b=b: (lambda y: y.dma_start(
                    ptok[b][:, :], proj_s[:, tI * 128:(tI + 1) * 128],
                    transpose=True)))(), waits=w, inc=PT7[b], amt=16)

            b7_load(0)
            b7_load(1)
            for tI in range(NT):
                b = tI % 2
                nb7 = (tI // 2 + 1) * 16
                w = [(XL7, (tI + 1) * 16), (PT7[b], nb7)]
                if tI >= 1:
                    w += [(XMW, tI * 16), (C7, tI * NS)]
                sch.op("v", (lambda b=b: (lambda v: v.tensor_add(
                    xmid_sb[:, :], xt1[:, :], ptok[b][:, :])))(), waits=w, inc=C7)
                sch.op("v", lambda v: v.reduce_sum(statc[:, 0:1], xmid_sb[:, :],
                                                   axis=AX.X),
                       waits=[(C7, tI * NS + 1)], inc=C7)
                sqw7 = [(C7, tI * NS + 1)]
                if tI >= 1:
                    sqw7.append((X2W, tI * 16))
                sch.op("s", lambda sc: sc.activation(x2t[:, :], xmid_sb[:, :],
                                                     AF.Square, accum_out=statc[:, 2:3]),
                       waits=sqw7, inc=SQ7)
                sch.op("v", lambda v: v.tensor_scalar(statc[:, 4:5], statc[:, 0:1],
                                                      1.0 / E, None, ALU.mult),
                       waits=[(C7, tI * NS + 2)], inc=C7)
                sch.op("v", lambda v: v.tensor_mul(statc[:, 6:7], statc[:, 4:5],
                                                   statc[:, 4:5]),
                       waits=[(C7, tI * NS + 3)], inc=C7)
                sch.op("v", lambda v: v.scalar_tensor_tensor(
                    statc[:, 8:9], statc[:, 2:3], 1.0 / E, statc[:, 6:7],
                    ALU.mult, ALU.subtract),
                    waits=[(C7, tI * NS + 4), (SQ7, 2 * tI + 1)], inc=C7)
                sch.op("s", lambda sc: sc.activation(statc[:, 10:11], statc[:, 8:9],
                                                     AF.Sqrt, bias=epsln_c[:, :]),
                       waits=[(C7, tI * NS + 5)], inc=SQ7)
                sch.op("v", lambda v: v.reciprocal(statc[:, 12:13], statc[:, 10:11]),
                       waits=[(SQ7, 2 * tI + 2)], inc=C7)
                w = [(C7, tI * NS + 6)]
                if tI >= 1:
                    w.append((X2W, tI * 16))
                sch.op("v", lambda v: v.tensor_scalar(
                    x2t[:, :], xmid_sb[:, :], statc[:, 4:5], statc[:, 12:13],
                    ALU.subtract, ALU.mult), waits=w, inc=C7)
                sch.op("g", (lambda tI=tI: (lambda g: g.dma_start(
                    xmid_s[tI * 128:(tI + 1) * 128, :], xmid_sb[:, :])))(),
                    waits=[(C7, tI * NS + 7)], inc=XMW, amt=16)
                sch.op("y", (lambda tI=tI: (lambda y: y.dma_start(
                    x2_s[tI * 128:(tI + 1) * 128, :], x2t[:, :])))(),
                    waits=[(C7, tI * NS + 7)], inc=X2W, amt=16)
                sch.op("v", lambda v: v.tensor_copy(statc[0:1, 14:15], statc[0:1, 0:1]),
                       waits=[(C7, tI * NS + 7)], inc=C7)
                if tI + 2 < NT:
                    b7_load(tI + 2)
            sch.emit_into(nc)
        if _PH <= 8:
            nc.finalize(); return nc

        with contextlib.ExitStack() as r6:
            e6 = r6.enter_context
            wslab = [e6(nc.sbuf_tensor(f"w3slab{i}", [128, NK * 128], BF16)) for i in range(2)]
            oev = [e6(nc.sbuf_tensor(f"o3ev{i}", [128, TPC], BF16)) for i in range(2)]
            ztil = [e6(nc.sbuf_tensor(f"ztil{i}", [128, 512], BF16)) for i in range(2)]
            zf32 = e6(nc.sbuf_tensor("zf32", [128, 512], F32))
            alpha = e6(nc.sbuf_tensor("alpha", [128, 32], F32))
            alphaT = e6(nc.sbuf_tensor("alphaT", [16, 1024], F32))
            nmaxc = e6(nc.sbuf_tensor("nmaxc", [128, 2], F32))
            sumc = e6(nc.sbuf_tensor("sumc", [128, 2], F32))
            rsumc = e6(nc.sbuf_tensor("rsumc", [128, 2], F32))
            psq = [e6(nc.psum_tensor(f"ps3q{i}", [128, 512], F32)) for i in range(4)]
            ps_r = [e6(nc.psum_tensor(f"ps_r{i}", [128, 16], F32)) for i in range(2)]
            ps_aT = e6(nc.psum_tensor("ps_aT", [16, 128], F32))
            ps_zw = e6(nc.psum_tensor("ps_zw", [128, 512], F32))

            sch = Sched()
            FT3 = S_("FT3")
            XMW = S_("XMW")
            X2W = S_("X2W")
            for k in range(NK):
                w = [(X2W, X2W.n)] if k == 0 else []
                sch.op("y", (lambda k=k: (lambda y: y.dma_start(
                    featT[:, k * 1024:(k + 1) * 1024], x2_s[:, k * 128:(k + 1) * 128],
                    transpose=True)))(), waits=w, inc=FT3, amt=16)

            RMM = S_("RMM"); RMX = S_("RMX"); AEX = S_("AEX"); ARC = S_("ARC")
            ANM = S_("ANM"); ATR = S_("ATR"); AEV = S_("AEV"); WMM = S_("WMM")
            ZMM = S_("ZMM"); ZEV = S_("ZEV"); ZTL = S_("ZTL")
            mi = S_("mi")
            for g2 in range(2):
                for ts in range(4):
                    i = g2 * 4 + ts
                    bts = i % 2
                    w = [(FT3, NK * 16), (mi, 7)] if i == 0 else []
                    if i >= 2:
                        w.append((AEX, i - 1))
                    def emit_r(t, g2=g2, ts=ts, bts=bts):
                        pr = ps_r[bts]
                        t.matmul(pr[:, :], ones_r128f[:, :], rblnp[:, :],
                                 start=True, stop=False)
                        mm = None
                        for k in range(NK):
                            mm = t.matmul(pr[:, :],
                                          featT[:, k * 1024 + g2 * 512 + ts * 128:
                                                k * 1024 + g2 * 512 + ts * 128 + 128],
                                          rwsb[:, k * 128:k * 128 + 16],
                                          start=False, stop=(k == NK - 1))
                        return mm
                    sch.op("t", emit_r, waits=w, inc=RMM)
                    w = [(RMM, i + 1)]
                    if i >= 2:
                        w.append((AEX, i - 1))
                    sch.op("v", (lambda bts=bts: (lambda v: v.reduce_max(
                        nmaxc[:, bts:bts + 1], ps_r[bts][:, :], axis=AX.X,
                        negate=True)))(), waits=w, inc=RMX)
                    w = [(RMX, i + 1)]
                    if i >= 2:
                        w.append((ATR, i - 1))
                    sch.op("s", (lambda bts=bts: (lambda sc: sc.activation(
                        alpha[:, bts * 16:bts * 16 + 16], ps_r[bts][:, :], AF.Exp,
                        bias=nmaxc[:, bts:bts + 1],
                        accum_out=sumc[:, bts:bts + 1])))(), waits=w, inc=AEX)
                    sch.op("v", (lambda bts=bts: (lambda v: v.reciprocal(
                        rsumc[:, bts:bts + 1], sumc[:, bts:bts + 1])))(),
                        waits=[(AEX, i + 1)], inc=ARC)
                    sch.op("v", (lambda bts=bts: (lambda v: v.tensor_scalar(
                        alpha[:, bts * 16:bts * 16 + 16], alpha[:, bts * 16:bts * 16 + 16],
                        rsumc[:, bts:bts + 1], None, ALU.mult)))(),
                        waits=[(ARC, i + 1)], inc=ANM)
                    w = [(ANM, i + 1)]
                    if i >= 1:
                        w.append((AEV, i))
                    sch.op("t", (lambda bts=bts: (lambda t: t.transpose(
                        ps_aT[:, :], alpha[:, bts * 16:bts * 16 + 16], identf[:, :])))(),
                        waits=w, inc=ATR)
                    sch.op("s", (lambda g2=g2, ts=ts: (lambda sc: sc.activation(
                        alphaT[:, g2 * 512 + ts * 128:g2 * 512 + (ts + 1) * 128],
                        ps_aT[:, :], AF.Copy)))(), waits=[(ATR, i + 1)], inc=AEV)
            for g2 in range(2):
                w = [(FT3, NK * 16)]
                if g2 == 1:
                    w.append((ZTL, 1))
                def emit_z(t, g2=g2):
                    mm = None
                    for k in range(NK):
                        mm = t.matmul(ps_zw[:, :], acat_sb[:, k * 128:(k + 1) * 128],
                                      featT[:, k * 1024 + g2 * 512:
                                            k * 1024 + g2 * 512 + 512],
                                      start=(k == 0), stop=(k == NK - 1))
                    return mm
                sch.op("t", emit_z, waits=w, inc=ZMM)
                zevw = [(ZMM, g2 + 1)]
                if g2 == 1:
                    zevw.append((ZTL, 1))
                sch.op("v", (lambda: (lambda v: v.tensor_copy(zf32[:, :],
                                                              ps_zw[:, :])))(),
                       waits=zevw, inc=ZEV)
                sch.op("t", (lambda g2=g2: (lambda t: t.matmul(
                    ps_zw[:, :], e16sc[:, :], alphaT[:, g2 * 512:(g2 + 1) * 512],
                    start=True, stop=True)))(),
                    waits=[(AEV, g2 * 4 + 4), (ZEV, g2 + 1)], inc=WMM)
                sch.op("v", (lambda g2=g2: (lambda v: v.tensor_tensor(
                    ztil[g2][:, :], zf32[:, :], ps_zw[:, :], ALU.mult)))(),
                    waits=[(ZEV, g2 + 1), (WMM, g2 + 1)], inc=ZTL)

            def ffn_extra(t, p, m, g2):
                return t.matmul(p[:, :], bcat_sb[:, m * 128:(m + 1) * 128],
                                ztil[g2][:, :], start=False, stop=True)
            sch2, QW3 = mm_phase2("P5", wb_t, NK, ffn_s,
                                  first_waits=[(FT3, NK * 16)],
                                  bias_col=lambda m: bbT[:, m:m + 1],
                                  extra_mm=ffn_extra, extra_mm_waits=[(ZTL, 2)],
                                  psq=psq, wslab=wslab, oev=oev,
                                  load_first_waits=[(XMW, XMW.n), (X2W, X2W.n)])
            sch.ops.extend(sch2.ops)
            sch.emit_into(nc)
            FW = QW3
        if _PH <= 9:
            nc.finalize(); return nc

        with contextlib.ExitStack() as r7:
            e7 = r7.enter_context
            xm2 = [e7(nc.sbuf_tensor(f"xm2_{i}", [128, E], F32)) for i in range(2)]
            ftok = [e7(nc.sbuf_tensor(f"ftok{i}", [128, E], BF16)) for i in range(2)]

            sch = Sched()
            XL9 = [S_("XL90"), S_("XL91")]
            FT9 = [S_("FT90"), S_("FT91")]
            A9 = S_("A9")
            OW = [S_("OW0"), S_("OW1")]
            def b9_load(tI):
                b = tI % 2
                w = ([(XMW, XMW.n), (FW[0], FW[0].n), (FW[1], FW[1].n)]
                     if tI == 0 else [])
                if tI >= 2:
                    w.append((OW[b], (tI // 2) * 16))
                sch.op("g", (lambda tI=tI, b=b: (lambda g: g.dma_start(
                    xm2[b][:, :], xmid_s[tI * 128:(tI + 1) * 128, :])))(),
                    waits=w, inc=XL9[b], amt=16)
                w = ([(FW[0], FW[0].n), (FW[1], FW[1].n), (XMW, XMW.n)]
                     if tI == 0 else [])
                if tI >= 2:
                    w.append((A9, tI - 1))
                sch.op("y", (lambda tI=tI, b=b: (lambda y: y.dma_start(
                    ftok[b][:, :], ffn_s[:, tI * 128:(tI + 1) * 128],
                    transpose=True)))(), waits=w, inc=FT9[b], amt=16)

            b9_load(0)
            b9_load(1)
            for tI in range(NT):
                b = tI % 2
                w = [(XL9[b], (tI // 2 + 1) * 16), (FT9[b], (tI // 2 + 1) * 16)]
                sch.op("v", (lambda b=b: (lambda v: v.tensor_add(
                    xm2[b][:, :], xm2[b][:, :], ftok[b][:, :])))(), waits=w, inc=A9)
                sch.op("g", (lambda tI=tI, b=b: (lambda g: g.dma_start(
                    out_t[tI * 128:(tI + 1) * 128, :], xm2[b][:, :])))(),
                    waits=[(A9, tI + 1)], inc=OW[b], amt=16)
                if tI + 2 < NT:
                    b9_load(tI + 2)
            DBG = S_("DBG")
            for (o, src) in dbg_pairs:
                sch.op("g", (lambda o=o, src=src: (lambda g: g.dma_start(o, src)))(),
                       waits=[(A9, NT)], inc=DBG, amt=16)
            def fin_wait(g):
                g.wait_ge(OW[0].h, OW[0].n)
                g.wait_ge(OW[1].h, OW[1].n)
                if dbg_pairs:
                    g.wait_ge(DBG.h, DBG.n)
                return None
            sch.ops.append(("g", [], fin_wait, []))
            sch.emit_into(nc)

    nc.finalize()
    return nc


def _prep_shared(w_qkv, w_c, W_base, b_base, lora_A, lora_B, router_W,
                 router_b, domain_prior, active):
    def tile4(w, n_m):
        return np.ascontiguousarray(
            w.reshape(NK, 128, n_m, 128).transpose(2, 1, 0, 3).reshape(n_m, 128, NK * 128))
    sh = {}
    sh["wq_t"] = tile4(np.asarray(w_qkv, np.float32), NQ)
    sh["wc_t"] = tile4(np.asarray(w_c, np.float32), NK)
    sh["wb_t"] = tile4(np.ascontiguousarray(np.asarray(W_base, np.float32).T), NK)
    A = np.asarray(lora_A, np.float32).reshape(128, E)
    sh["acat_t"] = np.ascontiguousarray(
        A.T.reshape(NK, 128, 128).transpose(1, 0, 2).reshape(128, NK * 128))
    sh["bcat_t"] = np.ascontiguousarray(
        np.asarray(lora_B, np.float32).transpose(0, 2, 1).reshape(128, E))
    rwT = np.zeros((E, 128), np.float32)
    rwT[:, :16] = np.asarray(router_W, np.float32).T
    sh["rw_t"] = np.ascontiguousarray(
        rwT.reshape(NK, 128, 128).transpose(1, 0, 2).reshape(128, NK * 128))
    sh["bbT_i"] = np.ascontiguousarray(np.asarray(b_base, np.float32).reshape(32, 128).T)
    sh["rb_row"] = np.asarray(router_b, np.float32).reshape(1, 16)
    sh["prior_row"] = np.asarray(domain_prior, np.float32).reshape(1, 16)
    sh["active_col"] = np.asarray(active, np.int32).reshape(4, 1)
    sh["posrow_i"] = (np.arange(S, dtype=np.float64) + 8.0).astype(BFD).reshape(1, S)
    mk = np.zeros((128, 4 * 512), np.float32)
    for j in range(4):
        p = np.arange(128)[:, None]
        f = np.arange(512)[None, :]
        mk[:, j * 512:(j + 1) * 512] = np.where(f >= p + 128 * j, 0.0, -1e30)
    sh["maskadd_i"] = mk
    sh["iota16_i"] = np.broadcast_to(np.arange(16, dtype=np.float32)[None, :], (4, 16)).copy()
    e16 = np.zeros((16, 128), np.float32)
    for j in range(16):
        e16[j, j * 8:(j + 1) * 8] = 2.0
    sh["e16sc_i"] = e16
    sh["identf_i"] = np.eye(128, dtype=np.float32)
    return sh


def _prep_core(core):
    sl = _slopes()
    cc = core % 4
    heads = np.arange(8 * cc, 8 * cc + 8)
    bt = np.zeros((128, HPC * 32), np.float32)
    for hh in range(HPC):
        s = sl[heads[hh]]
        for kc in range(32):
            bt[:, hh * 32 + kc] = s * (128.0 * kc + np.arange(128))
    d = {"biastab_i": bt,
         "negslope_i": (-sl[heads]).reshape(1, HPC).astype(np.float32),
         "coff_i": np.asarray([[cc * 1024, cc * 512, (7 - cc) * 512, 0]], np.int32)}
    return d


def kernel(L0_attn_hidden_states, L0_attn_w_qkv, L0_attn_w_c, L0_ffn_W_base,
           L0_ffn_b_base, L0_ffn_lora_A, L0_ffn_lora_B, L0_ffn_router_W,
           L0_ffn_router_b, L0_ffn_domain_prior, L0_ffn_active):
    x = np.asarray(L0_attn_hidden_states, np.float32)
    sh = _prep_shared(L0_attn_w_qkv, L0_attn_w_c, L0_ffn_W_base, L0_ffn_b_base,
                      L0_ffn_lora_A, L0_ffn_lora_B, L0_ffn_router_W,
                      L0_ffn_router_b, L0_ffn_domain_prior, L0_ffn_active)
    dbg = bool(int(os.environ.get("MKDBG", "0")))
    nc = build_nc(dbg=dbg)
    in_maps = []
    for core in range(8):
        b, cc = core // 4, core % 4
        xs = np.concatenate([x[b, 512 * cc:512 * cc + 512],
                             x[b, 512 * (7 - cc):512 * (7 - cc) + 512]], axis=0)
        m = dict(sh)
        m["xs"] = np.ascontiguousarray(xs)
        m.update(_prep_core(core))
        in_maps.append(m)
    res = run_bass_kernel_spmd(nc, in_maps, core_ids=list(range(8)))
    out = np.empty((B, S, E), np.float32)
    for core in range(8):
        b, cc = core // 4, core % 4
        o = np.asarray(res.results[core]["out"])
        out[b, 512 * cc:512 * cc + 512] = o[0:512]
        out[b, 512 * (7 - cc):512 * (7 - cc) + 512] = o[512:1024]
    if dbg:
        kernel._dbg = res.results
    return out


# version 10 (final)
# speedup vs baseline: 1.1172x; 1.0620x over previous
import os
import math
import contextlib
import numpy as np
import ml_dtypes
import concourse.bass as bass
import concourse.mybir as mybir
from concourse.bass_utils import run_bass_kernel_spmd

F32 = mybir.dt.float32
BF16 = mybir.dt.bfloat16
I32 = mybir.dt.int32
AF = mybir.ActivationFunctionType
ALU = mybir.AluOpType
AX = mybir.AxisListType
BFD = ml_dtypes.bfloat16

B, S, E, H, D = 2, 4096, 4096, 32, 128
NK = E // 128
NQ = NK + 2
TPC = 1024
NT = TPC // 128
HPC = H // 4
SCALING = float(D) ** -0.5
EPS_LN = 1e-5
RANKOF = [a if a < 4 else 7 - a for a in range(8)]
LOCOF = [0 if a < 4 else 1 for a in range(8)]
UNITS = [(a, kc, kc >= 4 * a) for a in range(8) for kc in range(4 * a + 4)]
U = len(UNITS)
UPFX = [0]
for a in range(8):
    UPFX.append(UPFX[-1] + 4 * a + 4)
NDIAG = np.cumsum([int(d) for (_, _, d) in UNITS]).tolist()


def _slopes():
    base = 2.0 ** (-(2.0 ** (-(math.log2(H) - 3))))
    return (base ** np.arange(1, H + 1, dtype=np.float64)).astype(np.float32)


class SemC:
    def __init__(self, handle):
        self.h = handle
        self.n = 0


class Sched:
    def __init__(self):
        self.ops = []

    def op(self, eng, emit, waits=(), inc=None, amt=1):
        incs = []
        if inc is not None and amt > 0:
            inc.n += amt
            incs = [(inc, amt)]
        self.ops.append((eng, [(s, v) for (s, v) in waits], emit, incs))

    def emit_into(self, nc):
        keys = sorted({o[0] for o in self.ops})
        with nc.Block() as blk:
            for key in keys:
                def body(engv, key=key):
                    for (ek, waits, emit, incs) in self.ops:
                        if ek != key:
                            continue
                        for (s, v) in waits:
                            engv.wait_ge(s.h, v)
                        inst = emit(engv)
                        for (s, amt) in incs:
                            inst.then_inc(s.h, amt)
                getattr(blk, {"t": "tensor", "v": "vector", "s": "scalar",
                              "g": "gpsimd", "y": "sync"}[key])(body)


def build_nc(dbg=False):
    import os as _os
    _PH = int(_os.environ.get("MKPHASES", "99"))
    nc = bass.Bass("TRN2", target_bir_lowering=False)
    ap = lambda t: t.ap()

    xs = ap(nc.dram_tensor("xs", [TPC, E], F32, kind="ExternalInput"))
    wq_t = ap(nc.dram_tensor("wq_t", [NQ, 128, NK * 128], F32, kind="ExternalInput"))
    wc_t = ap(nc.dram_tensor("wc_t", [NK, 128, NK * 128], F32, kind="ExternalInput"))
    wb_t = ap(nc.dram_tensor("wb_t", [NK, 128, NK * 128], F32, kind="ExternalInput"))
    acat_t = ap(nc.dram_tensor("acat_t", [128, NK * 128], F32, kind="ExternalInput"))
    bcat_t = ap(nc.dram_tensor("bcat_t", [128, NK * 128], F32, kind="ExternalInput"))
    rw_t = ap(nc.dram_tensor("rw_t", [128, NK * 128], F32, kind="ExternalInput"))
    bbT_i = ap(nc.dram_tensor("bbT_i", [128, 32], F32, kind="ExternalInput"))
    rb_row = ap(nc.dram_tensor("rb_row", [1, 16], F32, kind="ExternalInput"))
    prior_row = ap(nc.dram_tensor("prior_row", [1, 16], F32, kind="ExternalInput"))
    active_col = ap(nc.dram_tensor("active_col", [4, 1], I32, kind="ExternalInput"))
    biastab_i = ap(nc.dram_tensor("biastab_i", [128, HPC * 32], F32, kind="ExternalInput"))
    negslope_i = ap(nc.dram_tensor("negslope_i", [1, HPC], F32, kind="ExternalInput"))
    posrow_i = ap(nc.dram_tensor("posrow_i", [1, S], BF16, kind="ExternalInput"))
    maskadd_i = ap(nc.dram_tensor("maskadd_i", [128, 4 * 512], F32, kind="ExternalInput"))
    iota16_i = ap(nc.dram_tensor("iota16_i", [4, 16], F32, kind="ExternalInput"))
    e16sc_i = ap(nc.dram_tensor("e16sc_i", [16, 128], F32, kind="ExternalInput"))
    identf_i = ap(nc.dram_tensor("identf_i", [128, 128], F32, kind="ExternalInput"))
    coff_i = ap(nc.dram_tensor("coff_i", [1, 4], I32, kind="ExternalInput"))

    out_t = ap(nc.dram_tensor("out", [TPC, E], F32, kind="ExternalOutput"))

    x1_s = ap(nc.dram_tensor("x1_s", [TPC, E], BF16))
    qT_s = nc.dram_tensor("qT_s", [NK * 128, TPC], BF16)
    kv_s = nc.dram_tensor("kv_s", [256, TPC], BF16)
    gat1 = nc.dram_tensor("gat1", [2 * 4 * 2048, TPC], BF16)
    gkv = nc.dram_tensor("gkv", [4 * 256, TPC], BF16)
    attc_s = nc.dram_tensor("attc_s", [2 * S, 512], BF16)
    gat2 = nc.dram_tensor("gat2", [2 * 4 * S, 512], BF16)
    qex = nc.dram_tensor("qex", [4096, TPC], BF16)
    attex = nc.dram_tensor("attex", [4096, TPC], BF16)
    proj_s = ap(nc.dram_tensor("proj_s", [E, TPC], BF16))
    xmid_s = ap(nc.dram_tensor("xmid_s", [TPC, E], F32))
    x2_s = ap(nc.dram_tensor("x2_s", [TPC, E], BF16))
    ffn_s = ap(nc.dram_tensor("ffn_s", [E, TPC], BF16))

    dbg_pairs = []
    if dbg:
        for nm, src in [("d_x1", x1_s), ("d_qkvT", qT_s.ap()),
                        ("d_attc", attc_s.ap()), ("d_proj", proj_s),
                        ("d_xmid", xmid_s), ("d_x2", x2_s), ("d_ffn", ffn_s)]:
            o = ap(nc.dram_tensor(nm, list(src.tensor.shape), src.tensor.dtype,
                                  kind="ExternalOutput"))
            dbg_pairs.append((o, src))

    st = contextlib.ExitStack()
    with st:
        e = st.enter_context

        featT = e(nc.sbuf_tensor("featT", [128, NK * 1024], BF16))
        acat_sb = e(nc.sbuf_tensor("acat_sb", [128, NK * 128], BF16))
        bcat_sb = e(nc.sbuf_tensor("bcat_sb", [128, NK * 128], BF16))
        rwsb = e(nc.sbuf_tensor("rwsb", [128, NK * 128], BF16))
        maskadd = e(nc.sbuf_tensor("maskadd", [128, 4 * 512], F32))
        biastab = e(nc.sbuf_tensor("biastab", [128, HPC * 32], F32))
        posrow = e(nc.sbuf_tensor("posrow", [1, S], BF16))
        negslope = e(nc.sbuf_tensor("negslope", [1, HPC], F32))
        bbT = e(nc.sbuf_tensor("bbT", [128, 32], F32))
        identf = e(nc.sbuf_tensor("identf", [128, 128], F32))
        e16sc = e(nc.sbuf_tensor("e16sc", [16, 128], F32))
        iota16 = e(nc.sbuf_tensor("iota16", [4, 16], F32))
        activef = e(nc.sbuf_tensor("activef", [4, 1], F32))
        activei = e(nc.sbuf_tensor("activei", [4, 1], I32))
        onehotT = e(nc.sbuf_tensor("onehotT", [4, 16], F32))
        rblnp = e(nc.sbuf_tensor("rblnp", [1, 16], F32))
        r16a = e(nc.sbuf_tensor("r16a", [1, 16], F32))
        r16b = e(nc.sbuf_tensor("r16b", [1, 16], F32))
        ones_c128b = e(nc.sbuf_tensor("ones_c128b", [128, 1], BF16))
        ones_r128b = e(nc.sbuf_tensor("ones_r128b", [1, 128], BF16))
        ones_r128f = e(nc.sbuf_tensor("ones_r128f", [1, 128], F32))
        ones_c4f = e(nc.sbuf_tensor("ones_c4f", [4, 1], F32))
        negb = [e(nc.sbuf_tensor(f"negb{i}", [1, S], BF16)) for i in range(2)]
        qnk = e(nc.sbuf_tensor("qnk", [1, S], F32))
        kneg2 = e(nc.sbuf_tensor("kneg2", [1, 1], F32))
        kk = e(nc.sbuf_tensor("kk", [1, 1], F32))
        ktmp = e(nc.sbuf_tensor("ktmp", [1, 16], F32))
        statc = e(nc.sbuf_tensor("statc", [128, 16], F32))
        epsln_c = e(nc.sbuf_tensor("epsln_c", [128, 1], F32))
        eps12 = e(nc.sbuf_tensor("eps12", [1, 1], F32))
        eps30 = e(nc.sbuf_tensor("eps30", [1, 1], F32))
        coffs = e(nc.sbuf_tensor("coffs", [1, 4], I32))

        _sems = {}
        def S_(name):
            if name not in _sems:
                _sems[name] = SemC(e(nc.semaphore(name)))
            return _sems[name]

        sch = Sched()
        c0 = S_("c0")
        for dst, src in [(acat_sb, acat_t), (bcat_sb, bcat_t), (rwsb, rw_t),
                         (maskadd, maskadd_i), (biastab, biastab_i),
                         (posrow, posrow_i), (negslope, negslope_i), (bbT, bbT_i),
                         (identf, identf_i), (e16sc, e16sc_i), (iota16, iota16_i),
                         (activei, active_col), (r16a, rb_row), (r16b, prior_row),
                         (coffs, coff_i)]:
            sch.op("g", (lambda d, s2: (lambda g: g.dma_start(d[:, :], s2)))(dst, src),
                   inc=c0, amt=16)
        for buf, val in [(ones_c128b, 1.0), (ones_r128b, 1.0),
                         (ones_r128f, 1.0), (ones_c4f, 1.0),
                         (epsln_c, EPS_LN), (eps12, 1e-12), (eps30, 1e-30)]:
            sch.op("g", (lambda b2, v2: (lambda g: g.memset(b2[:, :], v2)))(buf, val),
                   inc=c0, amt=0)
        sch.emit_into(nc)

        b0b_st = contextlib.ExitStack()
        ps_cnt = b0b_st.enter_context(nc.psum_tensor("ps_cnt", [1, 16], F32))
        sch = Sched()
        mi = S_("mi")
        sch.op("v", lambda v: v.tensor_copy(activef[:, :], activei[:, :]),
               waits=[(c0, c0.n)], inc=mi)
        sch.op("v", lambda v: v.tensor_scalar(onehotT[:, :], iota16[:, :],
                                              activef[:, :], None, ALU.is_equal),
               waits=[(mi, 1)], inc=mi)
        sch.op("t", lambda t: t.matmul(ps_cnt[:, :], ones_c4f[:, :], onehotT[:, :],
                                       start=True, stop=True),
               waits=[(mi, 2)], inc=mi)
        sch.op("s", lambda sc: sc.activation(r16b[:, :], r16b[:, :], AF.Ln, bias=eps12[0:1, 0:1]),
               waits=[(c0, c0.n), (mi, 3)], inc=mi)
        sch.op("s", lambda sc: sc.activation(rblnp[:, :], ps_cnt[:, :], AF.Ln, bias=eps30[0:1, 0:1]),
               waits=[(mi, 4)], inc=mi)
        sch.op("v", lambda v: v.tensor_add(rblnp[:, :], rblnp[:, :], r16a[:, :]),
               waits=[(mi, 5)], inc=mi)
        sch.op("v", lambda v: v.tensor_add(rblnp[:, :], rblnp[:, :], r16b[:, :]),
               waits=[(mi, 6)], inc=mi)
        sch.emit_into(nc)
        b0b_st.close()
        if _PH <= 1:
            nc.finalize(); return nc

        with contextlib.ExitStack() as r1:
            e1 = r1.enter_context
            xt = [e1(nc.sbuf_tensor(f"xt{i}", [128, E], F32)) for i in range(2)]
            x1o = [e1(nc.sbuf_tensor(f"x1o{i}", [128, E], BF16)) for i in range(2)]
            jnk = e1(nc.sbuf_tensor("jnk", [128, E], BF16))

            sch = Sched()
            XL = [S_("XL0"), S_("XL1")]
            CH = S_("CH")
            SQ = S_("SQ")
            XW = [S_("XW0"), S_("XW1")]
            for tI in range(NT):
                b = tI % 2
                w = [(CH, (tI - 2) * 6 + 6)] if tI >= 2 else []
                sch.op("g", (lambda tI=tI, b=b: (lambda g: g.dma_start(
                    xt[b][:, :], xs[tI * 128:(tI + 1) * 128, :])))(),
                    waits=w, inc=XL[b], amt=16)
            for tI in range(NT):
                b = tI % 2
                ldw = [(XL[b], (tI // 2 + 1) * 16)]
                sch.op("v", (lambda b=b: (lambda v: v.reduce_sum(
                    statc[:, 0 + b:1 + b], xt[b][:, :], axis=AX.X)))(),
                    waits=ldw + ([(CH, (tI - 2) * 6 + 6)] if tI >= 2 else []),
                    inc=CH)
                sqw = list(ldw)
                if tI >= 1:
                    sqw.append((SQ, 2 * (tI - 1) + 1))
                if tI >= 2:
                    sqw.append((CH, (tI - 2) * 6 + 6))
                sch.op("s", (lambda b=b: (lambda sc: sc.activation(
                    jnk[:, :], xt[b][:, :], AF.Square, accum_out=statc[:, 2 + b:3 + b])))(),
                    waits=sqw, inc=SQ)
                sch.op("v", (lambda b=b: (lambda v: v.tensor_scalar(
                    statc[:, 4 + b:5 + b], statc[:, 0 + b:1 + b], 1.0 / E, None, ALU.mult)))(),
                    waits=[(CH, tI * 6 + 1)], inc=CH)
                sch.op("v", (lambda b=b: (lambda v: v.tensor_mul(
                    statc[:, 6 + b:7 + b], statc[:, 4 + b:5 + b], statc[:, 4 + b:5 + b])))(),
                    waits=[(CH, tI * 6 + 2)], inc=CH)
                sch.op("v", (lambda b=b: (lambda v: v.scalar_tensor_tensor(
                    statc[:, 8 + b:9 + b], statc[:, 2 + b:3 + b], 1.0 / E,
                    statc[:, 6 + b:7 + b], ALU.mult, ALU.subtract)))(),
                    waits=[(CH, tI * 6 + 3), (SQ, 2 * tI + 1)], inc=CH)
                sqw2 = [(CH, tI * 6 + 4)]
                if tI >= 2:
                    sqw2.append((CH, (tI - 2) * 6 + 5))
                sch.op("s", (lambda b=b: (lambda sc: sc.activation(
                    statc[:, 10 + b:11 + b], statc[:, 8 + b:9 + b], AF.Sqrt, bias=epsln_c[:, :])))(),
                    waits=sqw2, inc=SQ)
                sch.op("v", (lambda b=b: (lambda v: v.reciprocal(
                    statc[:, 12 + b:13 + b], statc[:, 10 + b:11 + b])))(),
                    waits=[(SQ, 2 * tI + 2)], inc=CH)
                aw = [(CH, tI * 6 + 5)]
                if tI >= 2:
                    aw.append((XW[b], (tI // 2) * 16))
                sch.op("v", (lambda b=b: (lambda v: v.tensor_scalar(
                    x1o[b][:, :], xt[b][:, :], statc[:, 4 + b:5 + b],
                    statc[:, 12 + b:13 + b], ALU.subtract, ALU.mult)))(),
                    waits=aw, inc=CH)
                sch.op("y", (lambda tI=tI, b=b: (lambda y: y.dma_start(
                    x1_s[tI * 128:(tI + 1) * 128, :], x1o[b][:, :])))(),
                    waits=[(CH, tI * 6 + 6)], inc=XW[b], amt=16)
            sch.emit_into(nc)
        if _PH <= 2:
            nc.finalize(); return nc

        sch = Sched()
        FT = S_("FT")
        XW = [S_("XW0"), S_("XW1")]
        for k in range(NK):
            w = [(XW[0], XW[0].n), (XW[1], XW[1].n)] if k == 0 else []
            sch.op("y", (lambda k=k: (lambda y: y.dma_start(
                featT[:, k * 1024:(k + 1) * 1024], x1_s[:, k * 128:(k + 1) * 128],
                transpose=True)))(), waits=w, inc=FT, amt=16)
        sch.emit_into(nc)
        if _PH <= 3:
            nc.finalize(); return nc

        def mm_phase2(nm_id, w_tiles, n_m, dst, first_waits, scale_fn=None,
                      bias_col=None, extra_mm=None, extra_mm_waits=(),
                      psq=None, wslab=None, oev=None, load_first_waits=(),
                      m_order=None, dst_fn=None, post_m=None):
            sch = Sched()
            WL = [S_(f"{nm_id}WL0"), S_(f"{nm_id}WL1")]
            MM = S_(f"{nm_id}MM")
            EV = S_(f"{nm_id}EV")
            QW = [S_(f"{nm_id}QW0"), S_(f"{nm_id}QW1")]
            morder = list(m_order) if m_order is not None else list(range(n_m))
            nload = [0, 0]
            for mi_, mv_ in enumerate(morder):
                m, b = mi_, mi_ % 2
                w = [(MM, m - 1)] if m >= 2 else []
                if m == 0:
                    w += list(load_first_waits)
                nload[b] += 1
                sch.op("g", (lambda mv_=mv_, b=b: (lambda g: g.dma_start(
                    wslab[b][:, :], w_tiles[mv_, :, :])))(),
                    waits=w, inc=WL[b], amt=16)
                if post_m is not None:
                    post_m(sch, "load", mi_, QW)
            nl = [0, 0]
            for mi_, mv_ in enumerate(morder):
                m, b = mi_, mi_ % 2
                nl[b] += 1
                def emit_mm(t, m=m, b=b):
                    mm = None
                    for g2 in range(2):
                        p = psq[2 * b + g2]
                        for k in range(NK):
                            mm = t.matmul(p[:, :], wslab[b][:, k * 128:(k + 1) * 128],
                                          featT[:, k * 1024 + g2 * 512:
                                                k * 1024 + g2 * 512 + 512],
                                          start=(k == 0),
                                          stop=(k == NK - 1 and extra_mm is None))
                        if extra_mm is not None:
                            mm = extra_mm(t, p, m, g2)
                    return mm
                w = [(WL[b], nl[b] * 16)]
                if m == 0:
                    w += list(first_waits) + list(extra_mm_waits)
                if m >= 2:
                    w.append((EV, (m - 2) * 2 + 2))
                sch.op("t", emit_mm, waits=w, inc=MM)
            for mi_, mv_ in enumerate(morder):
                m, b = mi_, mi_ % 2
                if bias_col is None:
                    def emit_ev(sc, mv_=mv_, b=b, g2=0):
                        scl = scale_fn(mv_) if scale_fn else 1.0
                        return sc.activation(oev[b][:, g2 * 512:(g2 + 1) * 512],
                                             psq[2 * b + g2][:, :], AF.Copy,
                                             scale=scl)
                    eng_ev = "s"
                else:
                    def emit_ev(v, mv_=mv_, b=b, g2=0):
                        return v.tensor_scalar(oev[b][:, g2 * 512:(g2 + 1) * 512],
                                               psq[2 * b + g2][:, :], bias_col(mv_),
                                               None, ALU.add)
                    eng_ev = "v"
                w = [(MM, m + 1)]
                if m >= 2:
                    w.append((QW[b], (m - 2) // 2 * 16 + 16))
                import functools
                sch.op(eng_ev, functools.partial(emit_ev, g2=0), waits=w, inc=EV)
                sch.op(eng_ev, functools.partial(emit_ev, g2=1), waits=[], inc=EV)
                if dst_fn is None:
                    emit_st = (lambda mv_=mv_, b=b: (lambda y: y.dma_start(
                        dst[mv_ * 128:(mv_ + 1) * 128, :], oev[b][:, :])))()
                else:
                    emit_st = (lambda mv_=mv_, b=b: (lambda y: y.dma_start(
                        dst_fn(mv_), oev[b][:, :])))()
                sch.op("y", emit_st, waits=[(EV, (m + 1) * 2)], inc=QW[b], amt=16)
            return sch, QW

        with contextlib.ExitStack() as r2:
            e2 = r2.enter_context
            wslab = [e2(nc.sbuf_tensor(f"wslab{i}", [128, NK * 128], BF16)) for i in range(2)]
            oev = [e2(nc.sbuf_tensor(f"oev{i}", [128, TPC], BF16)) for i in range(2)]
            psq = [e2(nc.psum_tensor(f"psq{i}", [128, 512], F32)) for i in range(4)]
            XW = [S_("XW0"), S_("XW1")]
            AG1 = S_("AG1")
            AGK = S_("AGK")

            def qdst(mv):
                if mv < NK:
                    return qT_s.ap()[mv * 128:(mv + 1) * 128, :]
                return kv_s.ap()[(mv - NK) * 128:(mv - NK + 1) * 128, :]

            AGA = S_("AGA")

            def post_m(sch_, kind, mi_, QWl):
                if kind == "load" and mi_ == 4:
                    sch_.op("g", lambda g: g.collective_compute(
                        "AllGather", ALU.bypass,
                        replica_groups=[[0, 1, 2, 3], [4, 5, 6, 7]],
                        ins=[kv_s.ap().opt()], outs=[gkv.ap().opt()]),
                        waits=[(QWl[0], 16), (QWl[1], 16)], inc=AGK)
                if kind == "load" and mi_ == 22:
                    sch_.op("g", lambda g: g.collective_compute(
                        "AllGather", ALU.bypass,
                        replica_groups=[[0, 1, 2, 3], [4, 5, 6, 7]],
                        ins=[qT_s.ap()[0:2048, :].opt()],
                        outs=[gat1.ap()[0:4 * 2048, :].opt()]),
                        waits=[(QWl[0], 9 * 16), (QWl[1], 9 * 16)], inc=AGA)

            sch, QW = mm_phase2("P2", wq_t, NQ, None,
                                first_waits=[(FT, NK * 16)],
                                scale_fn=lambda m: SCALING if m < NK else 1.0,
                                psq=psq, wslab=wslab, oev=oev,
                                load_first_waits=[(XW[0], XW[0].n), (XW[1], XW[1].n)],
                                m_order=[NK, NK + 1] + list(range(NK)),
                                dst_fn=qdst, post_m=post_m)
            sch.op("g", lambda g: g.collective_compute(
                "AllGather", ALU.bypass,
                replica_groups=[[0, 1, 2, 3], [4, 5, 6, 7]],
                ins=[qT_s.ap()[2048:4096, :].opt()],
                outs=[gat1.ap()[4 * 2048:8 * 2048, :].opt()]),
                waits=[(QW[0], QW[0].n), (QW[1], QW[1].n)], inc=AG1)
            sch.emit_into(nc)
        if _PH <= 4:
            nc.finalize(); return nc

        with contextlib.ExitStack() as r3:
            e3 = r3.enter_context
            KT = e3(nc.sbuf_tensor("KT", [128, S], BF16))
            Vsb = e3(nc.sbuf_tensor("Vsb", [128, 32 * 160], BF16))
            sqb = e3(nc.sbuf_tensor("sqb", [128, E], BF16))
            qbh = [e3(nc.sbuf_tensor(f"qbh{i}", [128, S], BF16)) for i in range(2)]
            expT = [e3(nc.sbuf_tensor(f"expT{i}", [128, 512], BF16)) for i in range(3)]
            apc = [e3(nc.sbuf_tensor(f"apc{i}", [128, 512], BF16)) for i in range(2)]
            negbT = [e3(nc.sbuf_tensor(f"negbT{i}", [128, 512], BF16)) for i in range(2)]
            recipc = e3(nc.sbuf_tensor("recipc", [128, 8], F32))
            ps_o = [e3(nc.psum_tensor(f"ps_o{i}", [128, 129], F32)) for i in range(4)]
            ps_s = [e3(nc.psum_tensor(f"ps_s{i}", [128, 512], F32)) for i in range(3)]
            ps_q = [e3(nc.psum_tensor("ps_q0", [1, 512], F32))] * 2
            gA = gat1.ap()
            gK = gkv.ap()

            sch = Sched()
            VM = S_("VM")
            KTs = S_("KTs")
            VV = S_("VV")
            KX = S_("KX")
            sch.op("g", lambda g: g.memset(Vsb[:, :], 1.0), waits=[(AGK, 1)], inc=VM)
            for kc in range(32):
                a = kc // 4
                roff = RANKOF[a] * 256
                coff = LOCOF[a] * 512 + (kc % 4) * 128
                sch.op("g", (lambda kc=kc, roff=roff, coff=coff: (lambda g: g.dma_start(
                    KT[:, kc * 128:(kc + 1) * 128],
                    gK[roff:roff + 128, coff:coff + 128])))(),
                    waits=[(AGK, 1)] if kc == 0 else [], inc=KTs, amt=16)
            for kc in range(32):
                a = kc // 4
                roff = RANKOF[a] * 256 + 128
                coff = LOCOF[a] * 512 + (kc % 4) * 128
                sch.op("y", (lambda kc=kc, roff=roff, coff=coff: (lambda y: y.dma_start(
                    Vsb[:, kc * 160:kc * 160 + 128],
                    gK[roff:roff + 128, coff:coff + 128], transpose=True)))(),
                    waits=[(AGK, 1), (VM, 1)] if kc == 0 else [], inc=VV, amt=16)
            sch.op("s", lambda sc: sc.activation(sqb[:, :], KT[:, :], AF.Square),
                   waits=[(KTs, 32 * 16)], inc=KX)
            KQ = S_("KQ")
            for j in range(8):
                sch.op("t", (lambda j=j: (lambda t: t.matmul(
                    ps_q[j % 2][:, :], ones_c128b[:, :], sqb[:, j * 512:(j + 1) * 512],
                    start=True, stop=True)))(),
                    waits=[(KX, 1)] + ([(KQ, j)] if j >= 1 else []), inc=KX)
                sch.op("v", (lambda j=j: (lambda v: v.reduce_max(
                    ktmp[:, j:j + 1], ps_q[j % 2][:, :], axis=AX.X)))(),
                    waits=[(KX, j + 2)], inc=KQ)
            sch.op("v", lambda v: v.reduce_max(kk[:, :], ktmp[:, 0:8], axis=AX.X),
                   waits=[(KQ, 8)], inc=KQ)
            sch.op("v", lambda v: v.tensor_scalar(kneg2[:, :], kk[:, :], 1.024144,
                                                  None, ALU.mult),
                   waits=[(KQ, 9)], inc=KQ)
            sch.emit_into(nc)
            if _PH <= 5:
                nc.finalize(); return nc

            sch = Sched()
            QB = [S_("QB0"), S_("QB1")]
            SQA = S_("SQA")
            SSQ = S_("SSQ")
            QE = S_("QE")
            NB = S_("NB")
            SC = S_("SC")
            NBI = S_("NBI")
            NBT = S_("NBT")
            VA = S_("VA")
            EX = S_("EX")
            MK = S_("MK")
            PV = S_("PV")
            RC = S_("RC")
            FIN = S_("FIN")
            ACW = [S_("ACW0"), S_("ACW1")]
            AG2 = S_("AG2")

            QX = S_("QX")
            def emit_qx(y):
                with y.register("qxa") as r1, y.register("qxb") as r2, \
                        y.register("qxc") as r0:
                    y.reg_load(r0, coffs[0:1, 3:4])
                    y.reg_mul(r1, r0, TPC)
                    for r in range(4):
                        y.reg_add(r2, r1, r * 2048 * TPC)
                        y.dma_start(qex.ap()[r * 1024:(r + 1) * 1024, :],
                                    bass.AP(gat1, r2, [[TPC, 1024], [1, TPC]])) \
                            .then_inc(QX.h, 16)
                return None
            sch.ops.append(("y", [(AGA, 1), (AG1, AG1.n)], emit_qx, []))
            QX.n += 4 * 16
            qexa = qex.ap()

            def emit_qload_f(hh):
                b = hh % 2
                def emit_qload(y, hh=hh, b=b):
                    for a in range(8):
                        roff = RANKOF[a] * 1024 + hh * 128
                        coff = LOCOF[a] * 512
                        y.dma_start(qbh[b][:, a * 512:(a + 1) * 512],
                                    qexa[roff:roff + 128, coff:coff + 512]) \
                            .then_inc(QB[b].h, 16)
                    return None
                w = [(QX, 4 * 16)] if hh == 0 else []
                if hh >= 2:
                    w.append((PV, (hh - 1) * U))
                sch.ops.append(("y", list(w), emit_qload, []))
                QB[b].n += 8 * 16

            _NHH = int(_os.environ.get("MKNHH", str(HPC)))
            emit_qload_f(0)
            if _NHH > 1:
                emit_qload_f(1)
            for hh in range(_NHH):
                b = hh % 2
                w = [(QB[b], (hh // 2 + 1) * 8 * 16)]
                if hh >= 1:
                    w.append((SSQ, 8 * hh))
                else:
                    w.append((KQ, 10))
                sch.op("s", (lambda b=b: (lambda sc: sc.activation(
                    sqb[:, :], qbh[b][:, :], AF.Square)))(), waits=w, inc=SQA)
                for a in range(8):
                    w = [(SQA, hh + 1)]
                    if hh * 8 + a >= 1:
                        w.append((QE, hh * 8 + a))
                    if hh == 0 and a < 1:
                        w.append((KQ, 10))
                    sch.op("t", (lambda a=a: (lambda t: t.matmul(
                        ps_q[a % 2][:, :], ones_c128b[:, :],
                        sqb[:, a * 512:(a + 1) * 512], start=True, stop=True)))(),
                        waits=w, inc=SSQ)
                    w2 = [(SSQ, hh * 8 + a + 1), (KQ, 10)]
                    if hh >= 1 and a == 0:
                        w2.append((NB, hh))
                    sch.op("s", (lambda a=a: (lambda sc: sc.activation(
                        qnk[0:1, a * 512:(a + 1) * 512], ps_q[a % 2][:, :],
                        AF.Sqrt, scale=kneg2[0:1, 0:1])))(),
                        waits=w2, inc=QE)
                w = [(QE, (hh + 1) * 8)]
                if hh >= 2:
                    w.append((SC, (hh - 1) * U))
                sch.op("v", (lambda hh=hh, b=b: (lambda v: v.scalar_tensor_tensor(
                    negb[b][:, :], posrow[:, :], negslope[0:1, hh:hh + 1], qnk[:, :],
                    ALU.mult, ALU.subtract)))(), waits=w, inc=NB)

                def emit_pv(a, kc, gu, pi):
                    w = [(EX, gu + 1)]
                    if kc == 0 and pi >= 1:
                        w.append((FIN, pi * 4))
                    sch.op("t", (lambda a=a, kc=kc, bu=gu % 3: (lambda t: [
                        t.matmul(ps_o[qt][:, :], expT[bu][:, qt * 128:(qt + 1) * 128],
                                 Vsb[:, kc * 160:kc * 160 + 129],
                                 start=(kc == 0), stop=(kc == 4 * a + 3))
                        for qt in range(4)][-1]))(),
                        waits=w, inc=PV)

                pending = []
                for a in range(8):
                    pi = hh * 8 + a
                    bq = pi % 2
                    for kc in range(4 * a + 4):
                        u = UPFX[a] + kc
                        gu = hh * U + u
                        diag = kc >= 4 * a
                        w = [(NB, hh + 1)] if u == 0 else []
                        if gu >= 3:
                            w.append((EX, gu - 2))
                        if gu == 0:
                            w += [(KTs, 32 * 16), (VV, 32 * 16)]
                        sch.op("t", (lambda a=a, kc=kc, b=b, bu=gu % 3: (lambda t: (
                            t.matmul(ps_s[bu][:, :], ones_r128b[:, :],
                                     negb[b][0:1, a * 512:(a + 1) * 512],
                                     start=True, stop=False),
                            t.matmul(ps_s[bu][:, :], KT[:, kc * 128:(kc + 1) * 128],
                                     qbh[b][:, a * 512:(a + 1) * 512],
                                     start=False, stop=True))[-1]))(),
                            waits=w, inc=SC)
                        pending.append((a, kc, gu, pi))
                        if len(pending) > 2:
                            emit_pv(*pending.pop(0))
                        if diag:
                            j = kc - 4 * a
                            sch.op("v", (lambda j=j, bu=gu % 3: (lambda v: v.tensor_add(
                                ps_s[bu][:, :], ps_s[bu][:, :],
                                maskadd[:, j * 512:(j + 1) * 512])))(),
                                waits=[(SC, gu + 1)], inc=MK)
                        w = [(SC, gu + 1)]
                        if diag:
                            w.append((MK, MK.n))
                        if gu >= 3:
                            w.append((PV, gu - 2))
                        sch.op("s", (lambda hh=hh, kc=kc, bu=gu % 3: (lambda sc: sc.activation(
                            expT[bu][:, :], ps_s[bu][:, :], AF.Exp,
                            bias=biastab[:, hh * 32 + kc:hh * 32 + kc + 1])))(),
                            waits=w, inc=EX)
                        if kc == 4 * a + 3:
                            while pending:
                                emit_pv(*pending.pop(0))
                    for qt in range(4):
                        w = [(PV, hh * U + UPFX[a + 1])]
                        if pi >= 2:
                            w.append((FIN, (pi - 1) * 4))
                        sch.op("v", (lambda qt=qt, bq=bq: (lambda v: v.reciprocal(
                            recipc[:, bq * 4 + qt:bq * 4 + qt + 1],
                            ps_o[qt][:, 128:129])))(), waits=w, inc=RC)
                    for qt in range(4):
                        w = [(RC, pi * 4 + qt + 1)]
                        if pi >= 2:
                            w.append((ACW[bq], (pi // 2) * 16))
                        sch.op("v", (lambda qt=qt, bq=bq: (lambda v: v.tensor_scalar(
                            apc[bq][:, qt * 128:(qt + 1) * 128],
                            ps_o[qt][:, 0:128],
                            recipc[:, bq * 4 + qt:bq * 4 + qt + 1], None, ALU.mult)))(),
                            waits=w, inc=FIN)
                    sch.op("y", (lambda hh=hh, a=a, bq=bq: (lambda y: y.dma_start(
                        attc_s.ap().rearrange("(q p) c -> p q c", p=128)
                        [:, (hh // 4) * 32 + 4 * a:(hh // 4) * 32 + 4 * a + 4,
                         (hh % 4) * 128:(hh % 4) * 128 + 128],
                        apc[bq][:, :].rearrange("p (q c) -> p q c", q=4))))(),
                        waits=[(FIN, (pi + 1) * 4)], inc=ACW[bq], amt=16)
                if hh + 2 < _NHH:
                    emit_qload_f(hh + 2)
            AGB = S_("AGB")
            sch.op("g", lambda g: g.collective_compute(
                "AllGather", ALU.bypass,
                replica_groups=[[0, 1, 2, 3], [4, 5, 6, 7]],
                ins=[attc_s.ap()[0:S, :].opt()], outs=[gat2.ap()[0:4 * S, :].opt()]),
                waits=[(ACW[0], 16 * 16), (ACW[1], 16 * 16)], inc=AGB)
            sch.op("g", lambda g: g.collective_compute(
                "AllGather", ALU.bypass,
                replica_groups=[[0, 1, 2, 3], [4, 5, 6, 7]],
                ins=[attc_s.ap()[S:2 * S, :].opt()],
                outs=[gat2.ap()[4 * S:8 * S, :].opt()]),
                waits=[(ACW[0], ACW[0].n), (ACW[1], ACW[1].n)], inc=AG2)
            sch.emit_into(nc)
            if _PH <= 6:
                nc.finalize(); return nc

        with contextlib.ExitStack() as r4:
            e4 = r4.enter_context
            wslab = [e4(nc.sbuf_tensor(f"w2slab{i}", [128, NK * 128], BF16)) for i in range(2)]
            oev = [e4(nc.sbuf_tensor(f"o2ev{i}", [128, TPC], BF16)) for i in range(2)]
            psq = [e4(nc.psum_tensor(f"ps2q{i}", [128, 512], F32)) for i in range(4)]
            gB = gat2.ap()
            AG2 = S_("AG2")
            FT2 = S_("FT2")

            sch = Sched()
            AX2 = S_("AX2")
            AX3 = S_("AX3")
            AGB = S_("AGB")
            def mk_atx(hhalf, sem):
                def emit_atx(g):
                    with g.register(f"axa{hhalf}") as r1, \
                            g.register(f"axb{hhalf}") as r2, \
                            g.register(f"axc{hhalf}") as r0:
                        for th, cidx in ((0, 1), (1, 2)):
                            g.reg_load(r0, coffs[0:1, cidx:cidx + 1])
                            g.reg_mul(r1, r0, 512)
                            for J in range(4):
                                g.reg_add(r2, r1, (hhalf * 4 * S + J * S) * 512)
                                g.dma_start(
                                    attex.ap()[J * 1024 + th * 512:
                                               J * 1024 + th * 512 + 512,
                                               hhalf * 512:hhalf * 512 + 512],
                                    bass.AP(gat2, r2, [[512, 512], [1, 512]])) \
                                    .then_inc(sem.h, 16)
                    return None
                return emit_atx
            sch.ops.append(("g", [(AGB, 1)], mk_atx(0, AX2), []))
            sch.ops.append(("y", [(AG2, 1)], mk_atx(1, AX3), []))
            AX2.n += 8 * 16
            AX3.n += 8 * 16
            atxa = attex.ap()
            def emit_atl(y):
                ii = None
                for k in range(NK):
                    J, HHl = k // HPC, k % HPC
                    ii = y.dma_start(
                        featT[:, k * 1024:(k + 1) * 1024],
                        atxa[J * 1024:(J + 1) * 1024, HHl * 128:(HHl + 1) * 128],
                        transpose=True)
                    ii.then_inc(FT2.h, 16)
                return None
            sch.ops.append(("y", [(AX2, 8 * 16), (AX3, 8 * 16)], emit_atl, []))
            FT2.n += NK * 16
            ACW = [S_("ACW0"), S_("ACW1")]
            sch2, QW2 = mm_phase2("P4", wc_t, NK, proj_s,
                                  first_waits=[(FT2, FT2.n)],
                                  psq=psq, wslab=wslab, oev=oev,
                                  load_first_waits=[(ACW[0], ACW[0].n),
                                                    (ACW[1], ACW[1].n)])
            sch.ops.extend(sch2.ops)
            sch.emit_into(nc)
            PJ = QW2
        if _PH <= 7:
            nc.finalize(); return nc

        with contextlib.ExitStack() as r5:
            e5 = r5.enter_context
            xt1 = e5(nc.sbuf_tensor("xt1b7", [128, E], F32))
            ptok = [e5(nc.sbuf_tensor(f"ptok{i}", [128, E], BF16)) for i in range(2)]
            xmid_sb = e5(nc.sbuf_tensor("xmid_sb", [128, E], F32))
            x2t = e5(nc.sbuf_tensor("x2t", [128, E], BF16))

            sch = Sched()
            XL7 = S_("XL7")
            PT7 = [S_("PT70"), S_("PT71")]
            C7 = S_("C7")
            SQ7 = S_("SQ7")
            XMW = S_("XMW")
            X2W = S_("X2W")
            NS = 8

            def b7_load(tI):
                b = tI % 2
                w = ([(C7, (tI - 1) * NS + 1)] if tI >= 1 else
                     [(PJ[0], PJ[0].n), (PJ[1], PJ[1].n)])
                sch.op("g", (lambda tI=tI: (lambda g: g.dma_start(
                    xt1[:, :], xs[tI * 128:(tI + 1) * 128, :])))(),
                    waits=w, inc=XL7, amt=16)
                w = [(PJ[0], PJ[0].n), (PJ[1], PJ[1].n)] if tI == 0 else []
                if tI >= 2:
                    w.append((C7, (tI - 1) * NS + 1))
                sch.op("y", (lambda tI=tI, b=b: (lambda y: y.dma_start(
                    ptok[b][:, :], proj_s[:, tI * 128:(tI + 1) * 128],
                    transpose=True)))(), waits=w, inc=PT7[b], amt=16)

            b7_load(0)
            b7_load(1)
            for tI in range(NT):
                b = tI % 2
                nb7 = (tI // 2 + 1) * 16
                w = [(XL7, (tI + 1) * 16), (PT7[b], nb7)]
                if tI >= 1:
                    w += [(XMW, tI * 16), (C7, tI * NS)]
                sch.op("v", (lambda b=b: (lambda v: v.tensor_add(
                    xmid_sb[:, :], xt1[:, :], ptok[b][:, :])))(), waits=w, inc=C7)
                sch.op("v", lambda v: v.reduce_sum(statc[:, 0:1], xmid_sb[:, :],
                                                   axis=AX.X),
                       waits=[(C7, tI * NS + 1)], inc=C7)
                sqw7 = [(C7, tI * NS + 1)]
                if tI >= 1:
                    sqw7.append((X2W, tI * 16))
                sch.op("s", lambda sc: sc.activation(x2t[:, :], xmid_sb[:, :],
                                                     AF.Square, accum_out=statc[:, 2:3]),
                       waits=sqw7, inc=SQ7)
                sch.op("v", lambda v: v.tensor_scalar(statc[:, 4:5], statc[:, 0:1],
                                                      1.0 / E, None, ALU.mult),
                       waits=[(C7, tI * NS + 2)], inc=C7)
                sch.op("v", lambda v: v.tensor_mul(statc[:, 6:7], statc[:, 4:5],
                                                   statc[:, 4:5]),
                       waits=[(C7, tI * NS + 3)], inc=C7)
                sch.op("v", lambda v: v.scalar_tensor_tensor(
                    statc[:, 8:9], statc[:, 2:3], 1.0 / E, statc[:, 6:7],
                    ALU.mult, ALU.subtract),
                    waits=[(C7, tI * NS + 4), (SQ7, 2 * tI + 1)], inc=C7)
                sch.op("s", lambda sc: sc.activation(statc[:, 10:11], statc[:, 8:9],
                                                     AF.Sqrt, bias=epsln_c[:, :]),
                       waits=[(C7, tI * NS + 5)], inc=SQ7)
                sch.op("v", lambda v: v.reciprocal(statc[:, 12:13], statc[:, 10:11]),
                       waits=[(SQ7, 2 * tI + 2)], inc=C7)
                w = [(C7, tI * NS + 6)]
                if tI >= 1:
                    w.append((X2W, tI * 16))
                sch.op("v", lambda v: v.tensor_scalar(
                    x2t[:, :], xmid_sb[:, :], statc[:, 4:5], statc[:, 12:13],
                    ALU.subtract, ALU.mult), waits=w, inc=C7)
                sch.op("g", (lambda tI=tI: (lambda g: g.dma_start(
                    xmid_s[tI * 128:(tI + 1) * 128, :], xmid_sb[:, :])))(),
                    waits=[(C7, tI * NS + 7)], inc=XMW, amt=16)
                sch.op("y", (lambda tI=tI: (lambda y: y.dma_start(
                    x2_s[tI * 128:(tI + 1) * 128, :], x2t[:, :])))(),
                    waits=[(C7, tI * NS + 7)], inc=X2W, amt=16)
                sch.op("v", lambda v: v.tensor_copy(statc[0:1, 14:15], statc[0:1, 0:1]),
                       waits=[(C7, tI * NS + 7)], inc=C7)
                if tI + 2 < NT:
                    b7_load(tI + 2)
            sch.emit_into(nc)
        if _PH <= 8:
            nc.finalize(); return nc

        with contextlib.ExitStack() as r6:
            e6 = r6.enter_context
            wslab = [e6(nc.sbuf_tensor(f"w3slab{i}", [128, NK * 128], BF16)) for i in range(2)]
            oev = [e6(nc.sbuf_tensor(f"o3ev{i}", [128, TPC], BF16)) for i in range(2)]
            ztil = [e6(nc.sbuf_tensor(f"ztil{i}", [128, 512], BF16)) for i in range(2)]
            zf32 = e6(nc.sbuf_tensor("zf32", [128, 512], F32))
            alpha = e6(nc.sbuf_tensor("alpha", [128, 32], F32))
            alphaT = e6(nc.sbuf_tensor("alphaT", [16, 1024], F32))
            nmaxc = e6(nc.sbuf_tensor("nmaxc", [128, 2], F32))
            sumc = e6(nc.sbuf_tensor("sumc", [128, 2], F32))
            rsumc = e6(nc.sbuf_tensor("rsumc", [128, 2], F32))
            psq = [e6(nc.psum_tensor(f"ps3q{i}", [128, 512], F32)) for i in range(4)]
            ps_r = [e6(nc.psum_tensor(f"ps_r{i}", [128, 16], F32)) for i in range(2)]
            ps_aT = e6(nc.psum_tensor("ps_aT", [16, 128], F32))
            ps_zw = e6(nc.psum_tensor("ps_zw", [128, 512], F32))

            sch = Sched()
            FT3 = S_("FT3")
            XMW = S_("XMW")
            X2W = S_("X2W")
            for k in range(NK):
                w = [(X2W, X2W.n)] if k == 0 else []
                sch.op("y", (lambda k=k: (lambda y: y.dma_start(
                    featT[:, k * 1024:(k + 1) * 1024], x2_s[:, k * 128:(k + 1) * 128],
                    transpose=True)))(), waits=w, inc=FT3, amt=16)

            RMM = S_("RMM"); RMX = S_("RMX"); AEX = S_("AEX"); ARC = S_("ARC")
            ANM = S_("ANM"); ATR = S_("ATR"); AEV = S_("AEV"); WMM = S_("WMM")
            ZMM = S_("ZMM"); ZEV = S_("ZEV"); ZTL = S_("ZTL")
            mi = S_("mi")
            for g2 in range(2):
                for ts in range(4):
                    i = g2 * 4 + ts
                    bts = i % 2
                    w = [(FT3, NK * 16), (mi, 7)] if i == 0 else []
                    if i >= 2:
                        w.append((AEX, i - 1))
                    def emit_r(t, g2=g2, ts=ts, bts=bts):
                        pr = ps_r[bts]
                        t.matmul(pr[:, :], ones_r128f[:, :], rblnp[:, :],
                                 start=True, stop=False)
                        mm = None
                        for k in range(NK):
                            mm = t.matmul(pr[:, :],
                                          featT[:, k * 1024 + g2 * 512 + ts * 128:
                                                k * 1024 + g2 * 512 + ts * 128 + 128],
                                          rwsb[:, k * 128:k * 128 + 16],
                                          start=False, stop=(k == NK - 1))
                        return mm
                    sch.op("t", emit_r, waits=w, inc=RMM)
                    w = [(RMM, i + 1)]
                    if i >= 2:
                        w.append((AEX, i - 1))
                    sch.op("v", (lambda bts=bts: (lambda v: v.reduce_max(
                        nmaxc[:, bts:bts + 1], ps_r[bts][:, :], axis=AX.X,
                        negate=True)))(), waits=w, inc=RMX)
                    w = [(RMX, i + 1)]
                    if i >= 2:
                        w.append((ATR, i - 1))
                    sch.op("s", (lambda bts=bts: (lambda sc: sc.activation(
                        alpha[:, bts * 16:bts * 16 + 16], ps_r[bts][:, :], AF.Exp,
                        bias=nmaxc[:, bts:bts + 1],
                        accum_out=sumc[:, bts:bts + 1])))(), waits=w, inc=AEX)
                    sch.op("v", (lambda bts=bts: (lambda v: v.reciprocal(
                        rsumc[:, bts:bts + 1], sumc[:, bts:bts + 1])))(),
                        waits=[(AEX, i + 1)], inc=ARC)
                    sch.op("v", (lambda bts=bts: (lambda v: v.tensor_scalar(
                        alpha[:, bts * 16:bts * 16 + 16], alpha[:, bts * 16:bts * 16 + 16],
                        rsumc[:, bts:bts + 1], None, ALU.mult)))(),
                        waits=[(ARC, i + 1)], inc=ANM)
                    w = [(ANM, i + 1)]
                    if i >= 1:
                        w.append((AEV, i))
                    sch.op("t", (lambda bts=bts: (lambda t: t.transpose(
                        ps_aT[:, :], alpha[:, bts * 16:bts * 16 + 16], identf[:, :])))(),
                        waits=w, inc=ATR)
                    sch.op("s", (lambda g2=g2, ts=ts: (lambda sc: sc.activation(
                        alphaT[:, g2 * 512 + ts * 128:g2 * 512 + (ts + 1) * 128],
                        ps_aT[:, :], AF.Copy)))(), waits=[(ATR, i + 1)], inc=AEV)
            for g2 in range(2):
                w = [(FT3, NK * 16)]
                if g2 == 1:
                    w.append((ZTL, 1))
                def emit_z(t, g2=g2):
                    mm = None
                    for k in range(NK):
                        mm = t.matmul(ps_zw[:, :], acat_sb[:, k * 128:(k + 1) * 128],
                                      featT[:, k * 1024 + g2 * 512:
                                            k * 1024 + g2 * 512 + 512],
                                      start=(k == 0), stop=(k == NK - 1))
                    return mm
                sch.op("t", emit_z, waits=w, inc=ZMM)
                zevw = [(ZMM, g2 + 1)]
                if g2 == 1:
                    zevw.append((ZTL, 1))
                sch.op("v", (lambda: (lambda v: v.tensor_copy(zf32[:, :],
                                                              ps_zw[:, :])))(),
                       waits=zevw, inc=ZEV)
                sch.op("t", (lambda g2=g2: (lambda t: t.matmul(
                    ps_zw[:, :], e16sc[:, :], alphaT[:, g2 * 512:(g2 + 1) * 512],
                    start=True, stop=True)))(),
                    waits=[(AEV, g2 * 4 + 4), (ZEV, g2 + 1)], inc=WMM)
                sch.op("v", (lambda g2=g2: (lambda v: v.tensor_tensor(
                    ztil[g2][:, :], zf32[:, :], ps_zw[:, :], ALU.mult)))(),
                    waits=[(ZEV, g2 + 1), (WMM, g2 + 1)], inc=ZTL)

            def ffn_extra(t, p, m, g2):
                return t.matmul(p[:, :], bcat_sb[:, m * 128:(m + 1) * 128],
                                ztil[g2][:, :], start=False, stop=True)
            sch2, QW3 = mm_phase2("P5", wb_t, NK, ffn_s,
                                  first_waits=[(FT3, NK * 16)],
                                  bias_col=lambda m: bbT[:, m:m + 1],
                                  extra_mm=ffn_extra, extra_mm_waits=[(ZTL, 2)],
                                  psq=psq, wslab=wslab, oev=oev,
                                  load_first_waits=[(XMW, XMW.n), (X2W, X2W.n)])
            sch.ops.extend(sch2.ops)
            sch.emit_into(nc)
            FW = QW3
        if _PH <= 9:
            nc.finalize(); return nc

        with contextlib.ExitStack() as r7:
            e7 = r7.enter_context
            xm2 = [e7(nc.sbuf_tensor(f"xm2_{i}", [128, E], F32)) for i in range(2)]
            ftok = [e7(nc.sbuf_tensor(f"ftok{i}", [128, E], BF16)) for i in range(2)]

            sch = Sched()
            XL9 = [S_("XL90"), S_("XL91")]
            FT9 = [S_("FT90"), S_("FT91")]
            A9 = S_("A9")
            OW = [S_("OW0"), S_("OW1")]
            def b9_load(tI):
                b = tI % 2
                w = ([(XMW, XMW.n), (FW[0], FW[0].n), (FW[1], FW[1].n)]
                     if tI == 0 else [])
                if tI >= 2:
                    w.append((OW[b], (tI // 2) * 16))
                sch.op("g", (lambda tI=tI, b=b: (lambda g: g.dma_start(
                    xm2[b][:, :], xmid_s[tI * 128:(tI + 1) * 128, :])))(),
                    waits=w, inc=XL9[b], amt=16)
                w = ([(FW[0], FW[0].n), (FW[1], FW[1].n), (XMW, XMW.n)]
                     if tI == 0 else [])
                if tI >= 2:
                    w.append((A9, tI - 1))
                sch.op("y", (lambda tI=tI, b=b: (lambda y: y.dma_start(
                    ftok[b][:, :], ffn_s[:, tI * 128:(tI + 1) * 128],
                    transpose=True)))(), waits=w, inc=FT9[b], amt=16)

            b9_load(0)
            b9_load(1)
            for tI in range(NT):
                b = tI % 2
                w = [(XL9[b], (tI // 2 + 1) * 16), (FT9[b], (tI // 2 + 1) * 16)]
                sch.op("v", (lambda b=b: (lambda v: v.tensor_add(
                    xm2[b][:, :], xm2[b][:, :], ftok[b][:, :])))(), waits=w, inc=A9)
                sch.op("g", (lambda tI=tI, b=b: (lambda g: g.dma_start(
                    out_t[tI * 128:(tI + 1) * 128, :], xm2[b][:, :])))(),
                    waits=[(A9, tI + 1)], inc=OW[b], amt=16)
                if tI + 2 < NT:
                    b9_load(tI + 2)
            DBG = S_("DBG")
            for (o, src) in dbg_pairs:
                sch.op("g", (lambda o=o, src=src: (lambda g: g.dma_start(o, src)))(),
                       waits=[(A9, NT)], inc=DBG, amt=16)
            def fin_wait(g):
                g.wait_ge(OW[0].h, OW[0].n)
                g.wait_ge(OW[1].h, OW[1].n)
                if dbg_pairs:
                    g.wait_ge(DBG.h, DBG.n)
                return None
            sch.ops.append(("g", [], fin_wait, []))
            sch.emit_into(nc)

    nc.finalize()
    return nc


def _prep_shared(w_qkv, w_c, W_base, b_base, lora_A, lora_B, router_W,
                 router_b, domain_prior, active):
    def tile4(w, n_m):
        return np.ascontiguousarray(
            w.reshape(NK, 128, n_m, 128).transpose(2, 1, 0, 3).reshape(n_m, 128, NK * 128))
    sh = {}
    sh["wq_t"] = tile4(np.asarray(w_qkv, np.float32), NQ)
    sh["wc_t"] = tile4(np.asarray(w_c, np.float32), NK)
    sh["wb_t"] = tile4(np.ascontiguousarray(np.asarray(W_base, np.float32).T), NK)
    A = np.asarray(lora_A, np.float32).reshape(128, E)
    sh["acat_t"] = np.ascontiguousarray(
        A.T.reshape(NK, 128, 128).transpose(1, 0, 2).reshape(128, NK * 128))
    sh["bcat_t"] = np.ascontiguousarray(
        np.asarray(lora_B, np.float32).transpose(0, 2, 1).reshape(128, E))
    rwT = np.zeros((E, 128), np.float32)
    rwT[:, :16] = np.asarray(router_W, np.float32).T
    sh["rw_t"] = np.ascontiguousarray(
        rwT.reshape(NK, 128, 128).transpose(1, 0, 2).reshape(128, NK * 128))
    sh["bbT_i"] = np.ascontiguousarray(np.asarray(b_base, np.float32).reshape(32, 128).T)
    sh["rb_row"] = np.asarray(router_b, np.float32).reshape(1, 16)
    sh["prior_row"] = np.asarray(domain_prior, np.float32).reshape(1, 16)
    sh["active_col"] = np.asarray(active, np.int32).reshape(4, 1)
    sh["posrow_i"] = (np.arange(S, dtype=np.float64) + 8.0).astype(BFD).reshape(1, S)
    mk = np.zeros((128, 4 * 512), np.float32)
    for j in range(4):
        p = np.arange(128)[:, None]
        f = np.arange(512)[None, :]
        mk[:, j * 512:(j + 1) * 512] = np.where(f >= p + 128 * j, 0.0, -1e30)
    sh["maskadd_i"] = mk
    sh["iota16_i"] = np.broadcast_to(np.arange(16, dtype=np.float32)[None, :], (4, 16)).copy()
    e16 = np.zeros((16, 128), np.float32)
    for j in range(16):
        e16[j, j * 8:(j + 1) * 8] = 2.0
    sh["e16sc_i"] = e16
    sh["identf_i"] = np.eye(128, dtype=np.float32)
    return sh


def _prep_core(core):
    sl = _slopes()
    cc = core % 4
    heads = np.arange(8 * cc, 8 * cc + 8)
    bt = np.zeros((128, HPC * 32), np.float32)
    for hh in range(HPC):
        s = sl[heads[hh]]
        for kc in range(32):
            bt[:, hh * 32 + kc] = s * (128.0 * kc + np.arange(128))
    d = {"biastab_i": bt,
         "negslope_i": (-sl[heads]).reshape(1, HPC).astype(np.float32),
         "coff_i": np.asarray([[cc * 1024, cc * 512, (7 - cc) * 512,
                                (cc // 2) * 8192 + (cc % 2) * 1024]], np.int32)}
    return d


def kernel(L0_attn_hidden_states, L0_attn_w_qkv, L0_attn_w_c, L0_ffn_W_base,
           L0_ffn_b_base, L0_ffn_lora_A, L0_ffn_lora_B, L0_ffn_router_W,
           L0_ffn_router_b, L0_ffn_domain_prior, L0_ffn_active):
    x = np.asarray(L0_attn_hidden_states, np.float32)
    sh = _prep_shared(L0_attn_w_qkv, L0_attn_w_c, L0_ffn_W_base, L0_ffn_b_base,
                      L0_ffn_lora_A, L0_ffn_lora_B, L0_ffn_router_W,
                      L0_ffn_router_b, L0_ffn_domain_prior, L0_ffn_active)
    dbg = bool(int(os.environ.get("MKDBG", "0")))
    nc = build_nc(dbg=dbg)
    in_maps = []
    for core in range(8):
        b, cc = core // 4, core % 4
        xs = np.concatenate([x[b, 512 * cc:512 * cc + 512],
                             x[b, 512 * (7 - cc):512 * (7 - cc) + 512]], axis=0)
        m = dict(sh)
        m["xs"] = np.ascontiguousarray(xs)
        m.update(_prep_core(core))
        in_maps.append(m)
    res = run_bass_kernel_spmd(nc, in_maps, core_ids=list(range(8)))
    out = np.empty((B, S, E), np.float32)
    for core in range(8):
        b, cc = core // 4, core % 4
        o = np.asarray(res.results[core]["out"])
        out[b, 512 * cc:512 * cc + 512] = o[0:512]
        out[b, 512 * (7 - cc):512 * (7 - cc) + 512] = o[512:1024]
    if dbg:
        kernel._dbg = res.results
    return out
